# version 11
# baseline (speedup 1.0000x reference)
import contextlib
import numpy as np

import concourse.bass as bass
import concourse.mybir as mybir
from concourse.bass_utils import run_bass_kernel_spmd

F32 = mybir.dt.float32
BF16 = mybir.dt.bfloat16
AF = mybir.ActivationFunctionType
ALU = mybir.AluOpType

NCORES = 8
T = 1024
TH = 512
D = 2048
DC = 16
HID = 5632
HJ = 44
NH = 16
MM = 64
NW = float(HID * D)
GSL = HJ * 128 * 2048 // NCORES // 128

STAGE = 3


def build(stage=STAGE):
    nc = bass.Bass("TRN2", target_bir_lowering=False,
                   dynamic_dma_scratch_size=2048)

    dt_in = lambda n, s: nc.dram_tensor(n, s, F32, kind="ExternalInput").ap()
    xin = dt_in("xin", [T, D])
    w1t = dt_in("w1t", [HJ, 128, 2048])
    w2t = dt_in("w2t", [HJ, 128, 2048])
    w3t = dt_in("w3t", [DC, 128, HID])
    gsl = dt_in("gsl", [3, 128, GSL])
    wqt = dt_in("wqt", [NH, 128, 2048])
    wkt = dt_in("wkt", [NH, 128, 2048])
    wvt = dt_in("wvt", [DC, 128, 2048])
    wot = dt_in("wot", [DC, 128, 2048])
    wmapqt = dt_in("wmapqt", [128, MM])
    wmapkt = dt_in("wmapkt", [128, MM])
    bqt = dt_in("bqt", [128, NH])
    bkt = dt_in("bkt", [128, NH])
    bvr = dt_in("bvr", [1, D])
    bot = dt_in("bot", [128, NH])
    out = nc.dram_tensor("out", [T, D], F32, kind="ExternalOutput").ap()

    w1q = nc.dram_tensor("w1q", [HJ, 128, 2048], BF16).ap()
    w2q = nc.dram_tensor("w2q", [HJ, 128, 2048], BF16).ap()
    w3q = nc.dram_tensor("w3q", [DC, 128, HID], BF16).ap()
    wvb_d = nc.dram_tensor("wvb_d", [DC, 128, 2048], BF16).ap()
    cc_gin = nc.dram_tensor("cc_gin", [1, 4], F32)
    cc_gout = nc.dram_tensor("cc_gout", [1, 4], F32)
    cc_kin = nc.dram_tensor("cc_kin", [128, 1536], F32)
    cc_kout = nc.dram_tensor("cc_kout", [128, 1536], F32)
    nrm_d = nc.dram_tensor("nrm_d", [NH, T], F32)

    st = contextlib.ExitStack()
    sb = lambda name, shape, dt: st.enter_context(nc.sbuf_tensor(name, shape, dt))

    xf = sb("xf", [128, DC * T], F32)
    ybuf = sb("ybuf", [128, DC * T], BF16)
    wfa = [sb("wfa0", [128, 2048], F32), sb("wfa1", [128, 2048], F32)]
    wqa = [sb("wqa0", [128, 2048], BF16), sb("wqa1", [128, 2048], BF16)]
    AR = 18960
    arena = sb("arena", [128, AR], F32)
    x2t = [sb("x2t0", [128, TH], BF16), sb("x2t1", [128, TH], BF16)]
    sqt = sb("sqt", [128, TH], BF16)
    xbt = [sb("xbt0", [128, TH], BF16), sb("xbt1", [128, TH], BF16)]
    nrm = sb("nrm", [NH, T], F32)
    onesb_c = sb("onesb_c", [128, 1], BF16)
    onesb_r = sb("onesb_r", [1, 128], BF16)
    onesf_r = sb("onesf_r", [1, 128], F32)
    onesf_c = sb("onesf_c", [128, 1], F32)
    thrv = sb("thrv", [128, 8], F32)
    gpart = sb("gpart", [128, 2], F32)
    gprow = sb("gprow", [1, 4], F32)
    identf = sb("identf", [128, 128], F32)
    identb = sb("identb", [128, 128], BF16)
    wmapq_b = sb("wmapq_b", [128, MM], BF16)
    wmapk_b = sb("wmapk_b", [128, MM], BF16)
    wmap_f = sb("wmap_f", [128, MM], F32)
    bq_s = sb("bq_s", [128, NH], F32)
    bk_s = sb("bk_s", [128, NH], F32)
    bo_s = sb("bo_s", [128, NH], F32)
    negone = sb("negone", [128, 1], F32)
    halfc = sb("halfc", [128, 1], F32)
    qmb = sb("qmb", [128, (NH // 2) * T], BF16)
    bv_f = None
    bv_b = None

    def cf(c0, n):
        assert c0 + n <= AR
        return arena[:, c0:c0 + n]

    def cb(c0, nb):
        assert c0 + nb // 2 <= AR
        return arena[:, c0:c0 + nb // 2].bitcast(BF16)

    GH = cb(0, HJ * TH)
    SQW = [cb(11264, 2048), cb(12288, 2048)]
    G2B = [cf(13376, 2048), cf(15424, 2048)]
    W3R = [cb(13376, 2816), cb(14784, 2816)]
    rows = cf(16192, 2048)
    W3I = [cf(11264, 1408), cf(16192, 1408), qmb[:, 0:2816].bitcast(F32)]
    W3Rq = [W3R[0][:, 0:1408], W3R[1][:, 0:1408]]
    SQB3 = [W3R[0][:, 1408:2816], W3R[1][:, 1408:2816]]
    XB = cb(0, 16 * TH)
    SQ = cb(4096, 16 * TH)
    XTOK = [cf(c, 2048) for c in (0, 2048, 4096, 6144)]
    GAMS = cf(8192, GSL // 2)
    XOUT = cf(0, 2048)
    VBUF = cb(0, 2 * NH * 129)
    KMB = cb(2064, NH * 512)
    KTMP = [cb(6160, T), cb(6672, T)]
    WVS = cb(7184, DC * 256)
    QTMP = [cb(9232, T), cb(9744, T)]
    ELU = [cf(10256, T)[0:64, :], cf(11280, T)[0:64, :]]
    kvga = cf(12304, 1536)
    kvgb = cb(13840, 1536)
    RNBT = cb(15376, T)
    RNB2 = [RNBT[0:64, :], RNBT[64:128, :]]
    WVF = [cf(15888, 1024), cf(16912, 1024)]
    WVO = [cb(17936, 1024), cb(18448, 1024)]

    bv_f = wfa[0][0:1, 0:D]
    bv_b = nrm[0:1, :].bitcast(BF16)

    sems = {}
    for nm in ("lA", "lX", "lWe", "lWo", "lBe", "lBo", "lPe", "lPo",
               "lV", "lN", "sO", "lG", "l3e", "l3o", "lC",
               "sWe", "sWo", "sQe", "sQo", "sVe", "sVo",
               "g", "cc", "pe", "ac", "dv", "pl"):
        sems[nm] = st.enter_context(nc.semaphore("s_" + nm))
    cnt = {k: 0 for k in sems}

    def bump(inst, nm, k=1):
        inst.then_inc(sems[nm], k)
        cnt[nm] += k
        return cnt[nm]

    def wait(eng, nm, v=None):
        eng.wait_ge(sems[nm], cnt[nm] if v is None else v)

    block = st.enter_context(nc.Block())
    TT = block.tensor
    VV = block.vector
    AA = block.scalar
    GG = block.gpsimd
    SS = block.sync

    marks = {}

    def mark(key):
        marks[key] = dict(cnt)

    WG = {"dv": 0, "pe": 0, "ac": 0, "sWe": 0, "sWo": 0}

    def wguard(eng, wg):
        for nm, v in wg.items():
            if v:
                eng.wait_ge(sems[nm], v)

    def wg_update():
        for nm in WG:
            WG[nm] = cnt[nm]

    def f(g):
        for tile, val in ((identf, 0.0), (identb, 0.0), (onesb_c, 1.0),
                          (onesb_r, 1.0), (onesf_r, 1.0), (onesf_c, 1.0),
                          (negone, -1.0), (halfc, 0.5)):
            bump(g.memset(tile[:], val), "pl")
        wait(g, "pl")
        bump(g.affine_select(out=identf[:], in_=identf[:], compare_op=ALU.not_equal,
                             fill=1.0, base=0, pattern=[[-1, 128]],
                             channel_multiplier=1), "pl")
        bump(g.affine_select(out=identb[:], in_=identb[:], compare_op=ALU.not_equal,
                             fill=1.0, base=0, pattern=[[-1, 128]],
                             channel_multiplier=1), "pl")
    GG(f)
    mark("consts")

    def f(s):
        for dst, src in ((bq_s[:], bqt), (bk_s[:], bkt), (bo_s[:], bot),
                         (bv_f[:], bvr), (wmap_f[:], wmapqt)):
            bump(s.dma_start(out=dst, in_=src), "lA", 16)
    SS(f)
    mark("smallld")

    def f(a):
        wait(a, "lA", marks["smallld"]["lA"])
        bump(a.copy(out=wmapq_b[:], in_=wmap_f[:]), "ac")
        bump(a.copy(out=bv_b[:], in_=bv_f[:]), "ac")
    AA(f)
    mark("wmapq_cast")

    def f(s):
        wait(s, "ac", marks["wmapq_cast"]["ac"])
        bump(s.dma_start(out=wmap_f[:], in_=wmapkt), "lA", 16)
    SS(f)
    mark("wmapk_ld")

    def f(a):
        wait(a, "lA", marks["wmapk_ld"]["lA"])
        bump(a.copy(out=wmapk_b[:], in_=wmap_f[:]), "ac")
    AA(f)
    mark("wmapk_cast")

    with nc.psum_tensor("ps_g", [1, 4], F32) as ps_g, \
         nc.psum_tensor("ps_thr", [128, 8], F32) as ps_thr, \
         nc.psum_tensor("ps_xt0", [128, 512], F32) as ps_xt0, \
         nc.psum_tensor("ps_xt1", [128, 512], F32) as ps_xt1:
        ps_xt = [ps_xt0, ps_xt1]

        GH2 = GSL // 2
        for w in range(3):
            for hf in range(2):
                def f(a, w=w, hf=hf):
                    if w + hf > 0:
                        pw, ph = (w, hf - 1) if hf else (w - 1, 1)
                        wait(a, "dv", marks[f"gred{pw}_{ph}"]["dv"])
                    bump(a.dma_start(out=GAMS, in_=gsl[w][:, hf * GH2:(hf + 1) * GH2]),
                         "lG", 16)
                AA(f)
                mark(f"gld{w}_{hf}")

                def f(v, w=w, hf=hf):
                    wait(v, "lG", marks[f"gld{w}_{hf}"]["lG"])
                    if hf == 0 and w > 0:
                        wait(v, "pe", marks[f"gtot{w - 1}"]["pe"])
                    bump(v.reduce_sum(out=gpart[:, hf:hf + 1], in_=GAMS,
                                      axis=mybir.AxisListType.X,
                                      apply_absolute_value=True), "dv")
                VV(f)
                mark(f"gred{w}_{hf}")

            def f(t, w=w):
                wait(t, "dv", marks[f"gred{w}_1"]["dv"])
                if w == 0:
                    wait(t, "pl", marks["consts"]["pl"])
                bump(t.matmul(ps_g[0:1, w:w + 1], gpart[:, 0:1], onesf_c[:],
                              start=True, stop=False), "pe")
                bump(t.matmul(ps_g[0:1, w:w + 1], gpart[:, 1:2], onesf_c[:],
                              start=False, stop=True), "pe")
            TT(f)
            mark(f"gtot{w}")

        def f(v):
            bump(v.memset(gprow[:], 0.0), "dv")
            wait(v, "pe", marks["gtot2"]["pe"])
            wait(v, "dv")
            bump(v.tensor_copy(out=gprow[0:1, 0:3], in_=ps_g[0:1, 0:3]), "dv")
        VV(f)
        mark("gpack")

        def f(g):
            wait(g, "dv", marks["gpack"]["dv"])
            bump(g.dma_start(out=cc_gin.ap(), in_=gprow[:]), "g", 16)
            wait(g, "g")
            bump(g.collective_compute(
                "AllReduce", ALU.add, replica_groups=[list(range(NCORES))],
                ins=[cc_gin.ap().opt()], outs=[cc_gout.ap().opt()]), "cc", 1)
            wait(g, "cc")
            bump(g.dma_start(out=gprow[:], in_=cc_gout.ap()), "g", 16)
        GG(f)
        mark("gcc")

        def f(v):
            wait(v, "g", marks["gcc"]["g"])
            bump(v.tensor_scalar(out=rows[0:1, 0:4], in0=gprow[:],
                                 scalar1=0.5 / NW, scalar2=0.5e-8,
                                 op0=ALU.mult, op1=ALU.add), "dv")
            bump(v.tensor_scalar(out=rows[0:1, 4:8], in0=gprow[:],
                                 scalar1=-0.5 / NW, scalar2=-0.5e-8,
                                 op0=ALU.mult, op1=ALU.add), "dv")
        VV(f)
        mark("throw")

        def f(t):
            wait(t, "dv", marks["throw"]["dv"])
            bump(t.matmul(ps_thr[:, :], onesf_r[:], rows[0:1, 0:8],
                          start=True, stop=True), "pe")
        TT(f)
        mark("thrbc")

        def f(v):
            wait(v, "pe", marks["thrbc"]["pe"])
            bump(v.tensor_copy(out=thrv[:], in_=ps_thr[:, :]), "dv")
        VV(f)
        mark("thr")
        THR_DV = cnt["dv"]

        for grp in range(2):
            def f(s, grp=grp):
                if grp == 1:
                    wait(s, "pe", marks["xt0"]["pe"])
                for i in range(4):
                    m = grp * 4 + i
                    bump(s.dma_start(out=XTOK[i],
                                     in_=xin[m * 128:(m + 1) * 128, :]), "lX", 16)
            SS(f)
            mark(f"xld{grp}")

            for dc in range(DC):
                b = dc % 2

                def f(t, grp=grp, dc=dc, b=b):
                    if dc == 0:
                        wait(t, "lX", marks[f"xld{grp}"]["lX"])
                        if grp == 0:
                            wait(t, "pl", marks["consts"]["pl"])
                    if dc >= 2 or grp == 1:
                        pk = f"xtev{grp}_{dc - 2}" if dc >= 2 else f"xtev0_{DC - 2 + dc}"
                        wait(t, "dv", marks[pk]["dv"])
                    for i in range(3):
                        t.transpose(ps_xt[b][:, i * 128:(i + 1) * 128],
                                    XTOK[i][:, dc * 128:(dc + 1) * 128], identf[:])
                    bump(t.transpose(ps_xt[b][:, 3 * 128:4 * 128],
                                     XTOK[3][:, dc * 128:(dc + 1) * 128],
                                     identf[:]), "pe")
                TT(f)
                mark(f"xtp{grp}_{dc}")

                def f(v, grp=grp, dc=dc, b=b):
                    wait(v, "pe", marks[f"xtp{grp}_{dc}"]["pe"])
                    bump(v.tensor_copy(
                        out=xf[:, dc * T + grp * 512: dc * T + grp * 512 + 512],
                        in_=ps_xt[b][:, :]), "dv")
                VV(f)
                mark(f"xtev{grp}_{dc}")
            mark(f"xt{grp}")
    mark("xf_done")

    def emit_ln(tag, src_dv):
        with nc.psum_tensor("psS_" + tag, [1, TH], F32) as psS, \
             nc.psum_tensor("psQ_" + tag, [1, TH], F32) as psQ, \
             nc.psum_tensor("psB1_" + tag, [128, TH], F32) as psB1, \
             nc.psum_tensor("psB2_" + tag, [128, TH], F32) as psB2:
            for h2 in range(2):
                off = h2 * TH

                def f(v, off=off):
                    if off == 0:
                        wait(v, "dv", src_dv)
                    else:
                        wait(v, "pe", marks[f"{tag}mq_0"]["pe"])
                    for dc in range(DC):
                        bump(v.tensor_copy(out=XB[:, dc * TH:(dc + 1) * TH],
                                           in_=xf[:, dc * T + off: dc * T + off + TH]),
                             "dv")
                VV(f)
                mark(f"{tag}xb_{off}")

                def f(a, off=off):
                    wait(a, "dv", marks[f"{tag}xb_{off}"]["dv"])
                    bump(a.activation(out=SQ[:, :], in_=XB[:, :],
                                      func=AF.Square), "ac")
                AA(f)
                mark(f"{tag}sq_{off}")

                def f(t, off=off):
                    wait(t, "ac", marks[f"{tag}sq_{off}"]["ac"])
                    if off:
                        wait(t, "dv", marks[f"{tag}var_0"]["dv"])
                    for dc in range(DC):
                        t.matmul(psS[0:1, :], onesb_c[:], XB[:, dc * TH:(dc + 1) * TH],
                                 start=(dc == 0), stop=(dc == DC - 1))
                    for dc in range(DC - 1):
                        t.matmul(psQ[0:1, :], onesb_c[:], SQ[:, dc * TH:(dc + 1) * TH],
                                 start=(dc == 0), stop=False)
                    bump(t.matmul(psQ[0:1, :], onesb_c[:],
                                  SQ[:, (DC - 1) * TH:DC * TH],
                                  start=False, stop=True), "pe")
                TT(f)
                mark(f"{tag}mq_{off}")

                def f(v, off=off):
                    wait(v, "pe", marks[f"{tag}mq_{off}"]["pe"])
                    r = rows
                    bump(v.tensor_scalar(out=r[0:1, 0:TH], in0=psS[0:1, :],
                                         scalar1=1.0 / D, scalar2=None,
                                         op0=ALU.mult), "dv")
                    bump(v.tensor_scalar(out=r[0:1, TH:2 * TH], in0=psQ[0:1, :],
                                         scalar1=1.0 / D, scalar2=None,
                                         op0=ALU.mult), "dv")
                    wait(v, "dv")
                    bump(v.tensor_tensor(out=r[0:1, 2 * TH:3 * TH], in0=r[0:1, 0:TH],
                                         in1=r[0:1, 0:TH], op=ALU.mult), "dv")
                    wait(v, "dv")
                    bump(v.tensor_scalar(out=r[0:1, 2 * TH:3 * TH],
                                         in0=r[0:1, 2 * TH:3 * TH],
                                         scalar1=-1.0, scalar2=1e-5,
                                         op0=ALU.mult, op1=ALU.add), "dv")
                    wait(v, "dv")
                    bump(v.tensor_tensor(out=r[0:1, 2 * TH:3 * TH],
                                         in0=r[0:1, TH:2 * TH],
                                         in1=r[0:1, 2 * TH:3 * TH],
                                         op=ALU.add), "dv")
                VV(f)
                mark(f"{tag}var_{off}")

                def f(a, off=off):
                    wait(a, "dv", marks[f"{tag}var_{off}"]["dv"])
                    bump(a.activation(out=rows[0:1, 3 * TH:4 * TH],
                                      in_=rows[0:1, 2 * TH:3 * TH],
                                      func=AF.Sqrt), "ac")
                AA(f)
                mark(f"{tag}sd_{off}")

                def f(v, off=off):
                    wait(v, "ac", marks[f"{tag}sd_{off}"]["ac"])
                    r = rows
                    bump(v.reciprocal(out=r[0:1, TH:2 * TH],
                                      in_=r[0:1, 3 * TH:4 * TH]), "dv")
                    wait(v, "dv")
                    bump(v.tensor_tensor(out=r[0:1, 2 * TH:3 * TH], in0=r[0:1, 0:TH],
                                         in1=r[0:1, TH:2 * TH],
                                         op=ALU.mult), "dv")
                    wait(v, "dv")
                    bump(v.tensor_scalar(out=r[0:1, 2 * TH:3 * TH],
                                         in0=r[0:1, 2 * TH:3 * TH],
                                         scalar1=-1.0, scalar2=None,
                                         op0=ALU.mult), "dv")
                VV(f)
                mark(f"{tag}rs_{off}")

                def f(t, off=off):
                    wait(t, "dv", marks[f"{tag}rs_{off}"]["dv"])
                    bump(t.matmul(psB1[:, :], onesf_r[:], rows[0:1, TH:2 * TH],
                                  start=True, stop=True), "pe")
                    bump(t.matmul(psB2[:, :], onesf_r[:], rows[0:1, 2 * TH:3 * TH],
                                  start=True, stop=True), "pe")
                TT(f)
                mark(f"{tag}bc_{off}")

                for dc in range(DC):
                    def f(v, dc=dc, off=off):
                        if dc == 0:
                            wait(v, "pe", marks[f"{tag}bc_{off}"]["pe"])
                        else:
                            wait(v, "dv")
                        bump(v.tensor_tensor(out=G2B[0][:, 0:TH],
                                             in0=xf[:, dc * T + off: dc * T + off + TH],
                                             in1=psB1[:, :], op=ALU.mult), "dv")
                        wait(v, "dv")
                        bump(v.tensor_tensor(
                            out=ybuf[:, dc * T + off: dc * T + off + TH],
                            in0=G2B[0][:, 0:TH], in1=psB2[:, :], op=ALU.add), "dv")
                    VV(f)
                    mark(f"{tag}y{dc}_{off}")
        mark(f"{tag}done")

    def emit_ffn(call, lntag):
        with nc.psum_tensor(f"psX{call}a", [128, TH], F32) as psxa, \
             nc.psum_tensor(f"psX{call}b", [128, TH], F32) as psxb, \
             nc.psum_tensor(f"psO{call}a", [128, TH], F32) as psoa, \
             nc.psum_tensor(f"psO{call}b", [128, TH], F32) as psob:
            psX = [psxa, psxb]
            psO = [psoa, psob]
            guard = {"psX": None, "psO": None, "GH": None, "W3R": None}

            def sweep(sw, wsrc, wscr, h2):
                off = h2 * TH
                quant = (call == 0 and h2 == 0)
                key = f"F{call}s{sw}h{h2}"
                widx = 0 if sw == 1 else 1
                if quant:
                    wfaH = [wfa[s // 2][:, (s % 2) * 1024:(s % 2 + 1) * 1024]
                            for s in range(4)]
                    wqaH = [wqa[s // 2][:, (s % 2) * 1024:(s % 2 + 1) * 1024]
                            for s in range(4)]
                    SQWH = [SQW[s // 2][:, (s % 2) * 1024:(s % 2 + 1) * 1024]
                            for s in range(4)]
                    for jh in range(2 * HJ):
                        j, hf = jh // 2, jh % 2
                        sl = jh % 4
                        lw = "lWe" if jh % 2 == 0 else "lWo"
                        sws = "sWe" if jh % 2 == 0 else "sWo"

                        def f(s, jh=jh, j=j, hf=hf, sl=sl, sw=sw,
                              lw=lw, wg=dict(WG)):
                            if jh < 4:
                                wguard(s, wg)
                                if sw == 2:
                                    wait(s, "ac", marks["F0s1h0end"]["ac"])
                            else:
                                wait(s, "ac", marks[f"{key}s2_{jh - 4}"]["ac"])
                            bump(s.dma_start(
                                out=wfaH[sl],
                                in_=wsrc[j][:, hf * 1024:(hf + 1) * 1024]),
                                lw, 16)
                        SS(f)
                        mark(f"{key}l{jh}")

                        def f(a, jh=jh, sl=sl, sw=sw, widx=widx,
                              lw=lw, sws=sws):
                            wait(a, lw, marks[f"{key}l{jh}"][lw])
                            wait(a, "dv", THR_DV)
                            if jh >= 4:
                                wait(a, "pe", marks[f"{key}m{jh - 4}"]["pe"])
                                wait(a, sws, marks[f"{key}w{jh - 2}"][sws])
                                wait(a, "dv", marks[f"{key}q{jh - 4}"]["dv"])
                            elif sw == 2:
                                wait(a, "pe", marks["F0s1h0end"]["pe"])
                                wait(a, sws, marks["F0s1h0end"][sws])
                                wait(a, "dv", marks["F0s1h0end"]["dv"])
                            bump(a.activation(
                                out=wqaH[sl][:, :], in_=wfaH[sl][:, :],
                                func=AF.Sign,
                                bias=thrv[:, 4 + widx:5 + widx]), "ac")
                            wait(a, "ac")
                            bump(a.activation(
                                out=SQWH[sl][:, :], in_=wfaH[sl][:, :],
                                func=AF.Sign,
                                bias=thrv[:, widx:widx + 1]), "ac")
                        AA(f)
                        mark(f"{key}s2_{jh}")

                        def f(v, jh=jh, sl=sl):
                            wait(v, "ac", marks[f"{key}s2_{jh}"]["ac"])
                            bump(v.tensor_tensor(
                                out=wqaH[sl][:, :], in0=wqaH[sl][:, :],
                                in1=SQWH[sl][:, :], op=ALU.add), "dv")
                        VV(f)
                        mark(f"{key}q{jh}")

                        def f(s, jh=jh, j=j, hf=hf, sl=sl, sws=sws):
                            wait(s, "dv", marks[f"{key}q{jh}"]["dv"])
                            bump(s.dma_start(
                                out=wscr[j][:, hf * 1024:(hf + 1) * 1024],
                                in_=wqaH[sl][:, :]), sws, 16)
                        SS(f)
                        mark(f"{key}w{jh}")

                        def f(t, jh=jh, j=j, hf=hf, sl=sl):
                            wait(t, "dv", marks[f"{key}q{jh}"]["dv"])
                            if jh == 0:
                                wait(t, "dv", marks[f"{lntag}done"]["dv"])
                                if guard["psX"] is not None:
                                    for nm, v in guard["psX"].items():
                                        wait(t, nm, v)
                            if hf == 0 and j >= 2:
                                wait(t, "dv", marks[f"{key}c{j - 2}"]["dv"])
                            pj = j % 2
                            for di in range(8):
                                dc = hf * 8 + di
                                mm = t.matmul(
                                    psX[pj][:, :],
                                    wqaH[sl][:, di * 128:(di + 1) * 128],
                                    ybuf[:, dc * T + off: dc * T + off + TH],
                                    start=(dc == 0), stop=(dc == DC - 1))
                            bump(mm, "pe")
                        TT(f)
                        mark(f"{key}m{jh}")

                        if hf == 1:
                            if sw == 1:
                                def f(v, j=j, jh=jh):
                                    wait(v, "pe", marks[f"{key}m{jh}"]["pe"])
                                    if j == 0 and guard["GH"] is not None:
                                        wait(v, "pe", guard["GH"])
                                    bump(v.tensor_copy(
                                        out=GH[:, j * TH:(j + 1) * TH],
                                        in_=psX[j % 2][:, :]), "dv")
                                VV(f)
                                mark(f"{key}c{j}")
                            else:
                                def f(v, j=j, jh=jh):
                                    wait(v, "pe", marks[f"{key}m{jh}"]["pe"])
                                    if j == 0:
                                        wait(v, "ac",
                                             marks[f"sig{call}_{h2}"]["ac"])
                                    if j >= 2:
                                        wait(v, "dv", marks[f"{key}h{j - 2}"]["dv"])
                                    bump(v.tensor_copy(out=x2t[j % 2][:, :],
                                                       in_=psX[j % 2][:, :]), "dv")
                                    wait(v, "dv")
                                    bump(v.tensor_tensor(
                                        out=GH[:, j * TH:(j + 1) * TH],
                                        in0=GH[:, j * TH:(j + 1) * TH],
                                        in1=x2t[j % 2][:, :],
                                        op=ALU.mult), "dv")
                                VV(f)
                                mark(f"{key}c{j}")
                                marks[f"{key}h{j}"] = dict(cnt)
                    mark(f"{key}end")
                    guard["psX"] = {"dv": cnt["dv"]}
                    wg_update()
                    return
                for j in range(HJ):
                    p = j % 2
                    if True:
                        def f(s, j=j, p=p, sw=sw, wg=dict(WG)):
                            if j < 2:
                                wguard(s, wg)
                                if call == 0 and h2 == 1 and j == 0:
                                    kh = f"F0s{sw}h0"
                                    wait(s, "sWe", marks[f"{kh}w{2 * HJ - 2}"]["sWe"])
                                    wait(s, "sWo", marks[f"{kh}w{2 * HJ - 1}"]["sWo"])
                            if j >= 2:
                                wait(s, "pe", marks[f"{key}m{j - 2}"]["pe"])
                            bump(s.dma_start(out=wqa[p][:, :], in_=wscr[j]),
                                 "lWe" if j % 2 == 0 else "lWo", 16)
                        SS(f)
                        mark(f"{key}l{j}")

                    def f(t, j=j, p=p):
                        if quant:
                            wait(t, "dv", marks[f"{key}q{j}"]["dv"])
                        else:
                            lw = "lWe" if j % 2 == 0 else "lWo"
                            wait(t, lw, marks[f"{key}l{j}"][lw])
                        if j == 0:
                            wait(t, "dv", marks[f"{lntag}done"]["dv"])
                            if guard["psX"] is not None:
                                for nm, v in guard["psX"].items():
                                    wait(t, nm, v)
                        if j >= 2:
                            wait(t, "dv", marks[f"{key}c{j - 2}"]["dv"])
                        for dc in range(DC - 1):
                            t.matmul(psX[p][:, :],
                                     wqa[p][:, dc * 128:(dc + 1) * 128],
                                     ybuf[:, dc * T + off: dc * T + off + TH],
                                     start=(dc == 0), stop=False)
                        bump(t.matmul(psX[p][:, :],
                                      wqa[p][:, (DC - 1) * 128:DC * 128],
                                      ybuf[:, (DC - 1) * T + off: (DC - 1) * T + off + TH],
                                      start=False, stop=True), "pe")
                    TT(f)
                    mark(f"{key}m{j}")

                    if sw == 1:
                        def f(v, j=j, p=p):
                            wait(v, "pe", marks[f"{key}m{j}"]["pe"])
                            if j == 0 and guard["GH"] is not None:
                                wait(v, "pe", guard["GH"])
                            bump(v.tensor_copy(out=GH[:, j * TH:(j + 1) * TH],
                                               in_=psX[p][:, :]), "dv")
                        VV(f)
                        mark(f"{key}c{j}")
                    else:
                        def f(v, j=j, p=p):
                            wait(v, "pe", marks[f"{key}m{j}"]["pe"])
                            if j == 0:
                                wait(v, "ac", marks[f"sig{call}_{h2}"]["ac"])
                            if j >= 2:
                                wait(v, "dv", marks[f"{key}h{j - 2}"]["dv"])
                            bump(v.tensor_copy(out=x2t[p][:, :],
                                               in_=psX[p][:, :]), "dv")
                            wait(v, "dv")
                            bump(v.tensor_tensor(out=GH[:, j * TH:(j + 1) * TH],
                                                 in0=GH[:, j * TH:(j + 1) * TH],
                                                 in1=x2t[p][:, :],
                                                 op=ALU.mult), "dv")
                        VV(f)
                        mark(f"{key}c{j}")
                        marks[f"{key}h{j}"] = dict(cnt)
                mark(f"{key}end")
                guard["psX"] = {"dv": cnt["dv"]}
                wg_update()

            def sig_batch(h2):
                def f(a, h2=h2):
                    wait(a, "dv", marks[f"F{call}s1h{h2}c{HJ - 1}"]["dv"])
                    bump(a.activation(out=GH[:, :], in_=GH[:, :],
                                      func=AF.Sigmoid,
                                      scale=halfc[:, 0:1]), "ac")
                AA(f)
                mark(f"sig{call}_{h2}")

            def phase_b(h2):
                off = h2 * TH
                key = f"F{call}B{h2}"
                qmode = (call == 0 and h2 == 0)
                if qmode:
                    QB = HJ // 4
                    for o in range(DC):
                        for qq in range(4):
                            bidx = o * 4 + qq
                            p = bidx % 2
                            sin = bidx % 3

                            def f(s, o=o, qq=qq, bidx=bidx, sin=sin):
                                if bidx < 3:
                                    wait(s, "dv", marks["F0s2h0end"]["dv"])
                                else:
                                    wait(s, "ac",
                                         marks[f"{key}s2_{bidx - 3}"]["ac"])
                                bump(s.dma_start(
                                    out=W3I[sin],
                                    in_=w3t[o][:, qq * 1408:(qq + 1) * 1408]),
                                    "lBe" if bidx % 2 == 0 else "lBo", 16)
                            SS(f)
                            mark(f"{key}l{bidx}")

                            def f(a, bidx=bidx, p=p, sin=sin):
                                lb = "lBe" if bidx % 2 == 0 else "lBo"
                                wait(a, lb, marks[f"{key}l{bidx}"][lb])
                                sqp = "sQe" if bidx % 2 == 0 else "sQo"
                                if bidx >= 2:
                                    wait(a, "pe", marks[f"{key}m{bidx - 2}"]["pe"])
                                    wait(a, sqp, marks[f"{key}w{bidx - 2}"][sqp])
                                    wait(a, "dv", marks[f"{key}q{bidx - 2}"]["dv"])
                                bump(a.activation(
                                    out=W3Rq[p][:, :], in_=W3I[sin],
                                    func=AF.Sign, bias=thrv[:, 6:7]), "ac")
                                wait(a, "ac")
                                bump(a.activation(
                                    out=SQB3[p][:, :], in_=W3I[sin],
                                    func=AF.Sign, bias=thrv[:, 2:3]), "ac")
                            AA(f)
                            mark(f"{key}s2_{bidx}")

                            def f(v, bidx=bidx, p=p):
                                wait(v, "ac", marks[f"{key}s2_{bidx}"]["ac"])
                                bump(v.tensor_tensor(
                                    out=W3Rq[p][:, :], in0=W3Rq[p][:, :],
                                    in1=SQB3[p][:, :], op=ALU.add), "dv")
                            VV(f)
                            mark(f"{key}q{bidx}")

                            def f(s, o=o, qq=qq, bidx=bidx, p=p):
                                sqp = "sQe" if bidx % 2 == 0 else "sQo"
                                wait(s, "dv", marks[f"{key}q{bidx}"]["dv"])
                                bump(s.dma_start(
                                    out=w3q[o][:, qq * 1408:(qq + 1) * 1408],
                                    in_=W3Rq[p][:, :]), sqp, 16)
                            SS(f)
                            mark(f"{key}w{bidx}")

                            def f(t, o=o, qq=qq, bidx=bidx, p=p):
                                wait(t, "dv", marks[f"{key}q{bidx}"]["dv"])
                                if bidx == 0:
                                    wait(t, "dv",
                                         marks[f"F{call}s2h{h2}h{HJ - 1}"]["dv"])
                                    if guard["psO"] is not None:
                                        for nm, v in guard["psO"].items():
                                            wait(t, nm, v)
                                if bidx >= 8 and qq == 0:
                                    wait(t, "dv", marks[f"{key}e{o - 2}"]["dv"])
                                for j in range(QB):
                                    jj = qq * QB + j
                                    mm = t.matmul(
                                        psO[o % 2][:, :],
                                        W3Rq[p][:, j * 128:(j + 1) * 128],
                                        GH[:, jj * TH:(jj + 1) * TH],
                                        start=(jj == 0), stop=(jj == HJ - 1))
                                bump(mm, "pe")
                            TT(f)
                            mark(f"{key}m{bidx}")

                        def f(v, o=o):
                            wait(v, "pe", marks[f"{key}m{o * 4 + 3}"]["pe"])
                            sl = xf[:, o * T + off: o * T + off + TH]
                            bump(v.scalar_tensor_tensor(
                                out=sl, in0=psO[o % 2][:, :], scalar=0.125,
                                in1=sl, op0=ALU.mult, op1=ALU.add), "dv")
                        VV(f)
                        mark(f"{key}e{o}")
                    mark(f"{key}end")
                    guard["psO"] = {"dv": cnt["dv"]}
                    guard["W3R"] = cnt["pe"]
                    guard["GH"] = cnt["pe"]
                    return
                HB = HJ // 2
                for o in range(DC):
                    for hb in range(2):
                        bidx = o * 2 + hb
                        p = bidx % 2

                        def f(s, o=o, hb=hb, bidx=bidx, p=p):
                            if bidx < 2:
                                if call == 0 and h2 == 1:
                                    wait(s, "sQe", marks["F0B0end"]["sQe"])
                                    wait(s, "sQo", marks["F0B0end"]["sQo"])
                                if guard["W3R"] is not None:
                                    wait(s, "pe", guard["W3R"])
                            if bidx >= 2:
                                wait(s, "pe", marks[f"{key}m{bidx - 2}"]["pe"])
                            bump(s.dma_start(
                                out=W3R[p],
                                in_=w3q[o][:, hb * 2816:(hb + 1) * 2816]),
                                "lBe" if bidx % 2 == 0 else "lBo", 16)
                        SS(f)
                        mark(f"{key}l{bidx}")

                        def f(t, o=o, hb=hb, bidx=bidx, p=p):
                            lb = "lBe" if bidx % 2 == 0 else "lBo"
                            wait(t, lb, marks[f"{key}l{bidx}"][lb])
                            if bidx == 0:
                                wait(t, "dv", marks[f"F{call}s2h{h2}h{HJ - 1}"]["dv"])
                                if guard["psO"] is not None:
                                    for nm, v in guard["psO"].items():
                                        wait(t, nm, v)
                            if bidx >= 4 and hb == 0:
                                wait(t, "dv", marks[f"{key}e{o - 2}"]["dv"])
                            for j in range(HB):
                                jj = hb * HB + j
                                mm = t.matmul(psO[o % 2][:, :],
                                              W3R[p][:, j * 128:(j + 1) * 128],
                                              GH[:, jj * TH:(jj + 1) * TH],
                                              start=(jj == 0), stop=(jj == HJ - 1))
                            bump(mm, "pe")
                        TT(f)
                        mark(f"{key}m{bidx}")

                    def f(v, o=o):
                        wait(v, "pe", marks[f"{key}m{o * 2 + 1}"]["pe"])
                        sl = xf[:, o * T + off: o * T + off + TH]
                        bump(v.scalar_tensor_tensor(
                            out=sl, in0=psO[o % 2][:, :], scalar=0.125, in1=sl,
                            op0=ALU.mult, op1=ALU.add), "dv")
                    VV(f)
                    mark(f"{key}e{o}")
                mark(f"{key}end")
                guard["psO"] = {"dv": cnt["dv"]}
                guard["W3R"] = cnt["pe"]
                guard["GH"] = cnt["pe"]

            if call == 0:
                sweep(1, w1t, w1q, 0)
                sig_batch(0)
                sweep(2, w2t, w2q, 0)
                phase_b(0)
                sweep(1, None, w1q, 1)
                sig_batch(1)
                sweep(2, None, w2q, 1)
                phase_b(1)
            else:
                for h2 in range(2):
                    sweep(1, None, w1q, h2)
                    sig_batch(h2)
                    sweep(2, None, w2q, h2)
                    phase_b(h2)
        mark(f"ffn{call}done")

    def emit_attention():
        prev = dict(cnt)
        with nc.psum_tensor("psP0", [128, TH], F32) as psp0, \
             nc.psum_tensor("psP1", [128, TH], F32) as psp1, \
             nc.psum_tensor("psM0", [64, TH], F32) as psm0, \
             nc.psum_tensor("psM1", [64, TH], F32) as psm1, \
             nc.psum_tensor("psTq", [128, 512], BF16) as psTq, \
             nc.psum_tensor("psKV0", [128, 129], F32) as pskv0, \
             nc.psum_tensor("psKV1", [128, 129], F32) as pskv1:
            psP = [psp0, psp1]
            psM = [psm0, psm1]
            psK = psP
            psKV = [pskv0, pskv1]

            for ci in range(32):
                dc, hhalf = divmod(ci, 2)
                p = ci % 2

                svp = "sVe" if ci % 2 == 0 else "sVo"

                def f(a, ci=ci, dc=dc, hhalf=hhalf, p=p, svp=svp):
                    if ci == 0:
                        wait(a, "dv", marks["L2done"]["dv"])
                    if ci >= 2:
                        wait(a, "ac", marks[f"wvc{ci - 2}"]["ac"])
                    bump(a.dma_start(out=WVF[p],
                                     in_=wvt[dc][:, hhalf * 1024:(hhalf + 1) * 1024]),
                         "lC", 16)
                    wait(a, "lC")
                    if ci >= 2:
                        wait(a, svp, marks[f"wvs{ci - 2}"][svp])
                    bump(a.copy(out=WVO[p], in_=WVF[p]), "ac")
                AA(f)
                mark(f"wvc{ci}")

                def f(a, ci=ci, dc=dc, hhalf=hhalf, p=p, svp=svp):
                    wait(a, "ac", marks[f"wvc{ci}"]["ac"])
                    bump(a.dma_start(
                        out=wvb_d[dc][:, hhalf * 1024:(hhalf + 1) * 1024],
                        in_=WVO[p]), svp, 16)
                AA(f)
                mark(f"wvs{ci}")

            def proj_head(h, wsrc, bias_s, tmp, pref, is_q, tmp_guard=None):
                p = h % 2

                def f(s, h=h, p=p, wg=dict(WG)):
                    if h < 2:
                        wguard(s, wg)
                    if h >= 2:
                        wait(s, "dv", marks[f"{pref}cast{h - 2}"]["dv"])
                    bump(s.dma_start(out=wfa[p][:, :], in_=wsrc[h]),
                         "lPe" if h % 2 == 0 else "lPo", 16)
                SS(f)
                mark(f"{pref}ld{h}")

                def f(v, h=h, p=p):
                    lp = "lPe" if h % 2 == 0 else "lPo"
                    wait(v, lp, marks[f"{pref}ld{h}"][lp])
                    if h >= 2:
                        wait(v, "pe", marks[f"{pref}mm{h - 2}"]["pe"])
                    bump(v.tensor_copy(out=wqa[p][:, :], in_=wfa[p][:, :]), "dv")
                VV(f)
                mark(f"{pref}cast{h}")

                def f(t, h=h, p=p):
                    wait(t, "dv", marks[f"{pref}cast{h}"]["dv"])
                    if h == 0:
                        wait(t, "dv", marks["L2done"]["dv"])
                        for nm, v in prev.items():
                            wait(t, nm, v)
                    if h >= 1:
                        wait(t, "dv", marks[f"{pref}ev{h - 1}"]["dv"])
                    for n2 in range(2):
                        for dc in range(DC - 1):
                            t.matmul(psP[n2][:, :],
                                     wqa[p][:, dc * 128:(dc + 1) * 128],
                                     ybuf[:, dc * T + n2 * TH: dc * T + n2 * TH + TH],
                                     start=(dc == 0), stop=False)
                        mm = t.matmul(psP[n2][:, :],
                                      wqa[p][:, (DC - 1) * 128:DC * 128],
                                      ybuf[:, (DC - 1) * T + n2 * TH: (DC - 1) * T + n2 * TH + TH],
                                      start=False, stop=True)
                    bump(mm, "pe")
                TT(f)
                mark(f"{pref}mm{h}")

                def f(v, h=h, p=p):
                    wait(v, "pe", marks[f"{pref}mm{h}"]["pe"])
                    if h >= 2:
                        wait(v, "pe", marks[f"{pref}mp{h - 2}"]["pe"])
                        if tmp_guard is not None:
                            wait(v, "pe", marks[f"{tmp_guard}{h - 2}"]["pe"])
                    for n2 in range(2):
                        bump(v.tensor_scalar(out=tmp[p][:, n2 * TH:(n2 + 1) * TH],
                                             in0=psP[n2][:, :],
                                             scalar1=bias_s[:, h:h + 1],
                                             scalar2=None, op0=ALU.add), "dv")
                VV(f)
                mark(f"{pref}ev{h}")

                wmap = wmapq_b if is_q else wmapk_b

                def f(t, h=h, p=p):
                    wait(t, "dv", marks[f"{pref}ev{h}"]["dv"])
                    if h >= 1:
                        wait(t, "ac", marks[f"{pref}el{h - 1}"]["ac"])
                    for n2 in range(2):
                        mm = t.matmul(psM[n2][:, :], wmap[:, :],
                                      tmp[p][:, n2 * TH:(n2 + 1) * TH],
                                      start=True, stop=True)
                    bump(mm, "pe")
                TT(f)
                mark(f"{pref}mp{h}")

                def f(a, h=h, p=p):
                    wait(a, "pe", marks[f"{pref}mp{h}"]["pe"])
                    if h >= 1:
                        wait(a, "dv", marks[f"{pref}qm{h - 1}"]["dv"])
                    for n2 in range(2):
                        bump(a.activation(out=ELU[0][:, n2 * TH:(n2 + 1) * TH],
                                          in_=psM[n2][:, :], func=AF.Relu), "ac")
                        bump(a.activation(out=ELU[1][:, n2 * TH:(n2 + 1) * TH],
                                          in_=psM[n2][:, :], func=AF.Relu,
                                          scale=negone[0:64, 0:1]), "ac")
                    wait(a, "ac")
                    bump(a.activation(out=ELU[1][:, :], in_=ELU[1][:, :],
                                      func=AF.Exp, scale=negone[0:64, 0:1]), "ac")
                AA(f)
                mark(f"{pref}el{h}")

            wg_update()
            for h in range(NH):
                proj_head(h, wqt, bq_s, QTMP, "q", True)

                def f(v, h=h):
                    wait(v, "ac", marks[f"qel{h}"]["ac"])
                    bump(v.tensor_tensor(
                        out=qmb[(h % 2) * 64:(h % 2) * 64 + 64,
                                (h // 2) * T:(h // 2) * T + T],
                        in0=ELU[0][:, :], in1=ELU[1][:, :], op=ALU.add), "dv")
                VV(f)
                mark(f"qqm{h}")

            wg_update()
            for h in range(NH):
                proj_head(h, wkt, bk_s, KTMP, "k", False, tmp_guard="ktp")

                def f(v, h=h):
                    wait(v, "ac", marks[f"kel{h}"]["ac"])
                    if h >= 1:
                        wait(v, "pe", marks[f"ktp{h - 1}"]["pe"])
                    bump(v.tensor_tensor(
                        out=KTMP[h % 2][0:64, :],
                        in0=ELU[0][:, :], in1=ELU[1][:, :], op=ALU.add), "dv")
                VV(f)
                mark(f"kqm{h}")

                def f(t, h=h):
                    wait(t, "dv", marks[f"kqm{h}"]["dv"])
                    if h >= 1:
                        wait(t, "dv", marks[f"kmv{h - 1}"]["dv"])
                    for mb in range(7):
                        t.transpose(psTq[:, mb * 64:(mb + 1) * 64],
                                    KTMP[h % 2][0:64, mb * 128:(mb + 1) * 128],
                                    identb[0:64, 0:64])
                    bump(t.transpose(psTq[:, 7 * 64:8 * 64],
                                     KTMP[h % 2][0:64, 7 * 128:8 * 128],
                                     identb[0:64, 0:64]), "pe")
                TT(f)
                mark(f"ktp{h}")

                def f(v, h=h):
                    wait(v, "pe", marks[f"ktp{h}"]["pe"])
                    bump(v.tensor_copy(out=KMB[:, h * 512:(h + 1) * 512],
                                       in_=psTq[:, :]), "dv")
                VV(f)
                mark(f"kmv{h}")

            def f(g):
                wait(g, "pe", marks["L2done"]["pe"])
                wait(g, "dv", marks["L2done"]["dv"])
                vv = VBUF.rearrange("p (i h e) -> p i h e", i=2, e=129)
                bump(g.memset(vv[:, :, :, 128:129], 1.0), "pl")
            GG(f)
            mark("vones")

            NG = 4
            for g2 in range(NG):
                for osl in range(8):
                    sidx = g2 * 8 + osl

                    def f(s, g2=g2, osl=osl, sidx=sidx):
                        if sidx == 0:
                            wait(s, "sVe", marks["wvs31"]["sVe"])
                            wait(s, "sVo", marks["wvs31"]["sVo"])
                        if sidx >= 1:
                            wait(s, "pe", marks[f"vmm{sidx - 1}_1"]["pe"])
                        bump(s.dma_start(
                            out=WVS,
                            in_=wvb_d.rearrange("d p c -> p d c")
                            [:, :, osl * 256:(osl + 1) * 256]), "lV", 16)
                    SS(f)
                    mark(f"vld{sidx}")

                    for i in range(2):
                        mchunk = g2 * 2 + i
                        b = i % 2

                        def f(t, g2=g2, osl=osl, sidx=sidx, i=i, mchunk=mchunk, b=b):
                            if sidx == 0 and i == 0:
                                wait(t, "dv", marks[f"kev{NH - 1}"]["dv"])
                                wait(t, "dv", marks[f"kmv{NH - 1}"]["dv"])
                            if osl == 0 and i == 0 and g2 > 0:
                                wait(t, "pe", marks[f"kvmm{g2 - 1}"]["pe"])
                            if i == 0:
                                wait(t, "lV", marks[f"vld{sidx}"]["lV"])
                            if i >= 2 or sidx > 0:
                                pk = (f"vev{sidx}_{i - 2}" if i >= 2
                                      else f"vev{sidx - 1}_{i}")
                                wait(t, "dv", marks[pk]["dv"])
                            for dc in range(DC - 1):
                                t.matmul(psP[b][:, 0:256],
                                         ybuf[:, dc * T + mchunk * 128: dc * T + (mchunk + 1) * 128],
                                         WVS[:, dc * 256:(dc + 1) * 256],
                                         start=(dc == 0), stop=False)
                            t.matmul(psP[b][:, 0:256],
                                     ybuf[:, (DC - 1) * T + mchunk * 128: (DC - 1) * T + (mchunk + 1) * 128],
                                     WVS[:, (DC - 1) * 256:DC * 256],
                                     start=False, stop=False)
                            bump(t.matmul(psP[b][:, 0:256], onesb_r[:],
                                          bv_b[0:1, osl * 256:(osl + 1) * 256],
                                          start=False, stop=True), "pe")
                        TT(f)
                        mark(f"vmm{sidx}_{i}")

                        def f(v, g2=g2, osl=osl, sidx=sidx, i=i, b=b):
                            wait(v, "pe", marks[f"vmm{sidx}_{i}"]["pe"])
                            if sidx == 0 and i == 0:
                                wait(v, "pl", marks["vones"]["pl"])
                            for hh in range(2):
                                h = osl * 2 + hh
                                bump(v.tensor_copy(
                                    out=VBUF[:, i * (NH * 129) + h * 129:
                                             i * (NH * 129) + h * 129 + 128],
                                    in_=psP[b][:, hh * 128:(hh + 1) * 128]), "dv")
                        VV(f)
                        mark(f"vev{sidx}_{i}")

                for h in range(NH):
                    q3 = h // 6
                    j3 = (h % 6) // 2
                    i3 = h % 2
                    kidx = g2 * NH + h

                    def f(t, g2=g2, h=h, i3=i3, kidx=kidx):
                        if h == 0:
                            wait(t, "dv", marks[f"vev{g2 * 8 + 7}_1"]["dv"])
                        if kidx >= 2:
                            wait(t, "dv", marks[f"kvadd{kidx - 2}"]["dv"])
                        for i in range(2):
                            mchunk = g2 * 2 + i
                            mm = t.matmul(
                                psKV[h % 2][i3 * 64:(i3 + 1) * 64, :],
                                KMB[:, h * 512 + mchunk * 64: h * 512 + (mchunk + 1) * 64],
                                VBUF[:, i * (NH * 129) + h * 129:
                                     i * (NH * 129) + (h + 1) * 129],
                                start=(i == 0), stop=(i == 1))
                        bump(mm, "pe")
                    TT(f)
                    mark(f"kvmm{kidx}")

                    def f(v, g2=g2, h=h, q3=q3, j3=j3, i3=i3, kidx=kidx):
                        wait(v, "pe", marks[f"kvmm{kidx}"]["pe"])
                        dst = kvga[i3 * 64:(i3 + 1) * 64,
                                   q3 * 512 + j3 * 129: q3 * 512 + (j3 + 1) * 129]
                        ps = psKV[h % 2][i3 * 64:(i3 + 1) * 64, :]
                        if g2 == 0:
                            bump(v.tensor_copy(out=dst, in_=ps), "dv")
                        else:
                            bump(v.tensor_tensor(out=dst, in0=dst, in1=ps,
                                                 op=ALU.add), "dv")
                    VV(f)
                    mark(f"kvadd{kidx}")

            mark("kvev")

            def f(g):
                wait(g, "dv", marks["kvev"]["dv"])
                bump(g.dma_start(out=cc_kin.ap(), in_=kvga[:]), "g", 16)
                wait(g, "g")
                bump(g.collective_compute(
                    "AllReduce", ALU.add, replica_groups=[list(range(NCORES))],
                    ins=[cc_kin.ap().opt()], outs=[cc_kout.ap().opt()]), "cc", 1)
                wait(g, "cc")
                bump(g.dma_start(out=kvga[:], in_=cc_kout.ap()), "g", 16)
            GG(f)
            mark("kvcc")

            def f(v):
                wait(v, "g", marks["kvcc"]["g"])
                bump(v.tensor_copy(out=kvgb[:], in_=kvga[:]), "dv")
            VV(f)
            mark("kvcast")

            def kvoff(h):
                q3 = h // 6
                j3 = (h % 6) // 2
                i3 = h % 2
                return q3, j3, i3

            for h in range(NH):
                q3, j3, i3 = kvoff(h)

                def f(t, h=h, q3=q3, j3=j3, i3=i3):
                    if h == 0:
                        wait(t, "dv", marks["kvcast"]["dv"])
                    if h >= 1:
                        wait(t, "dv", marks[f"nev{h - 1}"]["dv"])
                    ks = kvgb[i3 * 64:(i3 + 1) * 64,
                              q3 * 512 + j3 * 129 + 128: q3 * 512 + j3 * 129 + 129]
                    for n2 in range(2):
                        mm = t.matmul(psM[n2][0:1, :], ks,
                                      qmb[i3 * 64:(i3 + 1) * 64,
                                          (h // 2) * T + n2 * TH: (h // 2) * T + (n2 + 1) * TH],
                                      start=True, stop=True)
                    bump(mm, "pe")
                TT(f)
                mark(f"nmm{h}")

                def f(v, h=h):
                    wait(v, "pe", marks[f"nmm{h}"]["pe"])
                    if h >= 2:
                        wait(v, "lN", marks[f"nsth{h - 1}"]["lN"])
                    ro = (h % 2) * T
                    bump(v.tensor_copy(out=rows[0:1, ro:ro + TH],
                                       in_=psM[0][0:1, :]), "dv")
                    bump(v.tensor_copy(out=rows[0:1, ro + TH:ro + T],
                                       in_=psM[1][0:1, :]), "dv")
                VV(f)
                mark(f"nev{h}")

                def f(s, h=h):
                    wait(s, "dv", marks[f"nev{h}"]["dv"])
                    ro = (h % 2) * T
                    bump(s.dma_start(out=nrm_d.ap()[h:h + 1, :],
                                     in_=rows[0:1, ro:ro + T]), "lN", 16)
                SS(f)
                mark(f"nsth{h}")

            NB = rows[:, 1024:1152]
            nflat0 = nrm_d.ap().rearrange("a b -> (a b)")

            def f(s):
                wait(s, "lN", marks[f"nsth{NH - 1}"]["lN"])
                bump(s.dma_start(out=NB, in_=nflat0[:]), "lN", 16)
            SS(f)
            mark("nbl")

            def f(v):
                wait(v, "lN", marks["nbl"]["lN"])
                bump(v.tensor_scalar(out=NB, in0=NB, scalar1=1e-6, scalar2=None,
                                     op0=ALU.add), "dv")
                wait(v, "dv")
                bump(v.reciprocal(out=NB, in_=NB), "dv")
            VV(f)
            mark("rnorm")

            def f(s):
                wait(s, "dv", marks["rnorm"]["dv"])
                bump(s.dma_start(out=nflat0[:], in_=NB), "lN", 16)
            SS(f)
            mark("nst")

            nflat = nrm_d.ap().rearrange("a b -> (a b)")

            for h in range(NH):
                q3, j3, i3 = kvoff(h)
                if h % 2 == 0:
                    def f(s, h=h):
                        if h == 0:
                            wait(s, "lN", marks["nst"]["lN"])
                        else:
                            wait(s, "pe", marks[f"rbc{h - 1}"]["pe"])
                        bump(s.dma_start(out=rows[0:1, 0:2048],
                                         in_=nflat[h * T:(h + 2) * T]), "lN", 16)
                    SS(f)
                    mark(f"nld{h}")

                def f(t, h=h):
                    wait(t, "lN", marks[f"nld{h - h % 2}"]["lN"])
                    if h == 0:
                        wait(t, "dv", marks["rnorm"]["dv"])
                    if h >= 1:
                        wait(t, "dv", marks[f"rnb{h - 1}"]["dv"])
                    for n2 in range(2):
                        mm = t.matmul(psM[n2][:, :],
                                      onesf_r[0:1, 0:64],
                                      rows[0:1, (h % 2) * T + n2 * TH:
                                           (h % 2) * T + (n2 + 1) * TH],
                                      start=True, stop=True)
                    bump(mm, "pe")
                TT(f)
                mark(f"rbc{h}")

                def f(v, h=h, i3=i3):
                    wait(v, "pe", marks[f"rbc{h}"]["pe"])
                    if h >= 1:
                        wait(v, "dv", marks[f"qn{h - 1}"]["dv"])
                    bump(v.tensor_copy(out=RNB2[i3][:, 0:TH], in_=psM[0][:, :]), "dv")
                    bump(v.tensor_copy(out=RNB2[i3][:, TH:T], in_=psM[1][:, :]), "dv")
                VV(f)
                mark(f"rnb{h}")

                def f(v, h=h, i3=i3):
                    wait(v, "dv", marks[f"rnb{h}"]["dv"])
                    sl = qmb[i3 * 64:(i3 + 1) * 64, (h // 2) * T:(h // 2) * T + T]
                    bump(v.tensor_tensor(out=sl, in0=sl, in1=RNB2[i3][:, :],
                                         op=ALU.mult), "dv")
                VV(f)
                mark(f"qn{h}")

                def f(t, h=h, q3=q3, j3=j3, i3=i3):
                    wait(t, "dv", marks[f"qn{h}"]["dv"])
                    if h >= 1:
                        wait(t, "dv", marks[f"aev{h - 1}"]["dv"])
                    kv = kvgb[i3 * 64:(i3 + 1) * 64,
                              q3 * 512 + j3 * 129: q3 * 512 + j3 * 129 + 128]
                    for n2 in range(2):
                        mm = t.matmul(psP[n2][:, :], kv,
                                      qmb[i3 * 64:(i3 + 1) * 64,
                                          (h // 2) * T + n2 * TH:(h // 2) * T + (n2 + 1) * TH],
                                      start=True, stop=True)
                    bump(mm, "pe")
                TT(f)
                mark(f"amm{h}")

                def f(v, h=h):
                    wait(v, "pe", marks[f"amm{h}"]["pe"])
                    for n2 in range(2):
                        bump(v.tensor_copy(
                            out=ybuf[:, h * T + n2 * TH: h * T + (n2 + 1) * TH],
                            in_=psP[n2][:, :]), "dv")
                VV(f)
                mark(f"aev{h}")

            wg_update()
            for o in range(DC):
                p = o % 2

                def f(s, o=o, p=p, wg=dict(WG)):
                    if o < 2:
                        wguard(s, wg)
                    if o >= 2:
                        wait(s, "dv", marks[f"ocast{o - 2}"]["dv"])
                    bump(s.dma_start(out=wfa[p][:, :], in_=wot[o]),
                         "lPe" if o % 2 == 0 else "lPo", 16)
                SS(f)
                mark(f"old{o}")

                def f(v, o=o, p=p):
                    lp = "lPe" if o % 2 == 0 else "lPo"
                    wait(v, lp, marks[f"old{o}"][lp])
                    if o >= 2:
                        wait(v, "pe", marks[f"omm{o - 2}"]["pe"])
                    bump(v.tensor_copy(out=wqa[p][:, :], in_=wfa[p][:, :]), "dv")
                VV(f)
                mark(f"ocast{o}")

                def f(t, o=o, p=p):
                    wait(t, "dv", marks[f"ocast{o}"]["dv"])
                    if o == 0:
                        wait(t, "dv", marks["kvev"]["dv"])
                    if o >= 1:
                        wait(t, "dv", marks[f"oep{o - 1}"]["dv"])
                    for n2 in range(2):
                        for dc in range(DC - 1):
                            t.matmul(psK[n2][:, :],
                                     wqa[p][:, dc * 128:(dc + 1) * 128],
                                     ybuf[:, dc * T + n2 * TH: dc * T + n2 * TH + TH],
                                     start=(dc == 0), stop=False)
                        mm = t.matmul(psK[n2][:, :],
                                      wqa[p][:, (DC - 1) * 128:DC * 128],
                                      ybuf[:, (DC - 1) * T + n2 * TH: (DC - 1) * T + n2 * TH + TH],
                                      start=False, stop=True)
                    bump(mm, "pe")
                TT(f)
                mark(f"omm{o}")

                def f(v, o=o, p=p):
                    wait(v, "pe", marks[f"omm{o}"]["pe"])
                    for n2 in range(2):
                        sl = xf[:, o * T + n2 * TH: o * T + (n2 + 1) * TH]
                        bump(v.scalar_tensor_tensor(
                            out=sl, in0=psK[n2][:, :], scalar=bo_s[:, o:o + 1],
                            in1=sl, op0=ALU.add, op1=ALU.add), "dv")
                VV(f)
                mark(f"oep{o}")
        wg_update()
        mark("attdone")

    def emit_output(src):
        with nc.psum_tensor("psT2a", [128, TH], F32) as pst0, \
             nc.psum_tensor("psT2b", [128, TH], F32) as pst1:
            psT2 = [pst0, pst1]
            for m in range(8):
                for qq in range(4):
                    b = qq % 2

                    def f(t, m=m, qq=qq, b=b):
                        if m == 0 and qq == 0:
                            for nm, v in marks[src].items():
                                wait(t, nm, v)
                        if m > 0 or qq >= 2:
                            pk = (f"oev{m}_{qq - 2}" if qq >= 2
                                  else f"oev{m - 1}_{qq + 2}")
                            wait(t, "dv", marks[pk]["dv"])
                        for i in range(4):
                            dc = qq * 4 + i
                            mm = t.transpose(psT2[b][:, i * 128:(i + 1) * 128],
                                             xf[:, dc * T + m * 128: dc * T + (m + 1) * 128],
                                             identf[:])
                        bump(mm, "pe")
                    TT(f)
                    mark(f"otp{m}_{qq}")

                    def f(v, m=m, qq=qq, b=b):
                        wait(v, "pe", marks[f"otp{m}_{qq}"]["pe"])
                        if m > 0:
                            wait(v, "sO", marks[f"ost{m - 1}"]["sO"])
                        bump(v.tensor_copy(out=XOUT[:, qq * 512:(qq + 1) * 512],
                                           in_=psT2[b][:, :]), "dv")
                    VV(f)
                    mark(f"oev{m}_{qq}")

                def f(s, m=m):
                    wait(s, "dv", marks[f"oev{m}_3"]["dv"])
                    bump(s.dma_start(out=out[m * 128:(m + 1) * 128, :], in_=XOUT),
                         "sO", 16)
                SS(f)
                mark(f"ost{m}")

            def f(s):
                wait(s, "sO", cnt["sO"])
            SS(f)

    emit_ln("L1", marks["xf_done"]["dv"])
    emit_ffn(0, "L1")
    if stage >= 2:
        emit_ln("L2", marks["ffn0done"]["dv"])
        emit_attention()
    if stage >= 3:
        emit_ln("L3", marks["attdone"]["dv"])
        emit_ffn(1, "L3")
        emit_output("ffn1done")
    elif stage == 2:
        emit_output("attdone")
    else:
        emit_output("ffn0done")

    st.close()
    return nc


def host_prep(inp):
    c = np.ascontiguousarray
    w1 = np.asarray(inp["L0_ffn_w1"]); w2 = np.asarray(inp["L0_ffn_w2"])
    w3 = np.asarray(inp["L0_ffn_w3"])
    w1t = c(w1.T).reshape(16, 128, 44, 128).transpose(2, 1, 0, 3)
    w2t = c(w2.T).reshape(16, 128, 44, 128).transpose(2, 1, 0, 3)
    w3t = c(w3.T).reshape(44, 128, 16, 128).transpose(2, 1, 0, 3)
    w1t = c(w1t.reshape(44, 128, 2048))
    w2t = c(w2t.reshape(44, 128, 2048))
    w3t = c(w3t.reshape(16, 128, 5632))
    wq = np.asarray(inp["L0_attn_wq"]); wk = np.asarray(inp["L0_attn_wk"])
    wv = np.asarray(inp["L0_attn_wv"]); wo = np.asarray(inp["L0_attn_wo"])
    mk = lambda w: c(c(w.T).reshape(16, 128, 16, 128).transpose(2, 1, 0, 3)
                     .reshape(16, 128, 2048))
    base = {
        "w1t": w1t, "w2t": w2t, "w3t": w3t,
        "wqt": mk(wq), "wkt": mk(wk), "wot": mk(wo),
        "wvt": c(wv.T.reshape(16, 128, 2048)),
        "wmapqt": c(np.asarray(inp["L0_attn_wmap_q"]).T),
        "wmapkt": c(np.asarray(inp["L0_attn_wmap_k"]).T),
        "bqt": c(np.asarray(inp["L0_attn_bq"]).reshape(16, 128).T),
        "bkt": c(np.asarray(inp["L0_attn_bk"]).reshape(16, 128).T),
        "bot": c(np.asarray(inp["L0_attn_bo"]).reshape(16, 128).T),
        "bvr": c(np.asarray(inp["L0_attn_bv"]).reshape(1, 2048)),
    }
    g1 = w1t.reshape(NCORES, 128, GSL)
    g2 = w2t.reshape(NCORES, 128, GSL)
    g3 = w3t.reshape(NCORES, 128, GSL)
    x = np.asarray(inp["L0_attn_hidden_states"]).reshape(8192, 2048)
    maps = []
    for ci in range(NCORES):
        m = dict(base)
        m["xin"] = c(x[ci * T:(ci + 1) * T, :])
        m["gsl"] = c(np.stack([g1[ci], g2[ci], g3[ci]]))
        maps.append(m)
    return maps


def kernel(**inputs):
    maps = host_prep(inputs)
    nc = build(STAGE)
    res = run_bass_kernel_spmd(nc, maps, core_ids=list(range(NCORES)))
    outs = [res.results[i]["out"] for i in range(NCORES)]
    return np.concatenate(outs, 0).reshape(1, 8192, 2048)


if __name__ == "__main__":
    import sys
    stg = int(sys.argv[1]) if len(sys.argv) > 1 else STAGE
    nc = build(stg)
    n = sum(1 for _ in nc.all_instructions())
    print(f"build ok (stage {stg}), instructions: {n}")


# version 12
# speedup vs baseline: 1.0075x; 1.0075x over previous
import contextlib
import numpy as np

import concourse.bass as bass
import concourse.mybir as mybir
from concourse.bass_utils import run_bass_kernel_spmd

F32 = mybir.dt.float32
BF16 = mybir.dt.bfloat16
AF = mybir.ActivationFunctionType
ALU = mybir.AluOpType

NCORES = 8
T = 1024
TH = 512
D = 2048
DC = 16
HID = 5632
HJ = 44
NH = 16
MM = 64
NW = float(HID * D)
GSL = HJ * 128 * 2048 // NCORES // 128

STAGE = 3


def build(stage=STAGE):
    nc = bass.Bass("TRN2", target_bir_lowering=False,
                   dynamic_dma_scratch_size=2048)

    dt_in = lambda n, s: nc.dram_tensor(n, s, F32, kind="ExternalInput").ap()
    xin = dt_in("xin", [T, D])
    w1t = dt_in("w1t", [HJ, 128, 2048])
    w2t = dt_in("w2t", [HJ, 128, 2048])
    w3t = dt_in("w3t", [DC, 128, HID])
    gsl = dt_in("gsl", [3, 128, GSL])
    wqt = dt_in("wqt", [NH, 128, 2048])
    wkt = dt_in("wkt", [NH, 128, 2048])
    wvt = dt_in("wvt", [DC, 128, 2048])
    wot = dt_in("wot", [DC, 128, 2048])
    wmapqt = dt_in("wmapqt", [128, MM])
    wmapkt = dt_in("wmapkt", [128, MM])
    bqt = dt_in("bqt", [128, NH])
    bkt = dt_in("bkt", [128, NH])
    bvr = dt_in("bvr", [1, D])
    bot = dt_in("bot", [128, NH])
    out = nc.dram_tensor("out", [T, D], F32, kind="ExternalOutput").ap()

    w1q = nc.dram_tensor("w1q", [HJ, 128, 2048], BF16).ap()
    w2q = nc.dram_tensor("w2q", [HJ, 128, 2048], BF16).ap()
    w3q = nc.dram_tensor("w3q", [DC, 128, HID], BF16).ap()
    wvb_d = nc.dram_tensor("wvb_d", [DC, 128, 2048], BF16).ap()
    cc_gin = nc.dram_tensor("cc_gin", [1, 4], F32)
    cc_gout = nc.dram_tensor("cc_gout", [1, 4], F32)
    cc_kin = nc.dram_tensor("cc_kin", [128, 1536], F32)
    cc_kout = nc.dram_tensor("cc_kout", [128, 1536], F32)
    nrm_d = nc.dram_tensor("nrm_d", [NH, T], F32)

    st = contextlib.ExitStack()
    sb = lambda name, shape, dt: st.enter_context(nc.sbuf_tensor(name, shape, dt))

    xf = sb("xf", [128, DC * T], F32)
    ybuf = sb("ybuf", [128, DC * T], BF16)
    wfa = [sb("wfa0", [128, 2048], F32), sb("wfa1", [128, 2048], F32)]
    wqa = [sb("wqa0", [128, 2048], BF16), sb("wqa1", [128, 2048], BF16)]
    AR = 18960
    arena = sb("arena", [128, AR], F32)
    x2t = [sb("x2t0", [128, TH], BF16), sb("x2t1", [128, TH], BF16)]
    sqt = sb("sqt", [128, TH], BF16)
    xbt = [sb("xbt0", [128, TH], BF16), sb("xbt1", [128, TH], BF16)]
    nrm = sb("nrm", [NH, T], F32)
    onesb_c = sb("onesb_c", [128, 1], BF16)
    onesb_r = sb("onesb_r", [1, 128], BF16)
    onesf_r = sb("onesf_r", [1, 128], F32)
    onesf_c = sb("onesf_c", [128, 1], F32)
    thrv = sb("thrv", [128, 8], F32)
    gpart = sb("gpart", [128, 2], F32)
    gprow = sb("gprow", [1, 4], F32)
    identf = sb("identf", [128, 128], F32)
    identb = sb("identb", [128, 128], BF16)
    wmapq_b = sb("wmapq_b", [128, MM], BF16)
    wmapk_b = sb("wmapk_b", [128, MM], BF16)
    wmap_f = sb("wmap_f", [128, MM], F32)
    bq_s = sb("bq_s", [128, NH], F32)
    bk_s = sb("bk_s", [128, NH], F32)
    bo_s = sb("bo_s", [128, NH], F32)
    negone = sb("negone", [128, 1], F32)
    halfc = sb("halfc", [128, 1], F32)
    qmb = sb("qmb", [128, (NH // 2) * T], BF16)
    bv_f = None
    bv_b = None

    def cf(c0, n):
        assert c0 + n <= AR
        return arena[:, c0:c0 + n]

    def cb(c0, nb):
        assert c0 + nb // 2 <= AR
        return arena[:, c0:c0 + nb // 2].bitcast(BF16)

    GH = cb(0, HJ * TH)
    SQW = [cb(11264, 2048), cb(12288, 2048)]
    G2B = [cf(13376, 2048), cf(15424, 2048)]
    W3R = [cb(13376, 2816), cb(14784, 2816)]
    rows = cf(16192, 2048)
    W3I = [cf(11264, 1408), cf(16192, 1408), qmb[:, 0:2816].bitcast(F32)]
    W3Rq = [W3R[0][:, 0:1408], W3R[1][:, 0:1408]]
    SQB3 = [W3R[0][:, 1408:2816], W3R[1][:, 1408:2816]]
    XB = cb(0, 16 * TH)
    SQ = cb(4096, 16 * TH)
    XTOK = [cf(c, 2048) for c in (0, 2048, 4096, 6144)]
    GAMS = cf(8192, GSL // 2)
    XOUT = cf(0, 2048)
    VBUF = cb(0, 2 * NH * 129)
    KMB = cb(2064, NH * 512)
    KTMP = [cb(6160, T), cb(6672, T)]
    WVS = cb(7184, DC * 256)
    QTMP = [cb(9232, T), cb(9744, T)]
    ELU = [cf(10256, T)[0:64, :], cf(11280, T)[0:64, :]]
    kvga = cf(12304, 1536)
    kvgb = cb(13840, 1536)
    RNBT = cb(15376, T)
    RNB2 = [RNBT[0:64, :], RNBT[64:128, :]]
    WVF = [cf(15888, 1024), cf(16912, 1024)]
    WVO = [cb(17936, 1024), cb(18448, 1024)]

    bv_f = wfa[0][0:1, 0:D]
    bv_b = nrm[0:1, :].bitcast(BF16)

    sems = {}
    for nm in ("lA", "lX", "lWe", "lWo", "lBe", "lBo", "lPe", "lPo",
               "lV", "lN", "sO", "lG", "l3e", "l3o", "lC",
               "sWe", "sWo", "sW2", "sW3", "sQe", "sQo", "sVe", "sVo",
               "g", "cc", "pe", "ac", "dv", "pl"):
        sems[nm] = st.enter_context(nc.semaphore("s_" + nm))
    cnt = {k: 0 for k in sems}

    def bump(inst, nm, k=1):
        inst.then_inc(sems[nm], k)
        cnt[nm] += k
        return cnt[nm]

    def wait(eng, nm, v=None):
        eng.wait_ge(sems[nm], cnt[nm] if v is None else v)

    block = st.enter_context(nc.Block())
    TT = block.tensor
    VV = block.vector
    AA = block.scalar
    GG = block.gpsimd
    SS = block.sync

    marks = {}

    def mark(key):
        marks[key] = dict(cnt)

    WG = {"dv": 0, "pe": 0, "ac": 0, "sWe": 0, "sWo": 0, "sW2": 0, "sW3": 0}

    def wguard(eng, wg):
        for nm, v in wg.items():
            if v:
                eng.wait_ge(sems[nm], v)

    def wg_update():
        for nm in WG:
            WG[nm] = cnt[nm]

    def f(g):
        for tile, val in ((identf, 0.0), (identb, 0.0), (onesb_c, 1.0),
                          (onesb_r, 1.0), (onesf_r, 1.0), (onesf_c, 1.0),
                          (negone, -1.0), (halfc, 0.5)):
            bump(g.memset(tile[:], val), "pl")
        wait(g, "pl")
        bump(g.affine_select(out=identf[:], in_=identf[:], compare_op=ALU.not_equal,
                             fill=1.0, base=0, pattern=[[-1, 128]],
                             channel_multiplier=1), "pl")
        bump(g.affine_select(out=identb[:], in_=identb[:], compare_op=ALU.not_equal,
                             fill=1.0, base=0, pattern=[[-1, 128]],
                             channel_multiplier=1), "pl")
    GG(f)
    mark("consts")

    def f(s):
        for dst, src in ((bq_s[:], bqt), (bk_s[:], bkt), (bo_s[:], bot),
                         (bv_f[:], bvr), (wmap_f[:], wmapqt)):
            bump(s.dma_start(out=dst, in_=src), "lA", 16)
    SS(f)
    mark("smallld")

    def f(a):
        wait(a, "lA", marks["smallld"]["lA"])
        bump(a.copy(out=wmapq_b[:], in_=wmap_f[:]), "ac")
        bump(a.copy(out=bv_b[:], in_=bv_f[:]), "ac")
    AA(f)
    mark("wmapq_cast")

    def f(s):
        wait(s, "ac", marks["wmapq_cast"]["ac"])
        bump(s.dma_start(out=wmap_f[:], in_=wmapkt), "lA", 16)
    SS(f)
    mark("wmapk_ld")

    def f(a):
        wait(a, "lA", marks["wmapk_ld"]["lA"])
        bump(a.copy(out=wmapk_b[:], in_=wmap_f[:]), "ac")
    AA(f)
    mark("wmapk_cast")

    with nc.psum_tensor("ps_g", [1, 4], F32) as ps_g, \
         nc.psum_tensor("ps_thr", [128, 8], F32) as ps_thr, \
         nc.psum_tensor("ps_xt0", [128, 512], F32) as ps_xt0, \
         nc.psum_tensor("ps_xt1", [128, 512], F32) as ps_xt1:
        ps_xt = [ps_xt0, ps_xt1]

        GH2 = GSL // 2
        for w in range(3):
            for hf in range(2):
                def f(a, w=w, hf=hf):
                    if w + hf > 0:
                        pw, ph = (w, hf - 1) if hf else (w - 1, 1)
                        wait(a, "dv", marks[f"gred{pw}_{ph}"]["dv"])
                    bump(a.dma_start(out=GAMS, in_=gsl[w][:, hf * GH2:(hf + 1) * GH2]),
                         "lG", 16)
                AA(f)
                mark(f"gld{w}_{hf}")

                def f(v, w=w, hf=hf):
                    wait(v, "lG", marks[f"gld{w}_{hf}"]["lG"])
                    if hf == 0 and w > 0:
                        wait(v, "pe", marks[f"gtot{w - 1}"]["pe"])
                    bump(v.reduce_sum(out=gpart[:, hf:hf + 1], in_=GAMS,
                                      axis=mybir.AxisListType.X,
                                      apply_absolute_value=True), "dv")
                VV(f)
                mark(f"gred{w}_{hf}")

            def f(t, w=w):
                wait(t, "dv", marks[f"gred{w}_1"]["dv"])
                if w == 0:
                    wait(t, "pl", marks["consts"]["pl"])
                bump(t.matmul(ps_g[0:1, w:w + 1], gpart[:, 0:1], onesf_c[:],
                              start=True, stop=False), "pe")
                bump(t.matmul(ps_g[0:1, w:w + 1], gpart[:, 1:2], onesf_c[:],
                              start=False, stop=True), "pe")
            TT(f)
            mark(f"gtot{w}")

        def f(v):
            bump(v.memset(gprow[:], 0.0), "dv")
            wait(v, "pe", marks["gtot2"]["pe"])
            wait(v, "dv")
            bump(v.tensor_copy(out=gprow[0:1, 0:3], in_=ps_g[0:1, 0:3]), "dv")
        VV(f)
        mark("gpack")

        def f(g):
            wait(g, "dv", marks["gpack"]["dv"])
            bump(g.dma_start(out=cc_gin.ap(), in_=gprow[:]), "g", 16)
            wait(g, "g")
            bump(g.collective_compute(
                "AllReduce", ALU.add, replica_groups=[list(range(NCORES))],
                ins=[cc_gin.ap().opt()], outs=[cc_gout.ap().opt()]), "cc", 1)
            wait(g, "cc")
            bump(g.dma_start(out=gprow[:], in_=cc_gout.ap()), "g", 16)
        GG(f)
        mark("gcc")

        def f(v):
            wait(v, "g", marks["gcc"]["g"])
            bump(v.tensor_scalar(out=rows[0:1, 0:4], in0=gprow[:],
                                 scalar1=0.5 / NW, scalar2=0.5e-8,
                                 op0=ALU.mult, op1=ALU.add), "dv")
            bump(v.tensor_scalar(out=rows[0:1, 4:8], in0=gprow[:],
                                 scalar1=-0.5 / NW, scalar2=-0.5e-8,
                                 op0=ALU.mult, op1=ALU.add), "dv")
        VV(f)
        mark("throw")

        def f(t):
            wait(t, "dv", marks["throw"]["dv"])
            bump(t.matmul(ps_thr[:, :], onesf_r[:], rows[0:1, 0:8],
                          start=True, stop=True), "pe")
        TT(f)
        mark("thrbc")

        def f(v):
            wait(v, "pe", marks["thrbc"]["pe"])
            bump(v.tensor_copy(out=thrv[:], in_=ps_thr[:, :]), "dv")
        VV(f)
        mark("thr")
        THR_DV = cnt["dv"]

        for grp in range(2):
            def f(s, grp=grp):
                if grp == 1:
                    wait(s, "pe", marks["xt0"]["pe"])
                for i in range(4):
                    m = grp * 4 + i
                    bump(s.dma_start(out=XTOK[i],
                                     in_=xin[m * 128:(m + 1) * 128, :]), "lX", 16)
            SS(f)
            mark(f"xld{grp}")

            for dc in range(DC):
                b = dc % 2

                def f(t, grp=grp, dc=dc, b=b):
                    if dc == 0:
                        wait(t, "lX", marks[f"xld{grp}"]["lX"])
                        if grp == 0:
                            wait(t, "pl", marks["consts"]["pl"])
                    if dc >= 2 or grp == 1:
                        pk = f"xtev{grp}_{dc - 2}" if dc >= 2 else f"xtev0_{DC - 2 + dc}"
                        wait(t, "dv", marks[pk]["dv"])
                    for i in range(3):
                        t.transpose(ps_xt[b][:, i * 128:(i + 1) * 128],
                                    XTOK[i][:, dc * 128:(dc + 1) * 128], identf[:])
                    bump(t.transpose(ps_xt[b][:, 3 * 128:4 * 128],
                                     XTOK[3][:, dc * 128:(dc + 1) * 128],
                                     identf[:]), "pe")
                TT(f)
                mark(f"xtp{grp}_{dc}")

                def f(v, grp=grp, dc=dc, b=b):
                    wait(v, "pe", marks[f"xtp{grp}_{dc}"]["pe"])
                    bump(v.tensor_copy(
                        out=xf[:, dc * T + grp * 512: dc * T + grp * 512 + 512],
                        in_=ps_xt[b][:, :]), "dv")
                VV(f)
                mark(f"xtev{grp}_{dc}")
            mark(f"xt{grp}")
    mark("xf_done")

    def emit_ln(tag, src_dv):
        with nc.psum_tensor("psS_" + tag, [1, TH], F32) as psS, \
             nc.psum_tensor("psQ_" + tag, [1, TH], F32) as psQ, \
             nc.psum_tensor("psB1_" + tag, [128, TH], F32) as psB1, \
             nc.psum_tensor("psB2_" + tag, [128, TH], F32) as psB2:
            for h2 in range(2):
                off = h2 * TH

                def f(v, off=off):
                    if off == 0:
                        wait(v, "dv", src_dv)
                    else:
                        wait(v, "pe", marks[f"{tag}mq_0"]["pe"])
                    for dc in range(DC):
                        bump(v.tensor_copy(out=XB[:, dc * TH:(dc + 1) * TH],
                                           in_=xf[:, dc * T + off: dc * T + off + TH]),
                             "dv")
                VV(f)
                mark(f"{tag}xb_{off}")

                def f(a, off=off):
                    wait(a, "dv", marks[f"{tag}xb_{off}"]["dv"])
                    bump(a.activation(out=SQ[:, :], in_=XB[:, :],
                                      func=AF.Square), "ac")
                AA(f)
                mark(f"{tag}sq_{off}")

                def f(t, off=off):
                    wait(t, "ac", marks[f"{tag}sq_{off}"]["ac"])
                    if off:
                        wait(t, "dv", marks[f"{tag}var_0"]["dv"])
                    for dc in range(DC):
                        t.matmul(psS[0:1, :], onesb_c[:], XB[:, dc * TH:(dc + 1) * TH],
                                 start=(dc == 0), stop=(dc == DC - 1))
                    for dc in range(DC - 1):
                        t.matmul(psQ[0:1, :], onesb_c[:], SQ[:, dc * TH:(dc + 1) * TH],
                                 start=(dc == 0), stop=False)
                    bump(t.matmul(psQ[0:1, :], onesb_c[:],
                                  SQ[:, (DC - 1) * TH:DC * TH],
                                  start=False, stop=True), "pe")
                TT(f)
                mark(f"{tag}mq_{off}")

                def f(v, off=off):
                    wait(v, "pe", marks[f"{tag}mq_{off}"]["pe"])
                    r = rows
                    bump(v.tensor_scalar(out=r[0:1, 0:TH], in0=psS[0:1, :],
                                         scalar1=1.0 / D, scalar2=None,
                                         op0=ALU.mult), "dv")
                    bump(v.tensor_scalar(out=r[0:1, TH:2 * TH], in0=psQ[0:1, :],
                                         scalar1=1.0 / D, scalar2=None,
                                         op0=ALU.mult), "dv")
                    wait(v, "dv")
                    bump(v.tensor_tensor(out=r[0:1, 2 * TH:3 * TH], in0=r[0:1, 0:TH],
                                         in1=r[0:1, 0:TH], op=ALU.mult), "dv")
                    wait(v, "dv")
                    bump(v.tensor_scalar(out=r[0:1, 2 * TH:3 * TH],
                                         in0=r[0:1, 2 * TH:3 * TH],
                                         scalar1=-1.0, scalar2=1e-5,
                                         op0=ALU.mult, op1=ALU.add), "dv")
                    wait(v, "dv")
                    bump(v.tensor_tensor(out=r[0:1, 2 * TH:3 * TH],
                                         in0=r[0:1, TH:2 * TH],
                                         in1=r[0:1, 2 * TH:3 * TH],
                                         op=ALU.add), "dv")
                VV(f)
                mark(f"{tag}var_{off}")

                def f(a, off=off):
                    wait(a, "dv", marks[f"{tag}var_{off}"]["dv"])
                    bump(a.activation(out=rows[0:1, 3 * TH:4 * TH],
                                      in_=rows[0:1, 2 * TH:3 * TH],
                                      func=AF.Sqrt), "ac")
                AA(f)
                mark(f"{tag}sd_{off}")

                def f(v, off=off):
                    wait(v, "ac", marks[f"{tag}sd_{off}"]["ac"])
                    r = rows
                    bump(v.reciprocal(out=r[0:1, TH:2 * TH],
                                      in_=r[0:1, 3 * TH:4 * TH]), "dv")
                    wait(v, "dv")
                    bump(v.tensor_tensor(out=r[0:1, 2 * TH:3 * TH], in0=r[0:1, 0:TH],
                                         in1=r[0:1, TH:2 * TH],
                                         op=ALU.mult), "dv")
                    wait(v, "dv")
                    bump(v.tensor_scalar(out=r[0:1, 2 * TH:3 * TH],
                                         in0=r[0:1, 2 * TH:3 * TH],
                                         scalar1=-1.0, scalar2=None,
                                         op0=ALU.mult), "dv")
                VV(f)
                mark(f"{tag}rs_{off}")

                def f(t, off=off):
                    wait(t, "dv", marks[f"{tag}rs_{off}"]["dv"])
                    bump(t.matmul(psB1[:, :], onesf_r[:], rows[0:1, TH:2 * TH],
                                  start=True, stop=True), "pe")
                    bump(t.matmul(psB2[:, :], onesf_r[:], rows[0:1, 2 * TH:3 * TH],
                                  start=True, stop=True), "pe")
                TT(f)
                mark(f"{tag}bc_{off}")

                for dc in range(DC):
                    def f(v, dc=dc, off=off):
                        if dc == 0:
                            wait(v, "pe", marks[f"{tag}bc_{off}"]["pe"])
                        else:
                            wait(v, "dv")
                        bump(v.tensor_tensor(out=G2B[0][:, 0:TH],
                                             in0=xf[:, dc * T + off: dc * T + off + TH],
                                             in1=psB1[:, :], op=ALU.mult), "dv")
                        wait(v, "dv")
                        bump(v.tensor_tensor(
                            out=ybuf[:, dc * T + off: dc * T + off + TH],
                            in0=G2B[0][:, 0:TH], in1=psB2[:, :], op=ALU.add), "dv")
                    VV(f)
                    mark(f"{tag}y{dc}_{off}")
        mark(f"{tag}done")

    def emit_ffn(call, lntag):
        with nc.psum_tensor(f"psX{call}a", [128, TH], F32) as psxa, \
             nc.psum_tensor(f"psX{call}b", [128, TH], F32) as psxb, \
             nc.psum_tensor(f"psO{call}a", [128, TH], F32) as psoa, \
             nc.psum_tensor(f"psO{call}b", [128, TH], F32) as psob:
            psX = [psxa, psxb]
            psO = [psoa, psob]
            guard = {"psX": None, "psO": None, "GH": None, "W3R": None}

            def sweep(sw, wsrc, wscr, h2):
                off = h2 * TH
                quant = (call == 0 and h2 == 0)
                key = f"F{call}s{sw}h{h2}"
                widx = 0 if sw == 1 else 1
                if quant:
                    wfaH = [wfa[s // 2][:, (s % 2) * 1024:(s % 2 + 1) * 1024]
                            for s in range(4)]
                    wqaH = [wqa[s // 2][:, (s % 2) * 1024:(s % 2 + 1) * 1024]
                            for s in range(4)]
                    SQWH = [SQW[s // 2][:, (s % 2) * 1024:(s % 2 + 1) * 1024]
                            for s in range(4)]
                    SLS = ("sWe", "sWo", "sW2", "sW3")
                    for jh in range(2 * HJ):
                        j, hf = jh // 2, jh % 2
                        sl = jh % 4
                        lw = "lWe" if jh % 2 == 0 else "lWo"
                        sws = SLS[sl]

                        def f(s, jh=jh, j=j, hf=hf, sl=sl, sw=sw,
                              lw=lw, wg=dict(WG)):
                            if jh < 4:
                                wguard(s, wg)
                                if sw == 2:
                                    wait(s, "ac", marks["F0s1h0end"]["ac"])
                            else:
                                wait(s, "ac", marks[f"{key}s2_{jh - 4}"]["ac"])
                            bump(s.dma_start(
                                out=wfaH[sl],
                                in_=wsrc[j][:, hf * 1024:(hf + 1) * 1024]),
                                lw, 16)
                        SS(f)
                        mark(f"{key}l{jh}")

                        def f(a, jh=jh, sl=sl, sw=sw, widx=widx,
                              lw=lw, sws=sws):
                            wait(a, lw, marks[f"{key}l{jh}"][lw])
                            wait(a, "dv", THR_DV)
                            if jh >= 4:
                                wait(a, "pe", marks[f"{key}m{jh - 4}"]["pe"])
                                wait(a, sws, marks[f"{key}w{jh - 4}"][sws])
                                wait(a, "dv", marks[f"{key}q{jh - 4}"]["dv"])
                            elif sw == 2:
                                wait(a, "pe", marks["F0s1h0end"]["pe"])
                                wait(a, sws, marks["F0s1h0end"][sws])
                                wait(a, "dv", marks["F0s1h0end"]["dv"])
                            bump(a.activation(
                                out=wqaH[sl][:, :], in_=wfaH[sl][:, :],
                                func=AF.Sign,
                                bias=thrv[:, 4 + widx:5 + widx]), "ac")
                            wait(a, "ac")
                            bump(a.activation(
                                out=SQWH[sl][:, :], in_=wfaH[sl][:, :],
                                func=AF.Sign,
                                bias=thrv[:, widx:widx + 1]), "ac")
                        AA(f)
                        mark(f"{key}s2_{jh}")

                        def f(v, jh=jh, sl=sl):
                            wait(v, "ac", marks[f"{key}s2_{jh}"]["ac"])
                            bump(v.tensor_tensor(
                                out=wqaH[sl][:, :], in0=wqaH[sl][:, :],
                                in1=SQWH[sl][:, :], op=ALU.add), "dv")
                        VV(f)
                        mark(f"{key}q{jh}")

                        def f(s, jh=jh, j=j, hf=hf, sl=sl, sws=sws):
                            wait(s, "dv", marks[f"{key}q{jh}"]["dv"])
                            bump(s.dma_start(
                                out=wscr[j][:, hf * 1024:(hf + 1) * 1024],
                                in_=wqaH[sl][:, :]), sws, 16)
                        SS(f)
                        mark(f"{key}w{jh}")

                        def f(t, jh=jh, j=j, hf=hf, sl=sl):
                            wait(t, "dv", marks[f"{key}q{jh}"]["dv"])
                            if jh == 0:
                                wait(t, "dv", marks[f"{lntag}done"]["dv"])
                                if guard["psX"] is not None:
                                    for nm, v in guard["psX"].items():
                                        wait(t, nm, v)
                            if hf == 0 and j >= 2:
                                wait(t, "dv", marks[f"{key}c{j - 2}"]["dv"])
                            pj = j % 2
                            for di in range(8):
                                dc = hf * 8 + di
                                mm = t.matmul(
                                    psX[pj][:, :],
                                    wqaH[sl][:, di * 128:(di + 1) * 128],
                                    ybuf[:, dc * T + off: dc * T + off + TH],
                                    start=(dc == 0), stop=(dc == DC - 1))
                            bump(mm, "pe")
                        TT(f)
                        mark(f"{key}m{jh}")

                        if hf == 1:
                            if sw == 1:
                                def f(v, j=j, jh=jh):
                                    wait(v, "pe", marks[f"{key}m{jh}"]["pe"])
                                    if j == 0 and guard["GH"] is not None:
                                        wait(v, "pe", guard["GH"])
                                    bump(v.tensor_copy(
                                        out=GH[:, j * TH:(j + 1) * TH],
                                        in_=psX[j % 2][:, :]), "dv")
                                VV(f)
                                mark(f"{key}c{j}")
                            else:
                                def f(v, j=j, jh=jh):
                                    wait(v, "pe", marks[f"{key}m{jh}"]["pe"])
                                    if j == 0:
                                        wait(v, "ac",
                                             marks[f"sig{call}_{h2}"]["ac"])
                                    if j >= 2:
                                        wait(v, "dv", marks[f"{key}h{j - 2}"]["dv"])
                                    bump(v.tensor_copy(out=x2t[j % 2][:, :],
                                                       in_=psX[j % 2][:, :]), "dv")
                                    wait(v, "dv")
                                    bump(v.tensor_tensor(
                                        out=GH[:, j * TH:(j + 1) * TH],
                                        in0=GH[:, j * TH:(j + 1) * TH],
                                        in1=x2t[j % 2][:, :],
                                        op=ALU.mult), "dv")
                                VV(f)
                                mark(f"{key}c{j}")
                                marks[f"{key}h{j}"] = dict(cnt)
                    mark(f"{key}end")
                    guard["psX"] = {"dv": cnt["dv"]}
                    wg_update()
                    return
                for j in range(HJ):
                    p = j % 2
                    if True:
                        def f(s, j=j, p=p, sw=sw, wg=dict(WG)):
                            if j < 2:
                                wguard(s, wg)
                                if call == 0 and h2 == 1 and j == 0:
                                    kh = f"F0s{sw}h0"
                                    for snm in ("sWe", "sWo", "sW2", "sW3"):
                                        wait(s, snm, marks[f"{kh}end"][snm])
                            if j >= 2:
                                wait(s, "pe", marks[f"{key}m{j - 2}"]["pe"])
                            bump(s.dma_start(out=wqa[p][:, :], in_=wscr[j]),
                                 "lWe" if j % 2 == 0 else "lWo", 16)
                        SS(f)
                        mark(f"{key}l{j}")

                    def f(t, j=j, p=p):
                        if quant:
                            wait(t, "dv", marks[f"{key}q{j}"]["dv"])
                        else:
                            lw = "lWe" if j % 2 == 0 else "lWo"
                            wait(t, lw, marks[f"{key}l{j}"][lw])
                        if j == 0:
                            wait(t, "dv", marks[f"{lntag}done"]["dv"])
                            if guard["psX"] is not None:
                                for nm, v in guard["psX"].items():
                                    wait(t, nm, v)
                        if j >= 2:
                            wait(t, "dv", marks[f"{key}c{j - 2}"]["dv"])
                        for dc in range(DC - 1):
                            t.matmul(psX[p][:, :],
                                     wqa[p][:, dc * 128:(dc + 1) * 128],
                                     ybuf[:, dc * T + off: dc * T + off + TH],
                                     start=(dc == 0), stop=False)
                        bump(t.matmul(psX[p][:, :],
                                      wqa[p][:, (DC - 1) * 128:DC * 128],
                                      ybuf[:, (DC - 1) * T + off: (DC - 1) * T + off + TH],
                                      start=False, stop=True), "pe")
                    TT(f)
                    mark(f"{key}m{j}")

                    if sw == 1:
                        def f(v, j=j, p=p):
                            wait(v, "pe", marks[f"{key}m{j}"]["pe"])
                            if j == 0 and guard["GH"] is not None:
                                wait(v, "pe", guard["GH"])
                            bump(v.tensor_copy(out=GH[:, j * TH:(j + 1) * TH],
                                               in_=psX[p][:, :]), "dv")
                        VV(f)
                        mark(f"{key}c{j}")
                    else:
                        def f(v, j=j, p=p):
                            wait(v, "pe", marks[f"{key}m{j}"]["pe"])
                            if j == 0:
                                wait(v, "ac", marks[f"sig{call}_{h2}"]["ac"])
                            if j >= 2:
                                wait(v, "dv", marks[f"{key}h{j - 2}"]["dv"])
                            bump(v.tensor_copy(out=x2t[p][:, :],
                                               in_=psX[p][:, :]), "dv")
                            wait(v, "dv")
                            bump(v.tensor_tensor(out=GH[:, j * TH:(j + 1) * TH],
                                                 in0=GH[:, j * TH:(j + 1) * TH],
                                                 in1=x2t[p][:, :],
                                                 op=ALU.mult), "dv")
                        VV(f)
                        mark(f"{key}c{j}")
                        marks[f"{key}h{j}"] = dict(cnt)
                mark(f"{key}end")
                guard["psX"] = {"dv": cnt["dv"]}
                wg_update()

            def sig_batch(h2):
                def f(a, h2=h2):
                    wait(a, "dv", marks[f"F{call}s1h{h2}c{HJ - 1}"]["dv"])
                    bump(a.activation(out=GH[:, :], in_=GH[:, :],
                                      func=AF.Sigmoid,
                                      scale=halfc[:, 0:1]), "ac")
                AA(f)
                mark(f"sig{call}_{h2}")

            def phase_b(h2):
                off = h2 * TH
                key = f"F{call}B{h2}"
                qmode = (call == 0 and h2 == 0)
                if qmode:
                    QB = HJ // 4
                    for o in range(DC):
                        for qq in range(4):
                            bidx = o * 4 + qq
                            p = bidx % 2
                            sin = bidx % 3

                            def f(s, o=o, qq=qq, bidx=bidx, sin=sin):
                                if bidx < 3:
                                    wait(s, "dv", marks["F0s2h0end"]["dv"])
                                else:
                                    wait(s, "ac",
                                         marks[f"{key}s2_{bidx - 3}"]["ac"])
                                bump(s.dma_start(
                                    out=W3I[sin],
                                    in_=w3t[o][:, qq * 1408:(qq + 1) * 1408]),
                                    "lBe" if bidx % 2 == 0 else "lBo", 16)
                            SS(f)
                            mark(f"{key}l{bidx}")

                            def f(a, bidx=bidx, p=p, sin=sin):
                                lb = "lBe" if bidx % 2 == 0 else "lBo"
                                wait(a, lb, marks[f"{key}l{bidx}"][lb])
                                sqp = "sQe" if bidx % 2 == 0 else "sQo"
                                if bidx >= 2:
                                    wait(a, "pe", marks[f"{key}m{bidx - 2}"]["pe"])
                                    wait(a, sqp, marks[f"{key}w{bidx - 2}"][sqp])
                                    wait(a, "dv", marks[f"{key}q{bidx - 2}"]["dv"])
                                bump(a.activation(
                                    out=W3Rq[p][:, :], in_=W3I[sin],
                                    func=AF.Sign, bias=thrv[:, 6:7]), "ac")
                                wait(a, "ac")
                                bump(a.activation(
                                    out=SQB3[p][:, :], in_=W3I[sin],
                                    func=AF.Sign, bias=thrv[:, 2:3]), "ac")
                            AA(f)
                            mark(f"{key}s2_{bidx}")

                            def f(v, bidx=bidx, p=p):
                                wait(v, "ac", marks[f"{key}s2_{bidx}"]["ac"])
                                bump(v.tensor_tensor(
                                    out=W3Rq[p][:, :], in0=W3Rq[p][:, :],
                                    in1=SQB3[p][:, :], op=ALU.add), "dv")
                            VV(f)
                            mark(f"{key}q{bidx}")

                            def f(s, o=o, qq=qq, bidx=bidx, p=p):
                                sqp = "sQe" if bidx % 2 == 0 else "sQo"
                                wait(s, "dv", marks[f"{key}q{bidx}"]["dv"])
                                bump(s.dma_start(
                                    out=w3q[o][:, qq * 1408:(qq + 1) * 1408],
                                    in_=W3Rq[p][:, :]), sqp, 16)
                            SS(f)
                            mark(f"{key}w{bidx}")

                            def f(t, o=o, qq=qq, bidx=bidx, p=p):
                                wait(t, "dv", marks[f"{key}q{bidx}"]["dv"])
                                if bidx == 0:
                                    wait(t, "dv",
                                         marks[f"F{call}s2h{h2}h{HJ - 1}"]["dv"])
                                    if guard["psO"] is not None:
                                        for nm, v in guard["psO"].items():
                                            wait(t, nm, v)
                                if bidx >= 8 and qq == 0:
                                    wait(t, "dv", marks[f"{key}e{o - 2}"]["dv"])
                                for j in range(QB):
                                    jj = qq * QB + j
                                    mm = t.matmul(
                                        psO[o % 2][:, :],
                                        W3Rq[p][:, j * 128:(j + 1) * 128],
                                        GH[:, jj * TH:(jj + 1) * TH],
                                        start=(jj == 0), stop=(jj == HJ - 1))
                                bump(mm, "pe")
                            TT(f)
                            mark(f"{key}m{bidx}")

                        def f(v, o=o):
                            wait(v, "pe", marks[f"{key}m{o * 4 + 3}"]["pe"])
                            sl = xf[:, o * T + off: o * T + off + TH]
                            bump(v.scalar_tensor_tensor(
                                out=sl, in0=psO[o % 2][:, :], scalar=0.125,
                                in1=sl, op0=ALU.mult, op1=ALU.add), "dv")
                        VV(f)
                        mark(f"{key}e{o}")
                    mark(f"{key}end")
                    guard["psO"] = {"dv": cnt["dv"]}
                    guard["W3R"] = cnt["pe"]
                    guard["GH"] = cnt["pe"]
                    return
                HB = HJ // 2
                for o in range(DC):
                    for hb in range(2):
                        bidx = o * 2 + hb
                        p = bidx % 2

                        def f(s, o=o, hb=hb, bidx=bidx, p=p):
                            if bidx < 2:
                                if call == 0 and h2 == 1:
                                    wait(s, "sQe", marks["F0B0end"]["sQe"])
                                    wait(s, "sQo", marks["F0B0end"]["sQo"])
                                if guard["W3R"] is not None:
                                    wait(s, "pe", guard["W3R"])
                            if bidx >= 2:
                                wait(s, "pe", marks[f"{key}m{bidx - 2}"]["pe"])
                            bump(s.dma_start(
                                out=W3R[p],
                                in_=w3q[o][:, hb * 2816:(hb + 1) * 2816]),
                                "lBe" if bidx % 2 == 0 else "lBo", 16)
                        SS(f)
                        mark(f"{key}l{bidx}")

                        def f(t, o=o, hb=hb, bidx=bidx, p=p):
                            lb = "lBe" if bidx % 2 == 0 else "lBo"
                            wait(t, lb, marks[f"{key}l{bidx}"][lb])
                            if bidx == 0:
                                wait(t, "dv", marks[f"F{call}s2h{h2}h{HJ - 1}"]["dv"])
                                if guard["psO"] is not None:
                                    for nm, v in guard["psO"].items():
                                        wait(t, nm, v)
                            if bidx >= 4 and hb == 0:
                                wait(t, "dv", marks[f"{key}e{o - 2}"]["dv"])
                            for j in range(HB):
                                jj = hb * HB + j
                                mm = t.matmul(psO[o % 2][:, :],
                                              W3R[p][:, j * 128:(j + 1) * 128],
                                              GH[:, jj * TH:(jj + 1) * TH],
                                              start=(jj == 0), stop=(jj == HJ - 1))
                            bump(mm, "pe")
                        TT(f)
                        mark(f"{key}m{bidx}")

                    def f(v, o=o):
                        wait(v, "pe", marks[f"{key}m{o * 2 + 1}"]["pe"])
                        sl = xf[:, o * T + off: o * T + off + TH]
                        bump(v.scalar_tensor_tensor(
                            out=sl, in0=psO[o % 2][:, :], scalar=0.125, in1=sl,
                            op0=ALU.mult, op1=ALU.add), "dv")
                    VV(f)
                    mark(f"{key}e{o}")
                mark(f"{key}end")
                guard["psO"] = {"dv": cnt["dv"]}
                guard["W3R"] = cnt["pe"]
                guard["GH"] = cnt["pe"]

            if call == 0:
                sweep(1, w1t, w1q, 0)
                sig_batch(0)
                sweep(2, w2t, w2q, 0)
                phase_b(0)
                sweep(1, None, w1q, 1)
                sig_batch(1)
                sweep(2, None, w2q, 1)
                phase_b(1)
            else:
                for h2 in range(2):
                    sweep(1, None, w1q, h2)
                    sig_batch(h2)
                    sweep(2, None, w2q, h2)
                    phase_b(h2)
        mark(f"ffn{call}done")

    def emit_attention():
        prev = dict(cnt)
        with nc.psum_tensor("psP0", [128, TH], F32) as psp0, \
             nc.psum_tensor("psP1", [128, TH], F32) as psp1, \
             nc.psum_tensor("psM0", [64, TH], F32) as psm0, \
             nc.psum_tensor("psM1", [64, TH], F32) as psm1, \
             nc.psum_tensor("psTq", [128, 512], BF16) as psTq, \
             nc.psum_tensor("psKV0", [128, 129], F32) as pskv0, \
             nc.psum_tensor("psKV1", [128, 129], F32) as pskv1:
            psP = [psp0, psp1]
            psM = [psm0, psm1]
            psK = psP
            psKV = [pskv0, pskv1]

            for ci in range(32):
                dc, hhalf = divmod(ci, 2)
                p = ci % 2

                svp = "sVe" if ci % 2 == 0 else "sVo"

                def f(a, ci=ci, dc=dc, hhalf=hhalf, p=p, svp=svp):
                    if ci == 0:
                        wait(a, "dv", marks["L2done"]["dv"])
                    if ci >= 2:
                        wait(a, "ac", marks[f"wvc{ci - 2}"]["ac"])
                    bump(a.dma_start(out=WVF[p],
                                     in_=wvt[dc][:, hhalf * 1024:(hhalf + 1) * 1024]),
                         "lC", 16)
                    wait(a, "lC")
                    if ci >= 2:
                        wait(a, svp, marks[f"wvs{ci - 2}"][svp])
                    bump(a.copy(out=WVO[p], in_=WVF[p]), "ac")
                AA(f)
                mark(f"wvc{ci}")

                def f(a, ci=ci, dc=dc, hhalf=hhalf, p=p, svp=svp):
                    wait(a, "ac", marks[f"wvc{ci}"]["ac"])
                    bump(a.dma_start(
                        out=wvb_d[dc][:, hhalf * 1024:(hhalf + 1) * 1024],
                        in_=WVO[p]), svp, 16)
                AA(f)
                mark(f"wvs{ci}")

            def proj_head(h, wsrc, bias_s, tmp, pref, is_q, tmp_guard=None):
                p = h % 2

                def f(s, h=h, p=p, wg=dict(WG)):
                    if h < 2:
                        wguard(s, wg)
                    if h >= 2:
                        wait(s, "dv", marks[f"{pref}cast{h - 2}"]["dv"])
                    bump(s.dma_start(out=wfa[p][:, :], in_=wsrc[h]),
                         "lPe" if h % 2 == 0 else "lPo", 16)
                SS(f)
                mark(f"{pref}ld{h}")

                def f(v, h=h, p=p):
                    lp = "lPe" if h % 2 == 0 else "lPo"
                    wait(v, lp, marks[f"{pref}ld{h}"][lp])
                    if h >= 2:
                        wait(v, "pe", marks[f"{pref}mm{h - 2}"]["pe"])
                    bump(v.tensor_copy(out=wqa[p][:, :], in_=wfa[p][:, :]), "dv")
                VV(f)
                mark(f"{pref}cast{h}")

                def f(t, h=h, p=p):
                    wait(t, "dv", marks[f"{pref}cast{h}"]["dv"])
                    if h == 0:
                        wait(t, "dv", marks["L2done"]["dv"])
                        for nm, v in prev.items():
                            wait(t, nm, v)
                    if h >= 1:
                        wait(t, "dv", marks[f"{pref}ev{h - 1}"]["dv"])
                    for n2 in range(2):
                        for dc in range(DC - 1):
                            t.matmul(psP[n2][:, :],
                                     wqa[p][:, dc * 128:(dc + 1) * 128],
                                     ybuf[:, dc * T + n2 * TH: dc * T + n2 * TH + TH],
                                     start=(dc == 0), stop=False)
                        mm = t.matmul(psP[n2][:, :],
                                      wqa[p][:, (DC - 1) * 128:DC * 128],
                                      ybuf[:, (DC - 1) * T + n2 * TH: (DC - 1) * T + n2 * TH + TH],
                                      start=False, stop=True)
                    bump(mm, "pe")
                TT(f)
                mark(f"{pref}mm{h}")

                def f(v, h=h, p=p):
                    wait(v, "pe", marks[f"{pref}mm{h}"]["pe"])
                    if h >= 2:
                        wait(v, "pe", marks[f"{pref}mp{h - 2}"]["pe"])
                        if tmp_guard is not None:
                            wait(v, "pe", marks[f"{tmp_guard}{h - 2}"]["pe"])
                    for n2 in range(2):
                        bump(v.tensor_scalar(out=tmp[p][:, n2 * TH:(n2 + 1) * TH],
                                             in0=psP[n2][:, :],
                                             scalar1=bias_s[:, h:h + 1],
                                             scalar2=None, op0=ALU.add), "dv")
                VV(f)
                mark(f"{pref}ev{h}")

                wmap = wmapq_b if is_q else wmapk_b

                def f(t, h=h, p=p):
                    wait(t, "dv", marks[f"{pref}ev{h}"]["dv"])
                    if h >= 1:
                        wait(t, "ac", marks[f"{pref}el{h - 1}"]["ac"])
                    for n2 in range(2):
                        mm = t.matmul(psM[n2][:, :], wmap[:, :],
                                      tmp[p][:, n2 * TH:(n2 + 1) * TH],
                                      start=True, stop=True)
                    bump(mm, "pe")
                TT(f)
                mark(f"{pref}mp{h}")

                def f(a, h=h, p=p):
                    wait(a, "pe", marks[f"{pref}mp{h}"]["pe"])
                    if h >= 1:
                        wait(a, "dv", marks[f"{pref}qm{h - 1}"]["dv"])
                    for n2 in range(2):
                        bump(a.activation(out=ELU[0][:, n2 * TH:(n2 + 1) * TH],
                                          in_=psM[n2][:, :], func=AF.Relu), "ac")
                        bump(a.activation(out=ELU[1][:, n2 * TH:(n2 + 1) * TH],
                                          in_=psM[n2][:, :], func=AF.Relu,
                                          scale=negone[0:64, 0:1]), "ac")
                    wait(a, "ac")
                    bump(a.activation(out=ELU[1][:, :], in_=ELU[1][:, :],
                                      func=AF.Exp, scale=negone[0:64, 0:1]), "ac")
                AA(f)
                mark(f"{pref}el{h}")

            wg_update()
            for h in range(NH):
                proj_head(h, wqt, bq_s, QTMP, "q", True)

                def f(v, h=h):
                    wait(v, "ac", marks[f"qel{h}"]["ac"])
                    bump(v.tensor_tensor(
                        out=qmb[(h % 2) * 64:(h % 2) * 64 + 64,
                                (h // 2) * T:(h // 2) * T + T],
                        in0=ELU[0][:, :], in1=ELU[1][:, :], op=ALU.add), "dv")
                VV(f)
                mark(f"qqm{h}")

            wg_update()
            for h in range(NH):
                proj_head(h, wkt, bk_s, KTMP, "k", False, tmp_guard="ktp")

                def f(v, h=h):
                    wait(v, "ac", marks[f"kel{h}"]["ac"])
                    if h >= 1:
                        wait(v, "pe", marks[f"ktp{h - 1}"]["pe"])
                    bump(v.tensor_tensor(
                        out=KTMP[h % 2][0:64, :],
                        in0=ELU[0][:, :], in1=ELU[1][:, :], op=ALU.add), "dv")
                VV(f)
                mark(f"kqm{h}")

                def f(t, h=h):
                    wait(t, "dv", marks[f"kqm{h}"]["dv"])
                    if h >= 1:
                        wait(t, "dv", marks[f"kmv{h - 1}"]["dv"])
                    for mb in range(7):
                        t.transpose(psTq[:, mb * 64:(mb + 1) * 64],
                                    KTMP[h % 2][0:64, mb * 128:(mb + 1) * 128],
                                    identb[0:64, 0:64])
                    bump(t.transpose(psTq[:, 7 * 64:8 * 64],
                                     KTMP[h % 2][0:64, 7 * 128:8 * 128],
                                     identb[0:64, 0:64]), "pe")
                TT(f)
                mark(f"ktp{h}")

                def f(v, h=h):
                    wait(v, "pe", marks[f"ktp{h}"]["pe"])
                    bump(v.tensor_copy(out=KMB[:, h * 512:(h + 1) * 512],
                                       in_=psTq[:, :]), "dv")
                VV(f)
                mark(f"kmv{h}")

            def f(g):
                wait(g, "pe", marks["L2done"]["pe"])
                wait(g, "dv", marks["L2done"]["dv"])
                vv = VBUF.rearrange("p (i h e) -> p i h e", i=2, e=129)
                bump(g.memset(vv[:, :, :, 128:129], 1.0), "pl")
            GG(f)
            mark("vones")

            NG = 4
            for g2 in range(NG):
                for osl in range(8):
                    sidx = g2 * 8 + osl

                    def f(s, g2=g2, osl=osl, sidx=sidx):
                        if sidx == 0:
                            wait(s, "sVe", marks["wvs31"]["sVe"])
                            wait(s, "sVo", marks["wvs31"]["sVo"])
                        if sidx >= 1:
                            wait(s, "pe", marks[f"vmm{sidx - 1}_1"]["pe"])
                        bump(s.dma_start(
                            out=WVS,
                            in_=wvb_d.rearrange("d p c -> p d c")
                            [:, :, osl * 256:(osl + 1) * 256]), "lV", 16)
                    SS(f)
                    mark(f"vld{sidx}")

                    for i in range(2):
                        mchunk = g2 * 2 + i
                        b = i % 2

                        def f(t, g2=g2, osl=osl, sidx=sidx, i=i, mchunk=mchunk, b=b):
                            if sidx == 0 and i == 0:
                                wait(t, "dv", marks[f"kev{NH - 1}"]["dv"])
                                wait(t, "dv", marks[f"kmv{NH - 1}"]["dv"])
                            if osl == 0 and i == 0 and g2 > 0:
                                wait(t, "pe", marks[f"kvmm{g2 - 1}"]["pe"])
                            if i == 0:
                                wait(t, "lV", marks[f"vld{sidx}"]["lV"])
                            if i >= 2 or sidx > 0:
                                pk = (f"vev{sidx}_{i - 2}" if i >= 2
                                      else f"vev{sidx - 1}_{i}")
                                wait(t, "dv", marks[pk]["dv"])
                            for dc in range(DC - 1):
                                t.matmul(psP[b][:, 0:256],
                                         ybuf[:, dc * T + mchunk * 128: dc * T + (mchunk + 1) * 128],
                                         WVS[:, dc * 256:(dc + 1) * 256],
                                         start=(dc == 0), stop=False)
                            t.matmul(psP[b][:, 0:256],
                                     ybuf[:, (DC - 1) * T + mchunk * 128: (DC - 1) * T + (mchunk + 1) * 128],
                                     WVS[:, (DC - 1) * 256:DC * 256],
                                     start=False, stop=False)
                            bump(t.matmul(psP[b][:, 0:256], onesb_r[:],
                                          bv_b[0:1, osl * 256:(osl + 1) * 256],
                                          start=False, stop=True), "pe")
                        TT(f)
                        mark(f"vmm{sidx}_{i}")

                        def f(v, g2=g2, osl=osl, sidx=sidx, i=i, b=b):
                            wait(v, "pe", marks[f"vmm{sidx}_{i}"]["pe"])
                            if sidx == 0 and i == 0:
                                wait(v, "pl", marks["vones"]["pl"])
                            for hh in range(2):
                                h = osl * 2 + hh
                                bump(v.tensor_copy(
                                    out=VBUF[:, i * (NH * 129) + h * 129:
                                             i * (NH * 129) + h * 129 + 128],
                                    in_=psP[b][:, hh * 128:(hh + 1) * 128]), "dv")
                        VV(f)
                        mark(f"vev{sidx}_{i}")

                for h in range(NH):
                    q3 = h // 6
                    j3 = (h % 6) // 2
                    i3 = h % 2
                    kidx = g2 * NH + h

                    def f(t, g2=g2, h=h, i3=i3, kidx=kidx):
                        if h == 0:
                            wait(t, "dv", marks[f"vev{g2 * 8 + 7}_1"]["dv"])
                        if kidx >= 2:
                            wait(t, "dv", marks[f"kvadd{kidx - 2}"]["dv"])
                        for i in range(2):
                            mchunk = g2 * 2 + i
                            mm = t.matmul(
                                psKV[h % 2][i3 * 64:(i3 + 1) * 64, :],
                                KMB[:, h * 512 + mchunk * 64: h * 512 + (mchunk + 1) * 64],
                                VBUF[:, i * (NH * 129) + h * 129:
                                     i * (NH * 129) + (h + 1) * 129],
                                start=(i == 0), stop=(i == 1))
                        bump(mm, "pe")
                    TT(f)
                    mark(f"kvmm{kidx}")

                    def f(v, g2=g2, h=h, q3=q3, j3=j3, i3=i3, kidx=kidx):
                        wait(v, "pe", marks[f"kvmm{kidx}"]["pe"])
                        dst = kvga[i3 * 64:(i3 + 1) * 64,
                                   q3 * 512 + j3 * 129: q3 * 512 + (j3 + 1) * 129]
                        ps = psKV[h % 2][i3 * 64:(i3 + 1) * 64, :]
                        if g2 == 0:
                            bump(v.tensor_copy(out=dst, in_=ps), "dv")
                        else:
                            bump(v.tensor_tensor(out=dst, in0=dst, in1=ps,
                                                 op=ALU.add), "dv")
                    VV(f)
                    mark(f"kvadd{kidx}")

            mark("kvev")

            def f(g):
                wait(g, "dv", marks["kvev"]["dv"])
                bump(g.dma_start(out=cc_kin.ap(), in_=kvga[:]), "g", 16)
                wait(g, "g")
                bump(g.collective_compute(
                    "AllReduce", ALU.add, replica_groups=[list(range(NCORES))],
                    ins=[cc_kin.ap().opt()], outs=[cc_kout.ap().opt()]), "cc", 1)
                wait(g, "cc")
                bump(g.dma_start(out=kvga[:], in_=cc_kout.ap()), "g", 16)
            GG(f)
            mark("kvcc")

            def f(v):
                wait(v, "g", marks["kvcc"]["g"])
                bump(v.tensor_copy(out=kvgb[:], in_=kvga[:]), "dv")
            VV(f)
            mark("kvcast")

            def kvoff(h):
                q3 = h // 6
                j3 = (h % 6) // 2
                i3 = h % 2
                return q3, j3, i3

            for h in range(NH):
                q3, j3, i3 = kvoff(h)

                def f(t, h=h, q3=q3, j3=j3, i3=i3):
                    if h == 0:
                        wait(t, "dv", marks["kvcast"]["dv"])
                    if h >= 1:
                        wait(t, "dv", marks[f"nev{h - 1}"]["dv"])
                    ks = kvgb[i3 * 64:(i3 + 1) * 64,
                              q3 * 512 + j3 * 129 + 128: q3 * 512 + j3 * 129 + 129]
                    for n2 in range(2):
                        mm = t.matmul(psM[n2][0:1, :], ks,
                                      qmb[i3 * 64:(i3 + 1) * 64,
                                          (h // 2) * T + n2 * TH: (h // 2) * T + (n2 + 1) * TH],
                                      start=True, stop=True)
                    bump(mm, "pe")
                TT(f)
                mark(f"nmm{h}")

                def f(v, h=h):
                    wait(v, "pe", marks[f"nmm{h}"]["pe"])
                    if h >= 2:
                        wait(v, "lN", marks[f"nsth{h - 1}"]["lN"])
                    ro = (h % 2) * T
                    bump(v.tensor_copy(out=rows[0:1, ro:ro + TH],
                                       in_=psM[0][0:1, :]), "dv")
                    bump(v.tensor_copy(out=rows[0:1, ro + TH:ro + T],
                                       in_=psM[1][0:1, :]), "dv")
                VV(f)
                mark(f"nev{h}")

                def f(s, h=h):
                    wait(s, "dv", marks[f"nev{h}"]["dv"])
                    ro = (h % 2) * T
                    bump(s.dma_start(out=nrm_d.ap()[h:h + 1, :],
                                     in_=rows[0:1, ro:ro + T]), "lN", 16)
                SS(f)
                mark(f"nsth{h}")

            NB = rows[:, 1024:1152]
            nflat0 = nrm_d.ap().rearrange("a b -> (a b)")

            def f(s):
                wait(s, "lN", marks[f"nsth{NH - 1}"]["lN"])
                bump(s.dma_start(out=NB, in_=nflat0[:]), "lN", 16)
            SS(f)
            mark("nbl")

            def f(v):
                wait(v, "lN", marks["nbl"]["lN"])
                bump(v.tensor_scalar(out=NB, in0=NB, scalar1=1e-6, scalar2=None,
                                     op0=ALU.add), "dv")
                wait(v, "dv")
                bump(v.reciprocal(out=NB, in_=NB), "dv")
            VV(f)
            mark("rnorm")

            def f(s):
                wait(s, "dv", marks["rnorm"]["dv"])
                bump(s.dma_start(out=nflat0[:], in_=NB), "lN", 16)
            SS(f)
            mark("nst")

            nflat = nrm_d.ap().rearrange("a b -> (a b)")

            for h in range(NH):
                q3, j3, i3 = kvoff(h)
                if h % 2 == 0:
                    def f(s, h=h):
                        if h == 0:
                            wait(s, "lN", marks["nst"]["lN"])
                        else:
                            wait(s, "pe", marks[f"rbc{h - 1}"]["pe"])
                        bump(s.dma_start(out=rows[0:1, 0:2048],
                                         in_=nflat[h * T:(h + 2) * T]), "lN", 16)
                    SS(f)
                    mark(f"nld{h}")

                def f(t, h=h):
                    wait(t, "lN", marks[f"nld{h - h % 2}"]["lN"])
                    if h == 0:
                        wait(t, "dv", marks["rnorm"]["dv"])
                    if h >= 1:
                        wait(t, "dv", marks[f"rnb{h - 1}"]["dv"])
                    for n2 in range(2):
                        mm = t.matmul(psM[n2][:, :],
                                      onesf_r[0:1, 0:64],
                                      rows[0:1, (h % 2) * T + n2 * TH:
                                           (h % 2) * T + (n2 + 1) * TH],
                                      start=True, stop=True)
                    bump(mm, "pe")
                TT(f)
                mark(f"rbc{h}")

                def f(v, h=h, i3=i3):
                    wait(v, "pe", marks[f"rbc{h}"]["pe"])
                    if h >= 1:
                        wait(v, "dv", marks[f"qn{h - 1}"]["dv"])
                    bump(v.tensor_copy(out=RNB2[i3][:, 0:TH], in_=psM[0][:, :]), "dv")
                    bump(v.tensor_copy(out=RNB2[i3][:, TH:T], in_=psM[1][:, :]), "dv")
                VV(f)
                mark(f"rnb{h}")

                def f(v, h=h, i3=i3):
                    wait(v, "dv", marks[f"rnb{h}"]["dv"])
                    sl = qmb[i3 * 64:(i3 + 1) * 64, (h // 2) * T:(h // 2) * T + T]
                    bump(v.tensor_tensor(out=sl, in0=sl, in1=RNB2[i3][:, :],
                                         op=ALU.mult), "dv")
                VV(f)
                mark(f"qn{h}")

                def f(t, h=h, q3=q3, j3=j3, i3=i3):
                    wait(t, "dv", marks[f"qn{h}"]["dv"])
                    if h >= 1:
                        wait(t, "dv", marks[f"aev{h - 1}"]["dv"])
                    kv = kvgb[i3 * 64:(i3 + 1) * 64,
                              q3 * 512 + j3 * 129: q3 * 512 + j3 * 129 + 128]
                    for n2 in range(2):
                        mm = t.matmul(psP[n2][:, :], kv,
                                      qmb[i3 * 64:(i3 + 1) * 64,
                                          (h // 2) * T + n2 * TH:(h // 2) * T + (n2 + 1) * TH],
                                      start=True, stop=True)
                    bump(mm, "pe")
                TT(f)
                mark(f"amm{h}")

                def f(v, h=h):
                    wait(v, "pe", marks[f"amm{h}"]["pe"])
                    for n2 in range(2):
                        bump(v.tensor_copy(
                            out=ybuf[:, h * T + n2 * TH: h * T + (n2 + 1) * TH],
                            in_=psP[n2][:, :]), "dv")
                VV(f)
                mark(f"aev{h}")

            wg_update()
            for o in range(DC):
                p = o % 2

                def f(s, o=o, p=p, wg=dict(WG)):
                    if o < 2:
                        wguard(s, wg)
                    if o >= 2:
                        wait(s, "dv", marks[f"ocast{o - 2}"]["dv"])
                    bump(s.dma_start(out=wfa[p][:, :], in_=wot[o]),
                         "lPe" if o % 2 == 0 else "lPo", 16)
                SS(f)
                mark(f"old{o}")

                def f(v, o=o, p=p):
                    lp = "lPe" if o % 2 == 0 else "lPo"
                    wait(v, lp, marks[f"old{o}"][lp])
                    if o >= 2:
                        wait(v, "pe", marks[f"omm{o - 2}"]["pe"])
                    bump(v.tensor_copy(out=wqa[p][:, :], in_=wfa[p][:, :]), "dv")
                VV(f)
                mark(f"ocast{o}")

                def f(t, o=o, p=p):
                    wait(t, "dv", marks[f"ocast{o}"]["dv"])
                    if o == 0:
                        wait(t, "dv", marks["kvev"]["dv"])
                    if o >= 1:
                        wait(t, "dv", marks[f"oep{o - 1}"]["dv"])
                    for n2 in range(2):
                        for dc in range(DC - 1):
                            t.matmul(psK[n2][:, :],
                                     wqa[p][:, dc * 128:(dc + 1) * 128],
                                     ybuf[:, dc * T + n2 * TH: dc * T + n2 * TH + TH],
                                     start=(dc == 0), stop=False)
                        mm = t.matmul(psK[n2][:, :],
                                      wqa[p][:, (DC - 1) * 128:DC * 128],
                                      ybuf[:, (DC - 1) * T + n2 * TH: (DC - 1) * T + n2 * TH + TH],
                                      start=False, stop=True)
                    bump(mm, "pe")
                TT(f)
                mark(f"omm{o}")

                def f(v, o=o, p=p):
                    wait(v, "pe", marks[f"omm{o}"]["pe"])
                    for n2 in range(2):
                        sl = xf[:, o * T + n2 * TH: o * T + (n2 + 1) * TH]
                        bump(v.scalar_tensor_tensor(
                            out=sl, in0=psK[n2][:, :], scalar=bo_s[:, o:o + 1],
                            in1=sl, op0=ALU.add, op1=ALU.add), "dv")
                VV(f)
                mark(f"oep{o}")
        wg_update()
        mark("attdone")

    def emit_output(src):
        with nc.psum_tensor("psT2a", [128, TH], F32) as pst0, \
             nc.psum_tensor("psT2b", [128, TH], F32) as pst1:
            psT2 = [pst0, pst1]
            for m in range(8):
                for qq in range(4):
                    b = qq % 2

                    def f(t, m=m, qq=qq, b=b):
                        if m == 0 and qq == 0:
                            for nm, v in marks[src].items():
                                wait(t, nm, v)
                        if m > 0 or qq >= 2:
                            pk = (f"oev{m}_{qq - 2}" if qq >= 2
                                  else f"oev{m - 1}_{qq + 2}")
                            wait(t, "dv", marks[pk]["dv"])
                        for i in range(4):
                            dc = qq * 4 + i
                            mm = t.transpose(psT2[b][:, i * 128:(i + 1) * 128],
                                             xf[:, dc * T + m * 128: dc * T + (m + 1) * 128],
                                             identf[:])
                        bump(mm, "pe")
                    TT(f)
                    mark(f"otp{m}_{qq}")

                    def f(v, m=m, qq=qq, b=b):
                        wait(v, "pe", marks[f"otp{m}_{qq}"]["pe"])
                        if m > 0:
                            wait(v, "sO", marks[f"ost{m - 1}"]["sO"])
                        bump(v.tensor_copy(out=XOUT[:, qq * 512:(qq + 1) * 512],
                                           in_=psT2[b][:, :]), "dv")
                    VV(f)
                    mark(f"oev{m}_{qq}")

                def f(s, m=m):
                    wait(s, "dv", marks[f"oev{m}_3"]["dv"])
                    bump(s.dma_start(out=out[m * 128:(m + 1) * 128, :], in_=XOUT),
                         "sO", 16)
                SS(f)
                mark(f"ost{m}")

            def f(s):
                wait(s, "sO", cnt["sO"])
            SS(f)

    emit_ln("L1", marks["xf_done"]["dv"])
    emit_ffn(0, "L1")
    if stage >= 2:
        emit_ln("L2", marks["ffn0done"]["dv"])
        emit_attention()
    if stage >= 3:
        emit_ln("L3", marks["attdone"]["dv"])
        emit_ffn(1, "L3")
        emit_output("ffn1done")
    elif stage == 2:
        emit_output("attdone")
    else:
        emit_output("ffn0done")

    st.close()
    return nc


def host_prep(inp):
    c = np.ascontiguousarray
    w1 = np.asarray(inp["L0_ffn_w1"]); w2 = np.asarray(inp["L0_ffn_w2"])
    w3 = np.asarray(inp["L0_ffn_w3"])
    w1t = c(w1.T).reshape(16, 128, 44, 128).transpose(2, 1, 0, 3)
    w2t = c(w2.T).reshape(16, 128, 44, 128).transpose(2, 1, 0, 3)
    w3t = c(w3.T).reshape(44, 128, 16, 128).transpose(2, 1, 0, 3)
    w1t = c(w1t.reshape(44, 128, 2048))
    w2t = c(w2t.reshape(44, 128, 2048))
    w3t = c(w3t.reshape(16, 128, 5632))
    wq = np.asarray(inp["L0_attn_wq"]); wk = np.asarray(inp["L0_attn_wk"])
    wv = np.asarray(inp["L0_attn_wv"]); wo = np.asarray(inp["L0_attn_wo"])
    mk = lambda w: c(c(w.T).reshape(16, 128, 16, 128).transpose(2, 1, 0, 3)
                     .reshape(16, 128, 2048))
    base = {
        "w1t": w1t, "w2t": w2t, "w3t": w3t,
        "wqt": mk(wq), "wkt": mk(wk), "wot": mk(wo),
        "wvt": c(wv.T.reshape(16, 128, 2048)),
        "wmapqt": c(np.asarray(inp["L0_attn_wmap_q"]).T),
        "wmapkt": c(np.asarray(inp["L0_attn_wmap_k"]).T),
        "bqt": c(np.asarray(inp["L0_attn_bq"]).reshape(16, 128).T),
        "bkt": c(np.asarray(inp["L0_attn_bk"]).reshape(16, 128).T),
        "bot": c(np.asarray(inp["L0_attn_bo"]).reshape(16, 128).T),
        "bvr": c(np.asarray(inp["L0_attn_bv"]).reshape(1, 2048)),
    }
    g1 = w1t.reshape(NCORES, 128, GSL)
    g2 = w2t.reshape(NCORES, 128, GSL)
    g3 = w3t.reshape(NCORES, 128, GSL)
    x = np.asarray(inp["L0_attn_hidden_states"]).reshape(8192, 2048)
    maps = []
    for ci in range(NCORES):
        m = dict(base)
        m["xin"] = c(x[ci * T:(ci + 1) * T, :])
        m["gsl"] = c(np.stack([g1[ci], g2[ci], g3[ci]]))
        maps.append(m)
    return maps


def kernel(**inputs):
    maps = host_prep(inputs)
    nc = build(STAGE)
    res = run_bass_kernel_spmd(nc, maps, core_ids=list(range(NCORES)))
    outs = [res.results[i]["out"] for i in range(NCORES)]
    return np.concatenate(outs, 0).reshape(1, 8192, 2048)


if __name__ == "__main__":
    import sys
    stg = int(sys.argv[1]) if len(sys.argv) > 1 else STAGE
    nc = build(stg)
    n = sum(1 for _ in nc.all_instructions())
    print(f"build ok (stage {stg}), instructions: {n}")


# version 15
# speedup vs baseline: 1.1935x; 1.1846x over previous
import contextlib
import numpy as np

import concourse.bass as bass
import concourse.mybir as mybir
from concourse.bass_utils import run_bass_kernel_spmd

F32 = mybir.dt.float32
BF16 = mybir.dt.bfloat16
AF = mybir.ActivationFunctionType
ALU = mybir.AluOpType

NCORES = 8
T = 1024
TH = 512
D = 2048
DC = 16
HID = 5632
HJ = 44
NH = 16
MM = 64
NW = float(HID * D)
GSL = HJ * 128 * 2048 // NCORES // 128

STAGE = 3


def build(stage=STAGE):
    nc = bass.Bass("TRN2", target_bir_lowering=False,
                   dynamic_dma_scratch_size=2048)

    dt_in = lambda n, s: nc.dram_tensor(n, s, F32, kind="ExternalInput").ap()
    xin = dt_in("xin", [T, D])
    w1t = dt_in("w1t", [HJ, 128, 2048])
    w2t = dt_in("w2t", [HJ, 128, 2048])
    w3t = dt_in("w3t", [DC, 128, HID])
    gsl = dt_in("gsl", [3, 128, GSL])
    wqt = dt_in("wqt", [NH, 128, 2048])
    wkt = dt_in("wkt", [NH, 128, 2048])
    wvt = dt_in("wvt", [DC, 128, 2048])
    wot = dt_in("wot", [DC, 128, 2048])
    wmapqt = dt_in("wmapqt", [128, MM])
    wmapkt = dt_in("wmapkt", [128, MM])
    bqt = dt_in("bqt", [128, NH])
    bkt = dt_in("bkt", [128, NH])
    bvr = dt_in("bvr", [1, D])
    bot = dt_in("bot", [128, NH])
    out = nc.dram_tensor("out", [T, D], F32, kind="ExternalOutput").ap()

    w1q = nc.dram_tensor("w1q", [HJ, 128, 2048], BF16).ap()
    w2q = nc.dram_tensor("w2q", [HJ, 128, 2048], BF16).ap()
    w3q = nc.dram_tensor("w3q", [DC, 128, HID], BF16).ap()
    wvb_d = nc.dram_tensor("wvb_d", [DC, 128, 2048], BF16).ap()
    cc_gin = nc.dram_tensor("cc_gin", [1, 4], F32)
    cc_gout = nc.dram_tensor("cc_gout", [1, 4], F32)
    cc_kin = nc.dram_tensor("cc_kin", [128, 1536], F32)
    cc_kout = nc.dram_tensor("cc_kout", [128, 1536], F32)
    nrm_d = nc.dram_tensor("nrm_d", [NH, T], F32)

    st = contextlib.ExitStack()
    sb = lambda name, shape, dt: st.enter_context(nc.sbuf_tensor(name, shape, dt))

    xf = sb("xf", [128, DC * T], F32)
    ybuf = sb("ybuf", [128, DC * T], BF16)
    wfa = [sb("wfa0", [128, 2048], F32), sb("wfa1", [128, 2048], F32)]
    wqa = [sb("wqa0", [128, 2048], BF16), sb("wqa1", [128, 2048], BF16)]
    AR = 18960
    arena = sb("arena", [128, AR], F32)
    x2t = [sb("x2t0", [128, TH], BF16), sb("x2t1", [128, TH], BF16)]
    sqt = sb("sqt", [128, TH], BF16)
    xbt = [sb("xbt0", [128, TH], BF16), sb("xbt1", [128, TH], BF16)]
    nrm = sb("nrm", [NH, T], F32)
    onesb_c = sb("onesb_c", [128, 1], BF16)
    onesb_r = sb("onesb_r", [1, 128], BF16)
    onesf_r = sb("onesf_r", [1, 128], F32)
    onesf_c = sb("onesf_c", [128, 1], F32)
    thrv = sb("thrv", [128, 8], F32)
    gpart = sb("gpart", [128, 2], F32)
    gprow = sb("gprow", [1, 4], F32)
    identf = sb("identf", [128, 128], F32)
    identb = sb("identb", [128, 128], BF16)
    wmapq_b = sb("wmapq_b", [128, MM], BF16)
    wmapk_b = sb("wmapk_b", [128, MM], BF16)
    wmap_f = sb("wmap_f", [128, MM], F32)
    bq_s = sb("bq_s", [128, NH], F32)
    bk_s = sb("bk_s", [128, NH], F32)
    bo_s = sb("bo_s", [128, NH], F32)
    negone = sb("negone", [128, 1], F32)
    halfc = sb("halfc", [128, 1], F32)
    qmb = sb("qmb", [128, (NH // 2) * T], BF16)
    bv_f = None
    bv_b = None

    def cf(c0, n):
        assert c0 + n <= AR
        return arena[:, c0:c0 + n]

    def cb(c0, nb):
        assert c0 + nb // 2 <= AR
        return arena[:, c0:c0 + nb // 2].bitcast(BF16)

    GH = cb(0, HJ * TH)
    SQW = [cb(11264, 2048), cb(12288, 2048)]
    G2B = [cf(13376, 2048), cf(15424, 2048)]
    W3R = [cb(13376, 2816), cb(14784, 2816)]
    rows = cf(16192, 2048)
    W3I = [cf(11264, 1408), cf(16192, 1408), qmb[:, 0:2816].bitcast(F32)]
    W3Rq = [W3R[0][:, 0:1408], W3R[1][:, 0:1408]]
    SQB3 = [W3R[0][:, 1408:2816], W3R[1][:, 1408:2816]]
    XB = cb(0, 16 * TH)
    SQ = cb(4096, 16 * TH)
    XTOK = [cf(c, 2048) for c in (0, 2048, 4096, 6144)]
    GAMS = cf(8192, GSL // 2)
    XOUT = cf(0, 2048)
    VBUF = cb(0, 2 * NH * 129)
    KMB = cb(2064, NH * 512)
    KTMP = [cb(6160, T), cb(6672, T)]
    WVS = cb(7184, DC * 256)
    QTMP = [cb(9232, T), cb(9744, T)]
    ELU = [cf(10256, T)[0:64, :], cf(11280, T)[0:64, :]]
    kvga = cf(12304, 1536)
    kvgb = cb(13840, 1536)
    RNBT = cb(15376, T)
    RNB2 = [RNBT[0:64, :], RNBT[64:128, :]]
    WVF = [cf(15888, 1024), cf(16912, 1024)]
    WVO = [cb(17936, 1024), cb(18448, 1024)]

    bv_f = wfa[0][0:1, 0:D]
    bv_b = nrm[0:1, :].bitcast(BF16)

    sems = {}
    for nm in ("lA", "lX", "lWe", "lWo", "lBe", "lBo", "lPe", "lPo",
               "lV", "lN", "sO", "lG", "l3e", "l3o", "lC",
               "sWe", "sWo", "sW2", "sW3", "sQe", "sQo", "sVe", "sVo",
               "g", "cc", "pe", "ac", "dv", "pl", "gc"):
        sems[nm] = st.enter_context(nc.semaphore("s_" + nm))
    cnt = {k: 0 for k in sems}

    def bump(inst, nm, k=1):
        inst.then_inc(sems[nm], k)
        cnt[nm] += k
        return cnt[nm]

    def wait(eng, nm, v=None):
        eng.wait_ge(sems[nm], cnt[nm] if v is None else v)

    block = st.enter_context(nc.Block())
    TT = block.tensor
    VV = block.vector
    AA = block.scalar
    GG = block.gpsimd
    SS = block.sync

    marks = {}

    def mark(key):
        marks[key] = dict(cnt)

    WG = {"dv": 0, "pe": 0, "ac": 0, "sWe": 0, "sWo": 0, "sW2": 0, "sW3": 0}

    def wguard(eng, wg):
        for nm, v in wg.items():
            if v:
                eng.wait_ge(sems[nm], v)

    def wg_update():
        for nm in WG:
            WG[nm] = cnt[nm]

    def f(g):
        for tile, val in ((identf, 0.0), (identb, 0.0), (onesb_c, 1.0),
                          (onesb_r, 1.0), (onesf_r, 1.0), (onesf_c, 1.0),
                          (negone, -1.0), (halfc, 0.5)):
            bump(g.memset(tile[:], val), "pl")
        wait(g, "pl")
        bump(g.affine_select(out=identf[:], in_=identf[:], compare_op=ALU.not_equal,
                             fill=1.0, base=0, pattern=[[-1, 128]],
                             channel_multiplier=1), "pl")
        bump(g.affine_select(out=identb[:], in_=identb[:], compare_op=ALU.not_equal,
                             fill=1.0, base=0, pattern=[[-1, 128]],
                             channel_multiplier=1), "pl")
    GG(f)
    mark("consts")

    def f(s):
        for dst, src in ((bq_s[:], bqt), (bk_s[:], bkt), (bo_s[:], bot),
                         (bv_f[:], bvr), (wmap_f[:], wmapqt)):
            bump(s.dma_start(out=dst, in_=src), "lA", 16)
    SS(f)
    mark("smallld")

    def f(a):
        wait(a, "lA", marks["smallld"]["lA"])
        bump(a.copy(out=wmapq_b[:], in_=wmap_f[:]), "ac")
        bump(a.copy(out=bv_b[:], in_=bv_f[:]), "ac")
    AA(f)
    mark("wmapq_cast")

    def f(s):
        wait(s, "ac", marks["wmapq_cast"]["ac"])
        bump(s.dma_start(out=wmap_f[:], in_=wmapkt), "lA", 16)
    SS(f)
    mark("wmapk_ld")

    def f(a):
        wait(a, "lA", marks["wmapk_ld"]["lA"])
        bump(a.copy(out=wmapk_b[:], in_=wmap_f[:]), "ac")
    AA(f)
    mark("wmapk_cast")

    with nc.psum_tensor("ps_g", [1, 4], F32) as ps_g, \
         nc.psum_tensor("ps_thr", [128, 8], F32) as ps_thr, \
         nc.psum_tensor("ps_xt0", [128, 512], F32) as ps_xt0, \
         nc.psum_tensor("ps_xt1", [128, 512], F32) as ps_xt1:
        ps_xt = [ps_xt0, ps_xt1]

        GH2 = GSL // 2
        for w in range(3):
            for hf in range(2):
                def f(a, w=w, hf=hf):
                    if w + hf > 0:
                        pw, ph = (w, hf - 1) if hf else (w - 1, 1)
                        wait(a, "dv", marks[f"gred{pw}_{ph}"]["dv"])
                    bump(a.dma_start(out=GAMS, in_=gsl[w][:, hf * GH2:(hf + 1) * GH2]),
                         "lG", 16)
                AA(f)
                mark(f"gld{w}_{hf}")

                def f(v, w=w, hf=hf):
                    wait(v, "lG", marks[f"gld{w}_{hf}"]["lG"])
                    if hf == 0 and w > 0:
                        wait(v, "pe", marks[f"gtot{w - 1}"]["pe"])
                    bump(v.reduce_sum(out=gpart[:, hf:hf + 1], in_=GAMS,
                                      axis=mybir.AxisListType.X,
                                      apply_absolute_value=True), "dv")
                VV(f)
                mark(f"gred{w}_{hf}")

            def f(t, w=w):
                wait(t, "dv", marks[f"gred{w}_1"]["dv"])
                if w == 0:
                    wait(t, "pl", marks["consts"]["pl"])
                bump(t.matmul(ps_g[0:1, w:w + 1], gpart[:, 0:1], onesf_c[:],
                              start=True, stop=False), "pe")
                bump(t.matmul(ps_g[0:1, w:w + 1], gpart[:, 1:2], onesf_c[:],
                              start=False, stop=True), "pe")
            TT(f)
            mark(f"gtot{w}")

        def f(v):
            bump(v.memset(gprow[:], 0.0), "dv")
            wait(v, "pe", marks["gtot2"]["pe"])
            wait(v, "dv")
            bump(v.tensor_copy(out=gprow[0:1, 0:3], in_=ps_g[0:1, 0:3]), "dv")
        VV(f)
        mark("gpack")

        def f(g):
            wait(g, "dv", marks["gpack"]["dv"])
            bump(g.dma_start(out=cc_gin.ap(), in_=gprow[:]), "g", 16)
            wait(g, "g")
            bump(g.collective_compute(
                "AllReduce", ALU.add, replica_groups=[list(range(NCORES))],
                ins=[cc_gin.ap().opt()], outs=[cc_gout.ap().opt()]), "cc", 1)
            wait(g, "cc")
            bump(g.dma_start(out=gprow[:], in_=cc_gout.ap()), "g", 16)
        GG(f)
        mark("gcc")

        def f(v):
            wait(v, "g", marks["gcc"]["g"])
            bump(v.tensor_scalar(out=rows[0:1, 0:4], in0=gprow[:],
                                 scalar1=0.5 / NW, scalar2=0.5e-8,
                                 op0=ALU.mult, op1=ALU.add), "dv")
            bump(v.tensor_scalar(out=rows[0:1, 4:8], in0=gprow[:],
                                 scalar1=-0.5 / NW, scalar2=-0.5e-8,
                                 op0=ALU.mult, op1=ALU.add), "dv")
        VV(f)
        mark("throw")

        def f(t):
            wait(t, "dv", marks["throw"]["dv"])
            bump(t.matmul(ps_thr[:, :], onesf_r[:], rows[0:1, 0:8],
                          start=True, stop=True), "pe")
        TT(f)
        mark("thrbc")

        def f(v):
            wait(v, "pe", marks["thrbc"]["pe"])
            bump(v.tensor_copy(out=thrv[:], in_=ps_thr[:, :]), "dv")
        VV(f)
        mark("thr")
        THR_DV = cnt["dv"]

        for grp in range(2):
            def f(s, grp=grp):
                if grp == 1:
                    wait(s, "pe", marks["xt0"]["pe"])
                for i in range(4):
                    m = grp * 4 + i
                    bump(s.dma_start(out=XTOK[i],
                                     in_=xin[m * 128:(m + 1) * 128, :]), "lX", 16)
            SS(f)
            mark(f"xld{grp}")

            for dc in range(DC):
                b = dc % 2

                def f(t, grp=grp, dc=dc, b=b):
                    if dc == 0:
                        wait(t, "lX", marks[f"xld{grp}"]["lX"])
                        if grp == 0:
                            wait(t, "pl", marks["consts"]["pl"])
                    if dc >= 2 or grp == 1:
                        pk = f"xtev{grp}_{dc - 2}" if dc >= 2 else f"xtev0_{DC - 2 + dc}"
                        wait(t, "dv", marks[pk]["dv"])
                    for i in range(3):
                        t.transpose(ps_xt[b][:, i * 128:(i + 1) * 128],
                                    XTOK[i][:, dc * 128:(dc + 1) * 128], identf[:])
                    bump(t.transpose(ps_xt[b][:, 3 * 128:4 * 128],
                                     XTOK[3][:, dc * 128:(dc + 1) * 128],
                                     identf[:]), "pe")
                TT(f)
                mark(f"xtp{grp}_{dc}")

                def f(v, grp=grp, dc=dc, b=b):
                    wait(v, "pe", marks[f"xtp{grp}_{dc}"]["pe"])
                    bump(v.tensor_copy(
                        out=xf[:, dc * T + grp * 512: dc * T + grp * 512 + 512],
                        in_=ps_xt[b][:, :]), "dv")
                VV(f)
                mark(f"xtev{grp}_{dc}")
            mark(f"xt{grp}")
    mark("xf_done")

    def emit_ln(tag, src_dv):
        with nc.psum_tensor("psS_" + tag, [1, TH], F32) as psS, \
             nc.psum_tensor("psQ_" + tag, [1, TH], F32) as psQ, \
             nc.psum_tensor("psB1_" + tag, [128, TH], F32) as psB1, \
             nc.psum_tensor("psB2_" + tag, [128, TH], F32) as psB2:
            for h2 in range(2):
                off = h2 * TH

                def f(v, off=off):
                    if off == 0:
                        wait(v, "dv", src_dv)
                    else:
                        wait(v, "pe", marks[f"{tag}mq_0"]["pe"])
                    for dc in range(DC):
                        bump(v.tensor_copy(out=XB[:, dc * TH:(dc + 1) * TH],
                                           in_=xf[:, dc * T + off: dc * T + off + TH]),
                             "dv")
                VV(f)
                mark(f"{tag}xb_{off}")

                def f(a, off=off):
                    wait(a, "dv", marks[f"{tag}xb_{off}"]["dv"])
                    bump(a.activation(out=SQ[:, :], in_=XB[:, :],
                                      func=AF.Square), "ac")
                AA(f)
                mark(f"{tag}sq_{off}")

                def f(t, off=off):
                    wait(t, "ac", marks[f"{tag}sq_{off}"]["ac"])
                    if off:
                        wait(t, "dv", marks[f"{tag}var_0"]["dv"])
                    for dc in range(DC):
                        t.matmul(psS[0:1, :], onesb_c[:], XB[:, dc * TH:(dc + 1) * TH],
                                 start=(dc == 0), stop=(dc == DC - 1))
                    for dc in range(DC - 1):
                        t.matmul(psQ[0:1, :], onesb_c[:], SQ[:, dc * TH:(dc + 1) * TH],
                                 start=(dc == 0), stop=False)
                    bump(t.matmul(psQ[0:1, :], onesb_c[:],
                                  SQ[:, (DC - 1) * TH:DC * TH],
                                  start=False, stop=True), "pe")
                TT(f)
                mark(f"{tag}mq_{off}")

                def f(v, off=off):
                    wait(v, "pe", marks[f"{tag}mq_{off}"]["pe"])
                    r = rows
                    bump(v.tensor_scalar(out=r[0:1, 0:TH], in0=psS[0:1, :],
                                         scalar1=1.0 / D, scalar2=None,
                                         op0=ALU.mult), "dv")
                    bump(v.tensor_scalar(out=r[0:1, TH:2 * TH], in0=psQ[0:1, :],
                                         scalar1=1.0 / D, scalar2=None,
                                         op0=ALU.mult), "dv")
                    wait(v, "dv")
                    bump(v.tensor_tensor(out=r[0:1, 2 * TH:3 * TH], in0=r[0:1, 0:TH],
                                         in1=r[0:1, 0:TH], op=ALU.mult), "dv")
                    wait(v, "dv")
                    bump(v.tensor_scalar(out=r[0:1, 2 * TH:3 * TH],
                                         in0=r[0:1, 2 * TH:3 * TH],
                                         scalar1=-1.0, scalar2=1e-5,
                                         op0=ALU.mult, op1=ALU.add), "dv")
                    wait(v, "dv")
                    bump(v.tensor_tensor(out=r[0:1, 2 * TH:3 * TH],
                                         in0=r[0:1, TH:2 * TH],
                                         in1=r[0:1, 2 * TH:3 * TH],
                                         op=ALU.add), "dv")
                VV(f)
                mark(f"{tag}var_{off}")

                def f(a, off=off):
                    wait(a, "dv", marks[f"{tag}var_{off}"]["dv"])
                    bump(a.activation(out=rows[0:1, 3 * TH:4 * TH],
                                      in_=rows[0:1, 2 * TH:3 * TH],
                                      func=AF.Sqrt), "ac")
                AA(f)
                mark(f"{tag}sd_{off}")

                def f(v, off=off):
                    wait(v, "ac", marks[f"{tag}sd_{off}"]["ac"])
                    r = rows
                    bump(v.reciprocal(out=r[0:1, TH:2 * TH],
                                      in_=r[0:1, 3 * TH:4 * TH]), "dv")
                    wait(v, "dv")
                    bump(v.tensor_tensor(out=r[0:1, 2 * TH:3 * TH], in0=r[0:1, 0:TH],
                                         in1=r[0:1, TH:2 * TH],
                                         op=ALU.mult), "dv")
                    wait(v, "dv")
                    bump(v.tensor_scalar(out=r[0:1, 2 * TH:3 * TH],
                                         in0=r[0:1, 2 * TH:3 * TH],
                                         scalar1=-1.0, scalar2=None,
                                         op0=ALU.mult), "dv")
                VV(f)
                mark(f"{tag}rs_{off}")

                def f(t, off=off):
                    wait(t, "dv", marks[f"{tag}rs_{off}"]["dv"])
                    bump(t.matmul(psB1[:, :], onesf_r[:], rows[0:1, TH:2 * TH],
                                  start=True, stop=True), "pe")
                    bump(t.matmul(psB2[:, :], onesf_r[:], rows[0:1, 2 * TH:3 * TH],
                                  start=True, stop=True), "pe")
                TT(f)
                mark(f"{tag}bc_{off}")

                for dc in range(DC):
                    def f(v, dc=dc, off=off):
                        if dc == 0:
                            wait(v, "pe", marks[f"{tag}bc_{off}"]["pe"])
                        else:
                            wait(v, "dv")
                        bump(v.tensor_tensor(out=G2B[0][:, 0:TH],
                                             in0=xf[:, dc * T + off: dc * T + off + TH],
                                             in1=psB1[:, :], op=ALU.mult), "dv")
                        wait(v, "dv")
                        bump(v.tensor_tensor(
                            out=ybuf[:, dc * T + off: dc * T + off + TH],
                            in0=G2B[0][:, 0:TH], in1=psB2[:, :], op=ALU.add), "dv")
                    VV(f)
                    mark(f"{tag}y{dc}_{off}")
        mark(f"{tag}done")

    def emit_ffn(call, lntag):
        with nc.psum_tensor(f"psX{call}a", [128, TH], F32) as psxa, \
             nc.psum_tensor(f"psX{call}b", [128, TH], F32) as psxb, \
             nc.psum_tensor(f"psO{call}a", [128, TH], F32) as psoa, \
             nc.psum_tensor(f"psO{call}b", [128, TH], F32) as psob:
            psX = [psxa, psxb]
            psO = [psoa, psob]
            guard = {"psX": None, "psO": None, "GH": None, "W3R": None}

            def sweep(sw, wsrc, wscr, h2):
                off = h2 * TH
                quant = (call == 0 and h2 == 0)
                key = f"F{call}s{sw}h{h2}"
                widx = 0 if sw == 1 else 1
                if quant:
                    wfaH = [wfa[s // 2][:, (s % 2) * 1024:(s % 2 + 1) * 1024]
                            for s in range(4)]
                    wqaH = [wqa[s // 2][:, (s % 2) * 1024:(s % 2 + 1) * 1024]
                            for s in range(4)]
                    SQWH = [SQW[s // 2][:, (s % 2) * 1024:(s % 2 + 1) * 1024]
                            for s in range(4)]
                    SLS = ("sWe", "sWo", "sW2", "sW3")

                    def odd_load(a, jh, sw, wg):
                        j, hf = jh // 2, jh % 2
                        sl = jh % 4
                        if jh < 4:
                            wguard(a, wg)
                            if sw == 2:
                                wait(a, "ac", marks["F0s1h0end"]["ac"])
                        bump(a.dma_start(
                            out=wfaH[sl],
                            in_=wsrc[j][:, hf * 1024:(hf + 1) * 1024]),
                            "lWo", 16)
                        marks[f"{key}l{jh}"] = dict(cnt)

                    for jh2 in range(0, 2 * HJ, 2):
                        jh = jh2
                        j, hf = jh // 2, jh % 2
                        sl = jh % 4

                        def f(s, jh=jh, j=j, hf=hf, sl=sl, sw=sw,
                              wg=dict(WG)):
                            if jh < 4:
                                wguard(s, wg)
                                if sw == 2:
                                    wait(s, "ac", marks["F0s1h0end"]["ac"])
                            else:
                                wait(s, "ac", marks[f"{key}s2_{jh - 4}"]["ac"])
                            bump(s.dma_start(
                                out=wfaH[sl],
                                in_=wsrc[j][:, hf * 1024:(hf + 1) * 1024]),
                                "lWe", 16)
                        SS(f)
                        mark(f"{key}l{jh}")

                        def f(a, jh2=jh2, sw=sw, widx=widx, key=key,
                              wg=dict(WG)):
                            if jh2 == 0:
                                odd_load(a, 1, sw, wg)
                                odd_load(a, 3, sw, wg)
                            for jh in (jh2, jh2 + 1):
                                sl = jh % 4
                                lw = "lWe" if jh % 2 == 0 else "lWo"
                                sws = SLS[sl]
                                wait(a, lw, marks[f"{key}l{jh}"][lw])
                                wait(a, "dv", THR_DV)
                                if jh >= 4:
                                    wait(a, "pe", marks[f"{key}m{jh - 4}"]["pe"])
                                    wait(a, sws, marks[f"{key}w{jh - 4}"][sws])
                                    wait(a, "dv", marks[f"{key}q{jh - 4}"]["dv"])
                                elif sw == 2:
                                    wait(a, "pe", marks["F0s1h0end"]["pe"])
                                    wait(a, sws, marks["F0s1h0end"][sws])
                                    wait(a, "dv", marks["F0s1h0end"]["dv"])
                                bump(a.activation(
                                    out=wqaH[sl][:, :], in_=wfaH[sl][:, :],
                                    func=AF.Sign,
                                    bias=thrv[:, 4 + widx:5 + widx]), "ac")
                                wait(a, "ac")
                                bump(a.activation(
                                    out=SQWH[sl][:, :], in_=wfaH[sl][:, :],
                                    func=AF.Sign,
                                    bias=thrv[:, widx:widx + 1]), "ac")
                                marks[f"{key}s2_{jh}"] = dict(cnt)
                            if jh2 + 5 < 2 * HJ:
                                odd_load(a, jh2 + 5, sw, wg)
                        AA(f)

                        for jh in (jh2, jh2 + 1):
                            j, hf = jh // 2, jh % 2
                            sl = jh % 4
                            sws = SLS[sl]

                            def f(v, jh=jh, sl=sl):
                                wait(v, "ac", marks[f"{key}s2_{jh}"]["ac"])
                                bump(v.tensor_tensor(
                                    out=wqaH[sl][:, :], in0=wqaH[sl][:, :],
                                    in1=SQWH[sl][:, :], op=ALU.add), "dv")
                            VV(f)
                            mark(f"{key}q{jh}")

                            def f(g, jh=jh, j=j, hf=hf, sl=sl, sws=sws):
                                wait(g, "dv", marks[f"{key}q{jh}"]["dv"])
                                bump(g.dma_start(
                                    out=wscr[j][:, hf * 1024:(hf + 1) * 1024],
                                    in_=wqaH[sl][:, :]), sws, 16)
                            (SS if jh % 2 == 0 else AA)(f)
                            mark(f"{key}w{jh}")

                            def f(t, jh=jh, j=j, hf=hf, sl=sl):
                                wait(t, "dv", marks[f"{key}q{jh}"]["dv"])
                                if jh == 0:
                                    wait(t, "dv", marks[f"{lntag}done"]["dv"])
                                    if guard["psX"] is not None:
                                        for nm, v in guard["psX"].items():
                                            wait(t, nm, v)
                                if hf == 0 and j >= 2:
                                    wait(t, "dv", marks[f"{key}c{j - 2}"]["dv"])
                                pj = j % 2
                                for di in range(8):
                                    dc = hf * 8 + di
                                    mm = t.matmul(
                                        psX[pj][:, :],
                                        wqaH[sl][:, di * 128:(di + 1) * 128],
                                        ybuf[:, dc * T + off: dc * T + off + TH],
                                        start=(dc == 0), stop=(dc == DC - 1))
                                bump(mm, "pe")
                            TT(f)
                            mark(f"{key}m{jh}")

                            if hf == 1:
                                if sw == 1:
                                    def f(v, j=j, jh=jh):
                                        wait(v, "pe", marks[f"{key}m{jh}"]["pe"])
                                        if j == 0 and guard["GH"] is not None:
                                            wait(v, "pe", guard["GH"])
                                        bump(v.tensor_copy(
                                            out=GH[:, j * TH:(j + 1) * TH],
                                            in_=psX[j % 2][:, :]), "dv")
                                    VV(f)
                                    mark(f"{key}c{j}")
                                else:
                                    def f(v, j=j, jh=jh):
                                        wait(v, "pe", marks[f"{key}m{jh}"]["pe"])
                                        if j == 0:
                                            wait(v, "ac",
                                                 marks[f"sig{call}_{h2}"]["ac"])
                                        if j >= 2:
                                            wait(v, "dv", marks[f"{key}h{j - 2}"]["dv"])
                                        bump(v.tensor_copy(out=x2t[j % 2][:, :],
                                                           in_=psX[j % 2][:, :]), "dv")
                                        wait(v, "dv")
                                        bump(v.tensor_tensor(
                                            out=GH[:, j * TH:(j + 1) * TH],
                                            in0=GH[:, j * TH:(j + 1) * TH],
                                            in1=x2t[j % 2][:, :],
                                            op=ALU.mult), "dv")
                                    VV(f)
                                    mark(f"{key}c{j}")
                                    marks[f"{key}h{j}"] = dict(cnt)
                    mark(f"{key}end")
                    guard["psX"] = {"dv": cnt["dv"]}
                    wg_update()
                    return
                for j in range(HJ):
                    p = j % 2
                    if True:
                        def f(s, j=j, p=p, sw=sw, wg=dict(WG)):
                            if j < 2:
                                wguard(s, wg)
                                if call == 0 and h2 == 1 and j == 0:
                                    kh = f"F0s{sw}h0"
                                    for snm in ("sWe", "sWo", "sW2", "sW3"):
                                        wait(s, snm, marks[f"{kh}end"][snm])
                            if j >= 2:
                                wait(s, "pe", marks[f"{key}m{j - 2}"]["pe"])
                            bump(s.dma_start(out=wqa[p][:, :], in_=wscr[j]),
                                 "lWe" if j % 2 == 0 else "lWo", 16)
                        SS(f)
                        mark(f"{key}l{j}")

                    def f(t, j=j, p=p):
                        if quant:
                            wait(t, "dv", marks[f"{key}q{j}"]["dv"])
                        else:
                            lw = "lWe" if j % 2 == 0 else "lWo"
                            wait(t, lw, marks[f"{key}l{j}"][lw])
                        if j == 0:
                            wait(t, "dv", marks[f"{lntag}done"]["dv"])
                            if guard["psX"] is not None:
                                for nm, v in guard["psX"].items():
                                    wait(t, nm, v)
                        if j >= 2:
                            wait(t, "dv", marks[f"{key}c{j - 2}"]["dv"])
                        for dc in range(DC - 1):
                            t.matmul(psX[p][:, :],
                                     wqa[p][:, dc * 128:(dc + 1) * 128],
                                     ybuf[:, dc * T + off: dc * T + off + TH],
                                     start=(dc == 0), stop=False)
                        bump(t.matmul(psX[p][:, :],
                                      wqa[p][:, (DC - 1) * 128:DC * 128],
                                      ybuf[:, (DC - 1) * T + off: (DC - 1) * T + off + TH],
                                      start=False, stop=True), "pe")
                    TT(f)
                    mark(f"{key}m{j}")

                    if sw == 1:
                        def f(v, j=j, p=p):
                            wait(v, "pe", marks[f"{key}m{j}"]["pe"])
                            if j == 0 and guard["GH"] is not None:
                                wait(v, "pe", guard["GH"])
                            bump(v.tensor_copy(out=GH[:, j * TH:(j + 1) * TH],
                                               in_=psX[p][:, :]), "dv")
                        VV(f)
                        mark(f"{key}c{j}")
                    else:
                        def f(v, j=j, p=p):
                            wait(v, "pe", marks[f"{key}m{j}"]["pe"])
                            if j == 0:
                                wait(v, "ac", marks[f"sig{call}_{h2}"]["ac"])
                            if j >= 2:
                                wait(v, "dv", marks[f"{key}h{j - 2}"]["dv"])
                            bump(v.tensor_copy(out=x2t[p][:, :],
                                               in_=psX[p][:, :]), "dv")
                            wait(v, "dv")
                            bump(v.tensor_tensor(out=GH[:, j * TH:(j + 1) * TH],
                                                 in0=GH[:, j * TH:(j + 1) * TH],
                                                 in1=x2t[p][:, :],
                                                 op=ALU.mult), "dv")
                        VV(f)
                        mark(f"{key}c{j}")
                        marks[f"{key}h{j}"] = dict(cnt)
                mark(f"{key}end")
                guard["psX"] = {"dv": cnt["dv"]}
                wg_update()

            def sig_batch(h2):
                def f(a, h2=h2):
                    wait(a, "dv", marks[f"F{call}s1h{h2}c{HJ - 1}"]["dv"])
                    bump(a.activation(out=GH[:, :], in_=GH[:, :],
                                      func=AF.Sigmoid,
                                      scale=halfc[:, 0:1]), "ac")
                AA(f)
                mark(f"sig{call}_{h2}")

            def phase_b(h2):
                off = h2 * TH
                key = f"F{call}B{h2}"
                qmode = (call == 0 and h2 == 0)
                if qmode:
                    QB = HJ // 4

                    def odd_loadb(a, bidx):
                        o_, qq_ = divmod(bidx, 4)
                        sin = bidx % 3
                        if bidx < 3:
                            wait(a, "dv", marks["F0s2h0end"]["dv"])
                        bump(a.dma_start(
                            out=W3I[sin],
                            in_=w3t[o_][:, qq_ * 1408:(qq_ + 1) * 1408]),
                            "lBo", 16)
                        marks[f"{key}l{bidx}"] = dict(cnt)

                    for o in range(DC):
                        for qq2 in range(0, 4, 2):
                          if True:
                            bidx = o * 4 + qq2
                            sin = bidx % 3

                            def f(s, o=o, qq=qq2, bidx=bidx, sin=sin):
                                if bidx < 3:
                                    wait(s, "dv", marks["F0s2h0end"]["dv"])
                                else:
                                    wait(s, "ac",
                                         marks[f"{key}s2_{bidx - 3}"]["ac"])
                                bump(s.dma_start(
                                    out=W3I[sin],
                                    in_=w3t[o][:, qq * 1408:(qq + 1) * 1408]),
                                    "lBe", 16)
                            SS(f)
                            mark(f"{key}l{bidx}")

                          def f(a, o=o, qq2=qq2, key=key):
                            if o == 0 and qq2 == 0:
                                odd_loadb(a, 1)
                            for qq in (qq2, qq2 + 1):
                                bidx = o * 4 + qq
                                p = bidx % 2
                                sin = bidx % 3
                                lb = "lBe" if bidx % 2 == 0 else "lBo"
                                wait(a, lb, marks[f"{key}l{bidx}"][lb])
                                sqp = "sQe" if bidx % 2 == 0 else "sQo"
                                if bidx >= 2:
                                    wait(a, "pe", marks[f"{key}m{bidx - 2}"]["pe"])
                                    wait(a, sqp, marks[f"{key}w{bidx - 2}"][sqp])
                                    wait(a, "dv", marks[f"{key}q{bidx - 2}"]["dv"])
                                bump(a.activation(
                                    out=W3Rq[p][:, :], in_=W3I[sin],
                                    func=AF.Sign, bias=thrv[:, 6:7]), "ac")
                                wait(a, "ac")
                                bump(a.activation(
                                    out=SQB3[p][:, :], in_=W3I[sin],
                                    func=AF.Sign, bias=thrv[:, 2:3]), "ac")
                                marks[f"{key}s2_{bidx}"] = dict(cnt)
                            if o * 4 + qq2 + 3 < 4 * DC:
                                odd_loadb(a, o * 4 + qq2 + 3)
                          AA(f)

                          for qq in (qq2, qq2 + 1):
                            bidx = o * 4 + qq
                            p = bidx % 2

                            def f(v, bidx=bidx, p=p):
                                wait(v, "ac", marks[f"{key}s2_{bidx}"]["ac"])
                                bump(v.tensor_tensor(
                                    out=W3Rq[p][:, :], in0=W3Rq[p][:, :],
                                    in1=SQB3[p][:, :], op=ALU.add), "dv")
                            VV(f)
                            mark(f"{key}q{bidx}")

                            def f(g, o=o, qq=qq, bidx=bidx, p=p):
                                sqp = "sQe" if bidx % 2 == 0 else "sQo"
                                wait(g, "dv", marks[f"{key}q{bidx}"]["dv"])
                                bump(g.dma_start(
                                    out=w3q[o][:, qq * 1408:(qq + 1) * 1408],
                                    in_=W3Rq[p][:, :]), sqp, 16)
                            (SS if bidx % 2 == 0 else AA)(f)
                            mark(f"{key}w{bidx}")

                            def f(t, o=o, qq=qq, bidx=bidx, p=p):
                                wait(t, "dv", marks[f"{key}q{bidx}"]["dv"])
                                if bidx == 0:
                                    wait(t, "dv",
                                         marks[f"F{call}s2h{h2}h{HJ - 1}"]["dv"])
                                    if guard["psO"] is not None:
                                        for nm, v in guard["psO"].items():
                                            wait(t, nm, v)
                                if bidx >= 8 and qq == 0:
                                    wait(t, "dv", marks[f"{key}e{o - 2}"]["dv"])
                                for j in range(QB):
                                    jj = qq * QB + j
                                    mm = t.matmul(
                                        psO[o % 2][:, :],
                                        W3Rq[p][:, j * 128:(j + 1) * 128],
                                        GH[:, jj * TH:(jj + 1) * TH],
                                        start=(jj == 0), stop=(jj == HJ - 1))
                                bump(mm, "pe")
                            TT(f)
                            mark(f"{key}m{bidx}")

                        def f(v, o=o):
                            wait(v, "pe", marks[f"{key}m{o * 4 + 3}"]["pe"])
                            sl = xf[:, o * T + off: o * T + off + TH]
                            bump(v.scalar_tensor_tensor(
                                out=sl, in0=psO[o % 2][:, :], scalar=0.125,
                                in1=sl, op0=ALU.mult, op1=ALU.add), "dv")
                        VV(f)
                        mark(f"{key}e{o}")
                    mark(f"{key}end")
                    guard["psO"] = {"dv": cnt["dv"]}
                    guard["W3R"] = cnt["pe"]
                    guard["GH"] = cnt["pe"]
                    return
                HB = HJ // 2
                for o in range(DC):
                    for hb in range(2):
                        bidx = o * 2 + hb
                        p = bidx % 2

                        def f(s, o=o, hb=hb, bidx=bidx, p=p):
                            if bidx < 2:
                                if call == 0 and h2 == 1:
                                    wait(s, "sQe", marks["F0B0end"]["sQe"])
                                    wait(s, "sQo", marks["F0B0end"]["sQo"])
                                if guard["W3R"] is not None:
                                    wait(s, "pe", guard["W3R"])
                            if bidx >= 2:
                                wait(s, "pe", marks[f"{key}m{bidx - 2}"]["pe"])
                            bump(s.dma_start(
                                out=W3R[p],
                                in_=w3q[o][:, hb * 2816:(hb + 1) * 2816]),
                                "lBe" if bidx % 2 == 0 else "lBo", 16)
                        SS(f)
                        mark(f"{key}l{bidx}")

                        def f(t, o=o, hb=hb, bidx=bidx, p=p):
                            lb = "lBe" if bidx % 2 == 0 else "lBo"
                            wait(t, lb, marks[f"{key}l{bidx}"][lb])
                            if bidx == 0:
                                wait(t, "dv", marks[f"F{call}s2h{h2}h{HJ - 1}"]["dv"])
                                if guard["psO"] is not None:
                                    for nm, v in guard["psO"].items():
                                        wait(t, nm, v)
                            if bidx >= 4 and hb == 0:
                                wait(t, "dv", marks[f"{key}e{o - 2}"]["dv"])
                            for j in range(HB):
                                jj = hb * HB + j
                                mm = t.matmul(psO[o % 2][:, :],
                                              W3R[p][:, j * 128:(j + 1) * 128],
                                              GH[:, jj * TH:(jj + 1) * TH],
                                              start=(jj == 0), stop=(jj == HJ - 1))
                            bump(mm, "pe")
                        TT(f)
                        mark(f"{key}m{bidx}")

                    def f(v, o=o):
                        wait(v, "pe", marks[f"{key}m{o * 2 + 1}"]["pe"])
                        sl = xf[:, o * T + off: o * T + off + TH]
                        bump(v.scalar_tensor_tensor(
                            out=sl, in0=psO[o % 2][:, :], scalar=0.125, in1=sl,
                            op0=ALU.mult, op1=ALU.add), "dv")
                    VV(f)
                    mark(f"{key}e{o}")
                mark(f"{key}end")
                guard["psO"] = {"dv": cnt["dv"]}
                guard["W3R"] = cnt["pe"]
                guard["GH"] = cnt["pe"]

            if call == 0:
                sweep(1, w1t, w1q, 0)
                sig_batch(0)
                sweep(2, w2t, w2q, 0)
                phase_b(0)
                sweep(1, None, w1q, 1)
                sig_batch(1)
                sweep(2, None, w2q, 1)
                phase_b(1)
            else:
                for h2 in range(2):
                    sweep(1, None, w1q, h2)
                    sig_batch(h2)
                    sweep(2, None, w2q, h2)
                    phase_b(h2)
        mark(f"ffn{call}done")

    def emit_attention():
        prev = dict(cnt)
        with nc.psum_tensor("psP0", [128, TH], F32) as psp0, \
             nc.psum_tensor("psP1", [128, TH], F32) as psp1, \
             nc.psum_tensor("psM0", [64, TH], F32) as psm0, \
             nc.psum_tensor("psM1", [64, TH], F32) as psm1, \
             nc.psum_tensor("psTq", [128, 512], BF16) as psTq, \
             nc.psum_tensor("psKV0", [128, 129], F32) as pskv0, \
             nc.psum_tensor("psKV1", [128, 129], F32) as pskv1:
            psP = [psp0, psp1]
            psM = [psm0, psm1]
            psK = psP
            psKV = [pskv0, pskv1]

            for ci in range(32):
                dc, hhalf = divmod(ci, 2)
                p = ci % 2

                svp = "sVe" if ci % 2 == 0 else "sVo"

                def f(a, ci=ci, dc=dc, hhalf=hhalf, p=p, svp=svp):
                    if ci == 0:
                        wait(a, "dv", marks["L2done"]["dv"])
                    if ci >= 2:
                        wait(a, "ac", marks[f"wvc{ci - 2}"]["ac"])
                    bump(a.dma_start(out=WVF[p],
                                     in_=wvt[dc][:, hhalf * 1024:(hhalf + 1) * 1024]),
                         "lC", 16)
                    wait(a, "lC")
                    if ci >= 2:
                        wait(a, svp, marks[f"wvs{ci - 2}"][svp])
                    bump(a.copy(out=WVO[p], in_=WVF[p]), "ac")
                AA(f)
                mark(f"wvc{ci}")

                def f(a, ci=ci, dc=dc, hhalf=hhalf, p=p, svp=svp):
                    wait(a, "ac", marks[f"wvc{ci}"]["ac"])
                    bump(a.dma_start(
                        out=wvb_d[dc][:, hhalf * 1024:(hhalf + 1) * 1024],
                        in_=WVO[p]), svp, 16)
                AA(f)
                mark(f"wvs{ci}")

            def proj_head(h, wsrc, bias_s, tmp, pref, is_q, tmp_guard=None):
                p = h % 2

                def f(s, h=h, p=p, wg=dict(WG)):
                    if h < 2:
                        wguard(s, wg)
                    if h >= 2:
                        wait(s, "dv", marks[f"{pref}cast{h - 2}"]["dv"])
                    bump(s.dma_start(out=wfa[p][:, :], in_=wsrc[h]),
                         "lPe" if h % 2 == 0 else "lPo", 16)
                SS(f)
                mark(f"{pref}ld{h}")

                def f(v, h=h, p=p):
                    lp = "lPe" if h % 2 == 0 else "lPo"
                    wait(v, lp, marks[f"{pref}ld{h}"][lp])
                    if h >= 2:
                        wait(v, "pe", marks[f"{pref}mm{h - 2}"]["pe"])
                    bump(v.tensor_copy(out=wqa[p][:, :], in_=wfa[p][:, :]), "dv")
                VV(f)
                mark(f"{pref}cast{h}")

                def f(t, h=h, p=p):
                    wait(t, "dv", marks[f"{pref}cast{h}"]["dv"])
                    if h == 0:
                        wait(t, "dv", marks["L2done"]["dv"])
                        for nm, v in prev.items():
                            wait(t, nm, v)
                    if h >= 1:
                        wait(t, "dv", marks[f"{pref}ev{h - 1}"]["dv"])
                    for n2 in range(2):
                        for dc in range(DC - 1):
                            t.matmul(psP[n2][:, :],
                                     wqa[p][:, dc * 128:(dc + 1) * 128],
                                     ybuf[:, dc * T + n2 * TH: dc * T + n2 * TH + TH],
                                     start=(dc == 0), stop=False)
                        mm = t.matmul(psP[n2][:, :],
                                      wqa[p][:, (DC - 1) * 128:DC * 128],
                                      ybuf[:, (DC - 1) * T + n2 * TH: (DC - 1) * T + n2 * TH + TH],
                                      start=False, stop=True)
                    bump(mm, "pe")
                TT(f)
                mark(f"{pref}mm{h}")

                def f(v, h=h, p=p):
                    wait(v, "pe", marks[f"{pref}mm{h}"]["pe"])
                    if h >= 2:
                        wait(v, "pe", marks[f"{pref}mp{h - 2}"]["pe"])
                        if tmp_guard is not None:
                            wait(v, "pe", marks[f"{tmp_guard}{h - 2}"]["pe"])
                    for n2 in range(2):
                        bump(v.tensor_scalar(out=tmp[p][:, n2 * TH:(n2 + 1) * TH],
                                             in0=psP[n2][:, :],
                                             scalar1=bias_s[:, h:h + 1],
                                             scalar2=None, op0=ALU.add), "dv")
                VV(f)
                mark(f"{pref}ev{h}")

                wmap = wmapq_b if is_q else wmapk_b

                def f(t, h=h, p=p):
                    wait(t, "dv", marks[f"{pref}ev{h}"]["dv"])
                    if h >= 1:
                        wait(t, "ac", marks[f"{pref}el{h - 1}"]["ac"])
                    for n2 in range(2):
                        mm = t.matmul(psM[n2][:, :], wmap[:, :],
                                      tmp[p][:, n2 * TH:(n2 + 1) * TH],
                                      start=True, stop=True)
                    bump(mm, "pe")
                TT(f)
                mark(f"{pref}mp{h}")

                def f(a, h=h, p=p):
                    wait(a, "pe", marks[f"{pref}mp{h}"]["pe"])
                    if h >= 1:
                        wait(a, "dv", marks[f"{pref}qm{h - 1}"]["dv"])
                    for n2 in range(2):
                        bump(a.activation(out=ELU[0][:, n2 * TH:(n2 + 1) * TH],
                                          in_=psM[n2][:, :], func=AF.Relu), "ac")
                        bump(a.activation(out=ELU[1][:, n2 * TH:(n2 + 1) * TH],
                                          in_=psM[n2][:, :], func=AF.Relu,
                                          scale=negone[0:64, 0:1]), "ac")
                    wait(a, "ac")
                    bump(a.activation(out=ELU[1][:, :], in_=ELU[1][:, :],
                                      func=AF.Exp, scale=negone[0:64, 0:1]), "ac")
                AA(f)
                mark(f"{pref}el{h}")

            wg_update()
            for h in range(NH):
                proj_head(h, wqt, bq_s, QTMP, "q", True)

                def f(v, h=h):
                    wait(v, "ac", marks[f"qel{h}"]["ac"])
                    bump(v.tensor_tensor(
                        out=qmb[(h % 2) * 64:(h % 2) * 64 + 64,
                                (h // 2) * T:(h // 2) * T + T],
                        in0=ELU[0][:, :], in1=ELU[1][:, :], op=ALU.add), "dv")
                VV(f)
                mark(f"qqm{h}")

            wg_update()
            for h in range(NH):
                proj_head(h, wkt, bk_s, KTMP, "k", False, tmp_guard="ktp")

                def f(v, h=h):
                    wait(v, "ac", marks[f"kel{h}"]["ac"])
                    if h >= 1:
                        wait(v, "pe", marks[f"ktp{h - 1}"]["pe"])
                    bump(v.tensor_tensor(
                        out=KTMP[h % 2][0:64, :],
                        in0=ELU[0][:, :], in1=ELU[1][:, :], op=ALU.add), "dv")
                VV(f)
                mark(f"kqm{h}")

                def f(t, h=h):
                    wait(t, "dv", marks[f"kqm{h}"]["dv"])
                    if h >= 1:
                        wait(t, "dv", marks[f"kmv{h - 1}"]["dv"])
                    for mb in range(7):
                        t.transpose(psTq[:, mb * 64:(mb + 1) * 64],
                                    KTMP[h % 2][0:64, mb * 128:(mb + 1) * 128],
                                    identb[0:64, 0:64])
                    bump(t.transpose(psTq[:, 7 * 64:8 * 64],
                                     KTMP[h % 2][0:64, 7 * 128:8 * 128],
                                     identb[0:64, 0:64]), "pe")
                TT(f)
                mark(f"ktp{h}")

                def f(v, h=h):
                    wait(v, "pe", marks[f"ktp{h}"]["pe"])
                    bump(v.tensor_copy(out=KMB[:, h * 512:(h + 1) * 512],
                                       in_=psTq[:, :]), "dv")
                VV(f)
                mark(f"kmv{h}")

            def f(g):
                wait(g, "pe", marks["L2done"]["pe"])
                wait(g, "dv", marks["L2done"]["dv"])
                vv = VBUF.rearrange("p (i h e) -> p i h e", i=2, e=129)
                bump(g.memset(vv[:, :, :, 128:129], 1.0), "pl")
            GG(f)
            mark("vones")

            NG = 4
            for g2 in range(NG):
                for osl in range(8):
                    sidx = g2 * 8 + osl

                    def f(s, g2=g2, osl=osl, sidx=sidx):
                        if sidx == 0:
                            wait(s, "sVe", marks["wvs31"]["sVe"])
                            wait(s, "sVo", marks["wvs31"]["sVo"])
                        if sidx >= 1:
                            wait(s, "pe", marks[f"vmm{sidx - 1}_1"]["pe"])
                        bump(s.dma_start(
                            out=WVS,
                            in_=wvb_d.rearrange("d p c -> p d c")
                            [:, :, osl * 256:(osl + 1) * 256]), "lV", 16)
                    SS(f)
                    mark(f"vld{sidx}")

                    for i in range(2):
                        mchunk = g2 * 2 + i
                        b = i % 2

                        def f(t, g2=g2, osl=osl, sidx=sidx, i=i, mchunk=mchunk, b=b):
                            if sidx == 0 and i == 0:
                                wait(t, "dv", marks[f"kev{NH - 1}"]["dv"])
                                wait(t, "dv", marks[f"kmv{NH - 1}"]["dv"])
                            if osl == 0 and i == 0 and g2 > 0:
                                wait(t, "pe", marks[f"kvmm{g2 - 1}"]["pe"])
                            if i == 0:
                                wait(t, "lV", marks[f"vld{sidx}"]["lV"])
                            if i >= 2 or sidx > 0:
                                pk = (f"vev{sidx}_{i - 2}" if i >= 2
                                      else f"vev{sidx - 1}_{i}")
                                wait(t, "dv", marks[pk]["dv"])
                            for dc in range(DC - 1):
                                t.matmul(psP[b][:, 0:256],
                                         ybuf[:, dc * T + mchunk * 128: dc * T + (mchunk + 1) * 128],
                                         WVS[:, dc * 256:(dc + 1) * 256],
                                         start=(dc == 0), stop=False)
                            t.matmul(psP[b][:, 0:256],
                                     ybuf[:, (DC - 1) * T + mchunk * 128: (DC - 1) * T + (mchunk + 1) * 128],
                                     WVS[:, (DC - 1) * 256:DC * 256],
                                     start=False, stop=False)
                            bump(t.matmul(psP[b][:, 0:256], onesb_r[:],
                                          bv_b[0:1, osl * 256:(osl + 1) * 256],
                                          start=False, stop=True), "pe")
                        TT(f)
                        mark(f"vmm{sidx}_{i}")

                        def f(v, g2=g2, osl=osl, sidx=sidx, i=i, b=b):
                            wait(v, "pe", marks[f"vmm{sidx}_{i}"]["pe"])
                            if sidx == 0 and i == 0:
                                wait(v, "pl", marks["vones"]["pl"])
                            for hh in range(2):
                                h = osl * 2 + hh
                                bump(v.tensor_copy(
                                    out=VBUF[:, i * (NH * 129) + h * 129:
                                             i * (NH * 129) + h * 129 + 128],
                                    in_=psP[b][:, hh * 128:(hh + 1) * 128]), "dv")
                        VV(f)
                        mark(f"vev{sidx}_{i}")

                for h in range(NH):
                    q3 = h // 6
                    j3 = (h % 6) // 2
                    i3 = h % 2
                    kidx = g2 * NH + h

                    def f(t, g2=g2, h=h, i3=i3, kidx=kidx):
                        if h == 0:
                            wait(t, "dv", marks[f"vev{g2 * 8 + 7}_1"]["dv"])
                        if kidx >= 2:
                            wait(t, "dv", marks[f"kvadd{kidx - 2}"]["dv"])
                        for i in range(2):
                            mchunk = g2 * 2 + i
                            mm = t.matmul(
                                psKV[h % 2][i3 * 64:(i3 + 1) * 64, :],
                                KMB[:, h * 512 + mchunk * 64: h * 512 + (mchunk + 1) * 64],
                                VBUF[:, i * (NH * 129) + h * 129:
                                     i * (NH * 129) + (h + 1) * 129],
                                start=(i == 0), stop=(i == 1))
                        bump(mm, "pe")
                    TT(f)
                    mark(f"kvmm{kidx}")

                    def f(v, g2=g2, h=h, q3=q3, j3=j3, i3=i3, kidx=kidx):
                        wait(v, "pe", marks[f"kvmm{kidx}"]["pe"])
                        dst = kvga[i3 * 64:(i3 + 1) * 64,
                                   q3 * 512 + j3 * 129: q3 * 512 + (j3 + 1) * 129]
                        ps = psKV[h % 2][i3 * 64:(i3 + 1) * 64, :]
                        if g2 == 0:
                            bump(v.tensor_copy(out=dst, in_=ps), "dv")
                        else:
                            bump(v.tensor_tensor(out=dst, in0=dst, in1=ps,
                                                 op=ALU.add), "dv")
                    VV(f)
                    mark(f"kvadd{kidx}")

            mark("kvev")

            def f(g):
                wait(g, "dv", marks["kvev"]["dv"])
                bump(g.dma_start(out=cc_kin.ap(), in_=kvga[:]), "g", 16)
                wait(g, "g")
                bump(g.collective_compute(
                    "AllReduce", ALU.add, replica_groups=[list(range(NCORES))],
                    ins=[cc_kin.ap().opt()], outs=[cc_kout.ap().opt()]), "cc", 1)
                wait(g, "cc")
                bump(g.dma_start(out=kvga[:], in_=cc_kout.ap()), "g", 16)
            GG(f)
            mark("kvcc")

            def f(v):
                wait(v, "g", marks["kvcc"]["g"])
                bump(v.tensor_copy(out=kvgb[:], in_=kvga[:]), "dv")
            VV(f)
            mark("kvcast")

            def kvoff(h):
                q3 = h // 6
                j3 = (h % 6) // 2
                i3 = h % 2
                return q3, j3, i3

            for h in range(NH):
                q3, j3, i3 = kvoff(h)

                def f(t, h=h, q3=q3, j3=j3, i3=i3):
                    if h == 0:
                        wait(t, "dv", marks["kvcast"]["dv"])
                    if h >= 1:
                        wait(t, "dv", marks[f"nev{h - 1}"]["dv"])
                    ks = kvgb[i3 * 64:(i3 + 1) * 64,
                              q3 * 512 + j3 * 129 + 128: q3 * 512 + j3 * 129 + 129]
                    for n2 in range(2):
                        mm = t.matmul(psM[n2][0:1, :], ks,
                                      qmb[i3 * 64:(i3 + 1) * 64,
                                          (h // 2) * T + n2 * TH: (h // 2) * T + (n2 + 1) * TH],
                                      start=True, stop=True)
                    bump(mm, "pe")
                TT(f)
                mark(f"nmm{h}")

                def f(v, h=h):
                    wait(v, "pe", marks[f"nmm{h}"]["pe"])
                    if h >= 2:
                        wait(v, "lN", marks[f"nsth{h - 1}"]["lN"])
                    ro = (h % 2) * T
                    bump(v.tensor_copy(out=rows[0:1, ro:ro + TH],
                                       in_=psM[0][0:1, :]), "dv")
                    bump(v.tensor_copy(out=rows[0:1, ro + TH:ro + T],
                                       in_=psM[1][0:1, :]), "dv")
                VV(f)
                mark(f"nev{h}")

                def f(s, h=h):
                    wait(s, "dv", marks[f"nev{h}"]["dv"])
                    ro = (h % 2) * T
                    bump(s.dma_start(out=nrm_d.ap()[h:h + 1, :],
                                     in_=rows[0:1, ro:ro + T]), "lN", 16)
                SS(f)
                mark(f"nsth{h}")

            NB = rows[:, 1024:1152]
            nflat0 = nrm_d.ap().rearrange("a b -> (a b)")

            def f(s):
                wait(s, "lN", marks[f"nsth{NH - 1}"]["lN"])
                bump(s.dma_start(out=NB, in_=nflat0[:]), "lN", 16)
            SS(f)
            mark("nbl")

            def f(v):
                wait(v, "lN", marks["nbl"]["lN"])
                bump(v.tensor_scalar(out=NB, in0=NB, scalar1=1e-6, scalar2=None,
                                     op0=ALU.add), "dv")
                wait(v, "dv")
                bump(v.reciprocal(out=NB, in_=NB), "dv")
            VV(f)
            mark("rnorm")

            def f(s):
                wait(s, "dv", marks["rnorm"]["dv"])
                bump(s.dma_start(out=nflat0[:], in_=NB), "lN", 16)
            SS(f)
            mark("nst")

            nflat = nrm_d.ap().rearrange("a b -> (a b)")

            for h in range(NH):
                q3, j3, i3 = kvoff(h)
                if h % 2 == 0:
                    def f(s, h=h):
                        if h == 0:
                            wait(s, "lN", marks["nst"]["lN"])
                        else:
                            wait(s, "pe", marks[f"rbc{h - 1}"]["pe"])
                        bump(s.dma_start(out=rows[0:1, 0:2048],
                                         in_=nflat[h * T:(h + 2) * T]), "lN", 16)
                    SS(f)
                    mark(f"nld{h}")

                def f(t, h=h):
                    wait(t, "lN", marks[f"nld{h - h % 2}"]["lN"])
                    if h == 0:
                        wait(t, "dv", marks["rnorm"]["dv"])
                    if h >= 1:
                        wait(t, "dv", marks[f"rnb{h - 1}"]["dv"])
                    for n2 in range(2):
                        mm = t.matmul(psM[n2][:, :],
                                      onesf_r[0:1, 0:64],
                                      rows[0:1, (h % 2) * T + n2 * TH:
                                           (h % 2) * T + (n2 + 1) * TH],
                                      start=True, stop=True)
                    bump(mm, "pe")
                TT(f)
                mark(f"rbc{h}")

                def f(v, h=h, i3=i3):
                    wait(v, "pe", marks[f"rbc{h}"]["pe"])
                    if h >= 1:
                        wait(v, "dv", marks[f"qn{h - 1}"]["dv"])
                    bump(v.tensor_copy(out=RNB2[i3][:, 0:TH], in_=psM[0][:, :]), "dv")
                    bump(v.tensor_copy(out=RNB2[i3][:, TH:T], in_=psM[1][:, :]), "dv")
                VV(f)
                mark(f"rnb{h}")

                def f(v, h=h, i3=i3):
                    wait(v, "dv", marks[f"rnb{h}"]["dv"])
                    sl = qmb[i3 * 64:(i3 + 1) * 64, (h // 2) * T:(h // 2) * T + T]
                    bump(v.tensor_tensor(out=sl, in0=sl, in1=RNB2[i3][:, :],
                                         op=ALU.mult), "dv")
                VV(f)
                mark(f"qn{h}")

                def f(t, h=h, q3=q3, j3=j3, i3=i3):
                    wait(t, "dv", marks[f"qn{h}"]["dv"])
                    if h >= 1:
                        wait(t, "dv", marks[f"aev{h - 1}"]["dv"])
                    kv = kvgb[i3 * 64:(i3 + 1) * 64,
                              q3 * 512 + j3 * 129: q3 * 512 + j3 * 129 + 128]
                    for n2 in range(2):
                        mm = t.matmul(psP[n2][:, :], kv,
                                      qmb[i3 * 64:(i3 + 1) * 64,
                                          (h // 2) * T + n2 * TH:(h // 2) * T + (n2 + 1) * TH],
                                      start=True, stop=True)
                    bump(mm, "pe")
                TT(f)
                mark(f"amm{h}")

                def f(v, h=h):
                    wait(v, "pe", marks[f"amm{h}"]["pe"])
                    for n2 in range(2):
                        bump(v.tensor_copy(
                            out=ybuf[:, h * T + n2 * TH: h * T + (n2 + 1) * TH],
                            in_=psP[n2][:, :]), "dv")
                VV(f)
                mark(f"aev{h}")

            wg_update()
            for o in range(DC):
                p = o % 2

                def f(s, o=o, p=p, wg=dict(WG)):
                    if o < 2:
                        wguard(s, wg)
                    if o >= 2:
                        wait(s, "dv", marks[f"ocast{o - 2}"]["dv"])
                    bump(s.dma_start(out=wfa[p][:, :], in_=wot[o]),
                         "lPe" if o % 2 == 0 else "lPo", 16)
                SS(f)
                mark(f"old{o}")

                def f(v, o=o, p=p):
                    lp = "lPe" if o % 2 == 0 else "lPo"
                    wait(v, lp, marks[f"old{o}"][lp])
                    if o >= 2:
                        wait(v, "pe", marks[f"omm{o - 2}"]["pe"])
                    bump(v.tensor_copy(out=wqa[p][:, :], in_=wfa[p][:, :]), "dv")
                VV(f)
                mark(f"ocast{o}")

                def f(t, o=o, p=p):
                    wait(t, "dv", marks[f"ocast{o}"]["dv"])
                    if o == 0:
                        wait(t, "dv", marks["kvev"]["dv"])
                    if o >= 1:
                        wait(t, "dv", marks[f"oep{o - 1}"]["dv"])
                    for n2 in range(2):
                        for dc in range(DC - 1):
                            t.matmul(psK[n2][:, :],
                                     wqa[p][:, dc * 128:(dc + 1) * 128],
                                     ybuf[:, dc * T + n2 * TH: dc * T + n2 * TH + TH],
                                     start=(dc == 0), stop=False)
                        mm = t.matmul(psK[n2][:, :],
                                      wqa[p][:, (DC - 1) * 128:DC * 128],
                                      ybuf[:, (DC - 1) * T + n2 * TH: (DC - 1) * T + n2 * TH + TH],
                                      start=False, stop=True)
                    bump(mm, "pe")
                TT(f)
                mark(f"omm{o}")

                def f(v, o=o, p=p):
                    wait(v, "pe", marks[f"omm{o}"]["pe"])
                    for n2 in range(2):
                        sl = xf[:, o * T + n2 * TH: o * T + (n2 + 1) * TH]
                        bump(v.scalar_tensor_tensor(
                            out=sl, in0=psK[n2][:, :], scalar=bo_s[:, o:o + 1],
                            in1=sl, op0=ALU.add, op1=ALU.add), "dv")
                VV(f)
                mark(f"oep{o}")
        wg_update()
        mark("attdone")

    def emit_output(src):
        with nc.psum_tensor("psT2a", [128, TH], F32) as pst0, \
             nc.psum_tensor("psT2b", [128, TH], F32) as pst1:
            psT2 = [pst0, pst1]
            for m in range(8):
                for qq in range(4):
                    b = qq % 2

                    def f(t, m=m, qq=qq, b=b):
                        if m == 0 and qq == 0:
                            for nm, v in marks[src].items():
                                wait(t, nm, v)
                        if m > 0 or qq >= 2:
                            pk = (f"oev{m}_{qq - 2}" if qq >= 2
                                  else f"oev{m - 1}_{qq + 2}")
                            wait(t, "dv", marks[pk]["dv"])
                        for i in range(4):
                            dc = qq * 4 + i
                            mm = t.transpose(psT2[b][:, i * 128:(i + 1) * 128],
                                             xf[:, dc * T + m * 128: dc * T + (m + 1) * 128],
                                             identf[:])
                        bump(mm, "pe")
                    TT(f)
                    mark(f"otp{m}_{qq}")

                    def f(v, m=m, qq=qq, b=b):
                        wait(v, "pe", marks[f"otp{m}_{qq}"]["pe"])
                        if m > 0:
                            wait(v, "sO", marks[f"ost{m - 1}"]["sO"])
                        bump(v.tensor_copy(out=XOUT[:, qq * 512:(qq + 1) * 512],
                                           in_=psT2[b][:, :]), "dv")
                    VV(f)
                    mark(f"oev{m}_{qq}")

                def f(s, m=m):
                    wait(s, "dv", marks[f"oev{m}_3"]["dv"])
                    bump(s.dma_start(out=out[m * 128:(m + 1) * 128, :], in_=XOUT),
                         "sO", 16)
                SS(f)
                mark(f"ost{m}")

            def f(s):
                wait(s, "sO", cnt["sO"])
            SS(f)

    emit_ln("L1", marks["xf_done"]["dv"])
    emit_ffn(0, "L1")
    if stage >= 2:
        emit_ln("L2", marks["ffn0done"]["dv"])
        emit_attention()
    if stage >= 3:
        emit_ln("L3", marks["attdone"]["dv"])
        emit_ffn(1, "L3")
        emit_output("ffn1done")
    elif stage == 2:
        emit_output("attdone")
    else:
        emit_output("ffn0done")

    st.close()
    return nc


def host_prep(inp):
    c = np.ascontiguousarray
    w1 = np.asarray(inp["L0_ffn_w1"]); w2 = np.asarray(inp["L0_ffn_w2"])
    w3 = np.asarray(inp["L0_ffn_w3"])
    w1t = c(w1.T).reshape(16, 128, 44, 128).transpose(2, 1, 0, 3)
    w2t = c(w2.T).reshape(16, 128, 44, 128).transpose(2, 1, 0, 3)
    w3t = c(w3.T).reshape(44, 128, 16, 128).transpose(2, 1, 0, 3)
    w1t = c(w1t.reshape(44, 128, 2048))
    w2t = c(w2t.reshape(44, 128, 2048))
    w3t = c(w3t.reshape(16, 128, 5632))
    wq = np.asarray(inp["L0_attn_wq"]); wk = np.asarray(inp["L0_attn_wk"])
    wv = np.asarray(inp["L0_attn_wv"]); wo = np.asarray(inp["L0_attn_wo"])
    mk = lambda w: c(c(w.T).reshape(16, 128, 16, 128).transpose(2, 1, 0, 3)
                     .reshape(16, 128, 2048))
    base = {
        "w1t": w1t, "w2t": w2t, "w3t": w3t,
        "wqt": mk(wq), "wkt": mk(wk), "wot": mk(wo),
        "wvt": c(wv.T.reshape(16, 128, 2048)),
        "wmapqt": c(np.asarray(inp["L0_attn_wmap_q"]).T),
        "wmapkt": c(np.asarray(inp["L0_attn_wmap_k"]).T),
        "bqt": c(np.asarray(inp["L0_attn_bq"]).reshape(16, 128).T),
        "bkt": c(np.asarray(inp["L0_attn_bk"]).reshape(16, 128).T),
        "bot": c(np.asarray(inp["L0_attn_bo"]).reshape(16, 128).T),
        "bvr": c(np.asarray(inp["L0_attn_bv"]).reshape(1, 2048)),
    }
    g1 = w1t.reshape(NCORES, 128, GSL)
    g2 = w2t.reshape(NCORES, 128, GSL)
    g3 = w3t.reshape(NCORES, 128, GSL)
    x = np.asarray(inp["L0_attn_hidden_states"]).reshape(8192, 2048)
    maps = []
    for ci in range(NCORES):
        m = dict(base)
        m["xin"] = c(x[ci * T:(ci + 1) * T, :])
        m["gsl"] = c(np.stack([g1[ci], g2[ci], g3[ci]]))
        maps.append(m)
    return maps


def kernel(**inputs):
    maps = host_prep(inputs)
    nc = build(STAGE)
    res = run_bass_kernel_spmd(nc, maps, core_ids=list(range(NCORES)))
    outs = [res.results[i]["out"] for i in range(NCORES)]
    return np.concatenate(outs, 0).reshape(1, 8192, 2048)


if __name__ == "__main__":
    import sys
    stg = int(sys.argv[1]) if len(sys.argv) > 1 else STAGE
    nc = build(stg)
    n = sum(1 for _ in nc.all_instructions())
    print(f"build ok (stage {stg}), instructions: {n}")


# version 16
# speedup vs baseline: 1.2229x; 1.0247x over previous
import contextlib
import numpy as np

import concourse.bass as bass
import concourse.mybir as mybir
from concourse.bass_utils import run_bass_kernel_spmd

F32 = mybir.dt.float32
BF16 = mybir.dt.bfloat16
AF = mybir.ActivationFunctionType
ALU = mybir.AluOpType

NCORES = 8
T = 1024
TH = 512
D = 2048
DC = 16
HID = 5632
HJ = 44
NH = 16
MM = 64
NW = float(HID * D)
GSL = HJ * 128 * 2048 // NCORES // 128

STAGE = 3


def build(stage=STAGE):
    nc = bass.Bass("TRN2", target_bir_lowering=False,
                   dynamic_dma_scratch_size=2048)

    dt_in = lambda n, s: nc.dram_tensor(n, s, F32, kind="ExternalInput").ap()
    xin = dt_in("xin", [T, D])
    w1t = dt_in("w1t", [HJ, 128, 2048])
    w2t = dt_in("w2t", [HJ, 128, 2048])
    w3t = dt_in("w3t", [DC, 128, HID])
    gsl = dt_in("gsl", [3, 128, GSL])
    wqt = dt_in("wqt", [NH, 128, 2048])
    wkt = dt_in("wkt", [NH, 128, 2048])
    wvt = dt_in("wvt", [DC, 128, 2048])
    wot = dt_in("wot", [DC, 128, 2048])
    wmapqt = dt_in("wmapqt", [128, MM])
    wmapkt = dt_in("wmapkt", [128, MM])
    bqt = dt_in("bqt", [128, NH])
    bkt = dt_in("bkt", [128, NH])
    bvr = dt_in("bvr", [1, D])
    bot = dt_in("bot", [128, NH])
    out = nc.dram_tensor("out", [T, D], F32, kind="ExternalOutput").ap()

    w1q = nc.dram_tensor("w1q", [HJ, 128, 2048], BF16).ap()
    w2q = nc.dram_tensor("w2q", [HJ, 128, 2048], BF16).ap()
    w3q = nc.dram_tensor("w3q", [DC, 128, HID], BF16).ap()
    wvb_d = nc.dram_tensor("wvb_d", [DC, 128, 2048], BF16).ap()
    cc_gin = nc.dram_tensor("cc_gin", [1, 4], F32)
    cc_gout = nc.dram_tensor("cc_gout", [1, 4], F32)
    cc_kin = nc.dram_tensor("cc_kin", [128, 1536], F32)
    cc_kout = nc.dram_tensor("cc_kout", [128, 1536], F32)
    nrm_d = nc.dram_tensor("nrm_d", [NH, T], F32)

    st = contextlib.ExitStack()
    sb = lambda name, shape, dt: st.enter_context(nc.sbuf_tensor(name, shape, dt))

    xf = sb("xf", [128, DC * T], F32)
    ybuf = sb("ybuf", [128, DC * T], BF16)
    wfa = [sb("wfa0", [128, 2048], F32), sb("wfa1", [128, 2048], F32)]
    wqa = [sb("wqa0", [128, 2048], BF16), sb("wqa1", [128, 2048], BF16)]
    AR = 18960
    arena = sb("arena", [128, AR], F32)
    x2t = [sb("x2t0", [128, TH], BF16), sb("x2t1", [128, TH], BF16)]
    sqt = sb("sqt", [128, TH], BF16)
    xbt = [sb("xbt0", [128, TH], BF16), sb("xbt1", [128, TH], BF16)]
    nrm = sb("nrm", [NH, T], F32)
    onesb_c = sb("onesb_c", [128, 1], BF16)
    onesb_r = sb("onesb_r", [1, 128], BF16)
    onesf_r = sb("onesf_r", [1, 128], F32)
    onesf_c = sb("onesf_c", [128, 1], F32)
    thrv = sb("thrv", [128, 8], F32)
    gpart = sb("gpart", [128, 2], F32)
    gprow = sb("gprow", [1, 4], F32)
    identf = sb("identf", [128, 128], F32)
    identb = sb("identb", [128, 128], BF16)
    wmapq_b = sb("wmapq_b", [128, MM], BF16)
    wmapk_b = sb("wmapk_b", [128, MM], BF16)
    wmap_f = sb("wmap_f", [128, MM], F32)
    bq_s = sb("bq_s", [128, NH], F32)
    bk_s = sb("bk_s", [128, NH], F32)
    bo_s = sb("bo_s", [128, NH], F32)
    negone = sb("negone", [128, 1], F32)
    halfc = sb("halfc", [128, 1], F32)
    qmb = sb("qmb", [128, (NH // 2) * T], BF16)
    bv_f = None
    bv_b = None

    def cf(c0, n):
        assert c0 + n <= AR
        return arena[:, c0:c0 + n]

    def cb(c0, nb):
        assert c0 + nb // 2 <= AR
        return arena[:, c0:c0 + nb // 2].bitcast(BF16)

    GH = cb(0, HJ * TH)
    SQW = [cb(11264, 2048), cb(12288, 2048)]
    G2B = [cf(13376, 2048), cf(15424, 2048)]
    W3R = [cb(13376, 2816), cb(14784, 2816)]
    rows = cf(16192, 2048)
    W3I = [cf(11264, 1408), cf(16192, 1408), qmb[:, 0:2816].bitcast(F32)]
    W3Rq = [W3R[0][:, 0:1408], W3R[1][:, 0:1408]]
    SQB3 = [W3R[0][:, 1408:2816], W3R[1][:, 1408:2816]]
    XB = cb(0, 16 * TH)
    SQ = cb(4096, 16 * TH)
    XTOK = [cf(c, 2048) for c in (0, 2048, 4096, 6144)]
    GAMS = cf(8192, GSL // 2)
    XOUT = cf(0, 2048)
    VBUF = cb(0, 2 * NH * 129)
    KMB = cb(2064, NH * 512)
    KTMP = [cb(6160, T), cb(6672, T)]
    WVS = cb(7184, DC * 256)
    QTMP = [cb(9232, T), cb(9744, T)]
    ELU = [cf(10256, T)[0:64, :], cf(11280, T)[0:64, :]]
    kvga = cf(12304, 1536)
    kvgb = cb(13840, 1536)
    RNBT = cb(15376, T)
    RNB2 = [RNBT[0:64, :], RNBT[64:128, :]]
    WVF = [cf(15888, 1024), cf(16912, 1024)]
    WVO = [cb(17936, 1024), cb(18448, 1024)]

    bv_f = wfa[0][0:1, 0:D]
    bv_b = nrm[0:1, :].bitcast(BF16)

    sems = {}
    for nm in ("lA", "lX", "lWe", "lWo", "lBe", "lBo", "lPe", "lPo",
               "lV", "lN", "sO", "lG", "l3e", "l3o", "lC",
               "sWe", "sWo", "sW2", "sW3", "sQe", "sQo", "sVe", "sVo",
               "g", "cc", "pe", "ac", "dv", "pl", "gc"):
        sems[nm] = st.enter_context(nc.semaphore("s_" + nm))
    cnt = {k: 0 for k in sems}

    def bump(inst, nm, k=1):
        inst.then_inc(sems[nm], k)
        cnt[nm] += k
        return cnt[nm]

    def wait(eng, nm, v=None):
        eng.wait_ge(sems[nm], cnt[nm] if v is None else v)

    block = st.enter_context(nc.Block())
    TT = block.tensor
    VV = block.vector
    AA = block.scalar
    GG = block.gpsimd
    SS = block.sync

    marks = {}

    def mark(key):
        marks[key] = dict(cnt)

    WG = {"dv": 0, "pe": 0, "ac": 0, "sWe": 0, "sWo": 0, "sW2": 0, "sW3": 0}

    def wguard(eng, wg):
        for nm, v in wg.items():
            if v:
                eng.wait_ge(sems[nm], v)

    def wg_update():
        for nm in WG:
            WG[nm] = cnt[nm]

    def f(g):
        for tile, val in ((identf, 0.0), (identb, 0.0), (onesb_c, 1.0),
                          (onesb_r, 1.0), (onesf_r, 1.0), (onesf_c, 1.0),
                          (negone, -1.0), (halfc, 0.5)):
            bump(g.memset(tile[:], val), "pl")
        wait(g, "pl")
        bump(g.affine_select(out=identf[:], in_=identf[:], compare_op=ALU.not_equal,
                             fill=1.0, base=0, pattern=[[-1, 128]],
                             channel_multiplier=1), "pl")
        bump(g.affine_select(out=identb[:], in_=identb[:], compare_op=ALU.not_equal,
                             fill=1.0, base=0, pattern=[[-1, 128]],
                             channel_multiplier=1), "pl")
    GG(f)
    mark("consts")

    def f(s):
        for dst, src in ((bq_s[:], bqt), (bk_s[:], bkt), (bo_s[:], bot),
                         (bv_f[:], bvr), (wmap_f[:], wmapqt)):
            bump(s.dma_start(out=dst, in_=src), "lA", 16)
    SS(f)
    mark("smallld")

    def f(a):
        wait(a, "lA", marks["smallld"]["lA"])
        bump(a.copy(out=wmapq_b[:], in_=wmap_f[:]), "ac")
        bump(a.copy(out=bv_b[:], in_=bv_f[:]), "ac")
    AA(f)
    mark("wmapq_cast")

    def f(s):
        wait(s, "ac", marks["wmapq_cast"]["ac"])
        bump(s.dma_start(out=wmap_f[:], in_=wmapkt), "lA", 16)
    SS(f)
    mark("wmapk_ld")

    def f(a):
        wait(a, "lA", marks["wmapk_ld"]["lA"])
        bump(a.copy(out=wmapk_b[:], in_=wmap_f[:]), "ac")
    AA(f)
    mark("wmapk_cast")

    with nc.psum_tensor("ps_g", [1, 4], F32) as ps_g, \
         nc.psum_tensor("ps_thr", [128, 8], F32) as ps_thr, \
         nc.psum_tensor("ps_xt0", [128, 512], F32) as ps_xt0, \
         nc.psum_tensor("ps_xt1", [128, 512], F32) as ps_xt1:
        ps_xt = [ps_xt0, ps_xt1]

        GH2 = GSL // 2
        for w in range(3):
            for hf in range(2):
                def f(a, w=w, hf=hf):
                    if w + hf > 0:
                        pw, ph = (w, hf - 1) if hf else (w - 1, 1)
                        wait(a, "dv", marks[f"gred{pw}_{ph}"]["dv"])
                    bump(a.dma_start(out=GAMS, in_=gsl[w][:, hf * GH2:(hf + 1) * GH2]),
                         "lG", 16)
                AA(f)
                mark(f"gld{w}_{hf}")

                def f(v, w=w, hf=hf):
                    wait(v, "lG", marks[f"gld{w}_{hf}"]["lG"])
                    if hf == 0 and w > 0:
                        wait(v, "pe", marks[f"gtot{w - 1}"]["pe"])
                    bump(v.reduce_sum(out=gpart[:, hf:hf + 1], in_=GAMS,
                                      axis=mybir.AxisListType.X,
                                      apply_absolute_value=True), "dv")
                VV(f)
                mark(f"gred{w}_{hf}")

            def f(t, w=w):
                wait(t, "dv", marks[f"gred{w}_1"]["dv"])
                if w == 0:
                    wait(t, "pl", marks["consts"]["pl"])
                bump(t.matmul(ps_g[0:1, w:w + 1], gpart[:, 0:1], onesf_c[:],
                              start=True, stop=False), "pe")
                bump(t.matmul(ps_g[0:1, w:w + 1], gpart[:, 1:2], onesf_c[:],
                              start=False, stop=True), "pe")
            TT(f)
            mark(f"gtot{w}")

        def f(v):
            bump(v.memset(gprow[:], 0.0), "dv")
            wait(v, "pe", marks["gtot2"]["pe"])
            wait(v, "dv")
            bump(v.tensor_copy(out=gprow[0:1, 0:3], in_=ps_g[0:1, 0:3]), "dv")
        VV(f)
        mark("gpack")

        def f(g):
            wait(g, "dv", marks["gpack"]["dv"])
            bump(g.dma_start(out=cc_gin.ap(), in_=gprow[:]), "g", 16)
            wait(g, "g")
            bump(g.collective_compute(
                "AllReduce", ALU.add, replica_groups=[list(range(NCORES))],
                ins=[cc_gin.ap().opt()], outs=[cc_gout.ap().opt()]), "cc", 1)
            wait(g, "cc")
            bump(g.dma_start(out=gprow[:], in_=cc_gout.ap()), "g", 16)
        GG(f)
        mark("gcc")

        def f(v):
            wait(v, "g", marks["gcc"]["g"])
            bump(v.tensor_scalar(out=rows[0:1, 0:4], in0=gprow[:],
                                 scalar1=0.5 / NW, scalar2=0.5e-8,
                                 op0=ALU.mult, op1=ALU.add), "dv")
            bump(v.tensor_scalar(out=rows[0:1, 4:8], in0=gprow[:],
                                 scalar1=-0.5 / NW, scalar2=-0.5e-8,
                                 op0=ALU.mult, op1=ALU.add), "dv")
        VV(f)
        mark("throw")

        def f(t):
            wait(t, "dv", marks["throw"]["dv"])
            bump(t.matmul(ps_thr[:, :], onesf_r[:], rows[0:1, 0:8],
                          start=True, stop=True), "pe")
        TT(f)
        mark("thrbc")

        def f(v):
            wait(v, "pe", marks["thrbc"]["pe"])
            bump(v.tensor_copy(out=thrv[:], in_=ps_thr[:, :]), "dv")
        VV(f)
        mark("thr")
        THR_DV = cnt["dv"]

        for grp in range(2):
            def f(s, grp=grp):
                if grp == 1:
                    wait(s, "pe", marks["xt0"]["pe"])
                for i in range(4):
                    m = grp * 4 + i
                    bump(s.dma_start(out=XTOK[i],
                                     in_=xin[m * 128:(m + 1) * 128, :]), "lX", 16)
            SS(f)
            mark(f"xld{grp}")

            for dc in range(DC):
                b = dc % 2

                def f(t, grp=grp, dc=dc, b=b):
                    if dc == 0:
                        wait(t, "lX", marks[f"xld{grp}"]["lX"])
                        if grp == 0:
                            wait(t, "pl", marks["consts"]["pl"])
                    if dc >= 2 or grp == 1:
                        pk = f"xtev{grp}_{dc - 2}" if dc >= 2 else f"xtev0_{DC - 2 + dc}"
                        wait(t, "dv", marks[pk]["dv"])
                    for i in range(3):
                        t.transpose(ps_xt[b][:, i * 128:(i + 1) * 128],
                                    XTOK[i][:, dc * 128:(dc + 1) * 128], identf[:])
                    bump(t.transpose(ps_xt[b][:, 3 * 128:4 * 128],
                                     XTOK[3][:, dc * 128:(dc + 1) * 128],
                                     identf[:]), "pe")
                TT(f)
                mark(f"xtp{grp}_{dc}")

                def f(v, grp=grp, dc=dc, b=b):
                    wait(v, "pe", marks[f"xtp{grp}_{dc}"]["pe"])
                    bump(v.tensor_copy(
                        out=xf[:, dc * T + grp * 512: dc * T + grp * 512 + 512],
                        in_=ps_xt[b][:, :]), "dv")
                VV(f)
                mark(f"xtev{grp}_{dc}")
            mark(f"xt{grp}")
    mark("xf_done")

    def emit_ln(tag, src_dv):
        with nc.psum_tensor("psS_" + tag, [1, TH], F32) as psS, \
             nc.psum_tensor("psQ_" + tag, [1, TH], F32) as psQ, \
             nc.psum_tensor("psB1_" + tag, [128, TH], F32) as psB1, \
             nc.psum_tensor("psB2_" + tag, [128, TH], F32) as psB2:
            for h2 in range(2):
                off = h2 * TH

                def f(v, off=off):
                    if off == 0:
                        wait(v, "dv", src_dv)
                    else:
                        wait(v, "pe", marks[f"{tag}mq_0"]["pe"])
                    for dc in range(DC):
                        bump(v.tensor_copy(out=XB[:, dc * TH:(dc + 1) * TH],
                                           in_=xf[:, dc * T + off: dc * T + off + TH]),
                             "dv")
                VV(f)
                mark(f"{tag}xb_{off}")

                def f(a, off=off):
                    wait(a, "dv", marks[f"{tag}xb_{off}"]["dv"])
                    bump(a.activation(out=SQ[:, :], in_=XB[:, :],
                                      func=AF.Square), "ac")
                AA(f)
                mark(f"{tag}sq_{off}")

                def f(t, off=off):
                    wait(t, "ac", marks[f"{tag}sq_{off}"]["ac"])
                    if off:
                        wait(t, "dv", marks[f"{tag}var_0"]["dv"])
                    for dc in range(DC):
                        t.matmul(psS[0:1, :], onesb_c[:], XB[:, dc * TH:(dc + 1) * TH],
                                 start=(dc == 0), stop=(dc == DC - 1))
                    for dc in range(DC - 1):
                        t.matmul(psQ[0:1, :], onesb_c[:], SQ[:, dc * TH:(dc + 1) * TH],
                                 start=(dc == 0), stop=False)
                    bump(t.matmul(psQ[0:1, :], onesb_c[:],
                                  SQ[:, (DC - 1) * TH:DC * TH],
                                  start=False, stop=True), "pe")
                TT(f)
                mark(f"{tag}mq_{off}")

                def f(v, off=off):
                    wait(v, "pe", marks[f"{tag}mq_{off}"]["pe"])
                    r = rows
                    bump(v.tensor_scalar(out=r[0:1, 0:TH], in0=psS[0:1, :],
                                         scalar1=1.0 / D, scalar2=None,
                                         op0=ALU.mult), "dv")
                    bump(v.tensor_scalar(out=r[0:1, TH:2 * TH], in0=psQ[0:1, :],
                                         scalar1=1.0 / D, scalar2=None,
                                         op0=ALU.mult), "dv")
                    wait(v, "dv")
                    bump(v.tensor_tensor(out=r[0:1, 2 * TH:3 * TH], in0=r[0:1, 0:TH],
                                         in1=r[0:1, 0:TH], op=ALU.mult), "dv")
                    wait(v, "dv")
                    bump(v.tensor_scalar(out=r[0:1, 2 * TH:3 * TH],
                                         in0=r[0:1, 2 * TH:3 * TH],
                                         scalar1=-1.0, scalar2=1e-5,
                                         op0=ALU.mult, op1=ALU.add), "dv")
                    wait(v, "dv")
                    bump(v.tensor_tensor(out=r[0:1, 2 * TH:3 * TH],
                                         in0=r[0:1, TH:2 * TH],
                                         in1=r[0:1, 2 * TH:3 * TH],
                                         op=ALU.add), "dv")
                VV(f)
                mark(f"{tag}var_{off}")

                def f(a, off=off):
                    wait(a, "dv", marks[f"{tag}var_{off}"]["dv"])
                    bump(a.activation(out=rows[0:1, 3 * TH:4 * TH],
                                      in_=rows[0:1, 2 * TH:3 * TH],
                                      func=AF.Sqrt), "ac")
                AA(f)
                mark(f"{tag}sd_{off}")

                def f(v, off=off):
                    wait(v, "ac", marks[f"{tag}sd_{off}"]["ac"])
                    r = rows
                    bump(v.reciprocal(out=r[0:1, TH:2 * TH],
                                      in_=r[0:1, 3 * TH:4 * TH]), "dv")
                    wait(v, "dv")
                    bump(v.tensor_tensor(out=r[0:1, 2 * TH:3 * TH], in0=r[0:1, 0:TH],
                                         in1=r[0:1, TH:2 * TH],
                                         op=ALU.mult), "dv")
                    wait(v, "dv")
                    bump(v.tensor_scalar(out=r[0:1, 2 * TH:3 * TH],
                                         in0=r[0:1, 2 * TH:3 * TH],
                                         scalar1=-1.0, scalar2=None,
                                         op0=ALU.mult), "dv")
                VV(f)
                mark(f"{tag}rs_{off}")

                def f(t, off=off):
                    wait(t, "dv", marks[f"{tag}rs_{off}"]["dv"])
                    bump(t.matmul(psB1[:, :], onesf_r[:], rows[0:1, TH:2 * TH],
                                  start=True, stop=True), "pe")
                    bump(t.matmul(psB2[:, :], onesf_r[:], rows[0:1, 2 * TH:3 * TH],
                                  start=True, stop=True), "pe")
                TT(f)
                mark(f"{tag}bc_{off}")

                for dc in range(DC):
                    def f(v, dc=dc, off=off):
                        if dc == 0:
                            wait(v, "pe", marks[f"{tag}bc_{off}"]["pe"])
                        else:
                            wait(v, "dv")
                        bump(v.tensor_tensor(out=G2B[0][:, 0:TH],
                                             in0=xf[:, dc * T + off: dc * T + off + TH],
                                             in1=psB1[:, :], op=ALU.mult), "dv")
                        wait(v, "dv")
                        bump(v.tensor_tensor(
                            out=ybuf[:, dc * T + off: dc * T + off + TH],
                            in0=G2B[0][:, 0:TH], in1=psB2[:, :], op=ALU.add), "dv")
                    VV(f)
                    mark(f"{tag}y{dc}_{off}")
        mark(f"{tag}done")

    def emit_ffn(call, lntag):
        with nc.psum_tensor(f"psX{call}a", [128, TH], F32) as psxa, \
             nc.psum_tensor(f"psX{call}b", [128, TH], F32) as psxb, \
             nc.psum_tensor(f"psO{call}a", [128, TH], F32) as psoa, \
             nc.psum_tensor(f"psO{call}b", [128, TH], F32) as psob:
            psX = [psxa, psxb]
            psO = [psoa, psob]
            guard = {"psX": None, "psO": None, "GH": None, "W3R": None}

            def sweep(sw, wsrc, wscr, h2):
                off = h2 * TH
                quant = (call == 0 and h2 == 0)
                key = f"F{call}s{sw}h{h2}"
                widx = 0 if sw == 1 else 1
                if quant:
                    wfaH = [wfa[s // 2][:, (s % 2) * 1024:(s % 2 + 1) * 1024]
                            for s in range(4)]
                    wqaH = [wqa[s // 2][:, (s % 2) * 1024:(s % 2 + 1) * 1024]
                            for s in range(4)]
                    SQWH = [SQW[s // 2][:, (s % 2) * 1024:(s % 2 + 1) * 1024]
                            for s in range(4)]
                    SLS = ("sWe", "sWo", "sW2", "sW3")

                    def odd_load(a, jh, sw, wg):
                        j, hf = jh // 2, jh % 2
                        sl = jh % 4
                        if jh < 4:
                            wguard(a, wg)
                            if sw == 2:
                                wait(a, "ac", marks["F0s1h0end"]["ac"])
                        bump(a.dma_start(
                            out=wfaH[sl],
                            in_=wsrc[j][:, hf * 1024:(hf + 1) * 1024]),
                            "lWo", 16)
                        marks[f"{key}l{jh}"] = dict(cnt)

                    for jh2 in range(0, 2 * HJ, 2):
                        jh = jh2
                        j, hf = jh // 2, jh % 2
                        sl = jh % 4

                        def f(s, jh=jh, j=j, hf=hf, sl=sl, sw=sw,
                              wg=dict(WG)):
                            if jh < 4:
                                wguard(s, wg)
                                if sw == 2:
                                    wait(s, "ac", marks["F0s1h0end"]["ac"])
                            else:
                                wait(s, "ac", marks[f"{key}s2_{jh - 4}"]["ac"])
                            bump(s.dma_start(
                                out=wfaH[sl],
                                in_=wsrc[j][:, hf * 1024:(hf + 1) * 1024]),
                                "lWe", 16)
                        SS(f)
                        mark(f"{key}l{jh}")

                        def f(a, jh2=jh2, sw=sw, widx=widx, key=key,
                              wg=dict(WG)):
                            if jh2 == 0:
                                odd_load(a, 1, sw, wg)
                                odd_load(a, 3, sw, wg)
                            for jh in (jh2, jh2 + 1):
                                sl = jh % 4
                                lw = "lWe" if jh % 2 == 0 else "lWo"
                                sws = SLS[sl]
                                wait(a, lw, marks[f"{key}l{jh}"][lw])
                                wait(a, "dv", THR_DV)
                                if jh >= 4:
                                    wait(a, "pe", marks[f"{key}m{jh - 4}"]["pe"])
                                    wait(a, sws, marks[f"{key}w{jh - 4}"][sws])
                                    wait(a, "dv", marks[f"{key}q{jh - 4}"]["dv"])
                                elif sw == 2:
                                    wait(a, "pe", marks["F0s1h0end"]["pe"])
                                    wait(a, sws, marks["F0s1h0end"][sws])
                                    wait(a, "dv", marks["F0s1h0end"]["dv"])
                                bump(a.activation(
                                    out=wqaH[sl][:, :], in_=wfaH[sl][:, :],
                                    func=AF.Sign,
                                    bias=thrv[:, 4 + widx:5 + widx]), "ac")
                                wait(a, "ac")
                                bump(a.activation(
                                    out=SQWH[sl][:, :], in_=wfaH[sl][:, :],
                                    func=AF.Sign,
                                    bias=thrv[:, widx:widx + 1]), "ac")
                                marks[f"{key}s2_{jh}"] = dict(cnt)
                            if jh2 + 5 < 2 * HJ:
                                odd_load(a, jh2 + 5, sw, wg)
                        AA(f)

                        for jh in (jh2, jh2 + 1):
                            j, hf = jh // 2, jh % 2
                            sl = jh % 4
                            sws = SLS[sl]

                            def f(v, jh=jh, sl=sl):
                                wait(v, "ac", marks[f"{key}s2_{jh}"]["ac"])
                                bump(v.tensor_tensor(
                                    out=wqaH[sl][:, :], in0=wqaH[sl][:, :],
                                    in1=SQWH[sl][:, :], op=ALU.add), "dv")
                            VV(f)
                            mark(f"{key}q{jh}")

                            def f(g, jh=jh, j=j, hf=hf, sl=sl, sws=sws):
                                wait(g, "dv", marks[f"{key}q{jh}"]["dv"])
                                bump(g.dma_start(
                                    out=wscr[j][:, hf * 1024:(hf + 1) * 1024],
                                    in_=wqaH[sl][:, :]), sws, 16)
                            (SS if jh % 2 == 0 else AA)(f)
                            mark(f"{key}w{jh}")

                            def f(t, jh=jh, j=j, hf=hf, sl=sl):
                                wait(t, "dv", marks[f"{key}q{jh}"]["dv"])
                                if jh == 0:
                                    wait(t, "dv", marks[f"{lntag}done"]["dv"])
                                    if guard["psX"] is not None:
                                        for nm, v in guard["psX"].items():
                                            wait(t, nm, v)
                                if hf == 0 and j >= 2:
                                    wait(t, "dv", marks[f"{key}c{j - 2}"]["dv"])
                                pj = j % 2
                                for di in range(8):
                                    dc = hf * 8 + di
                                    mm = t.matmul(
                                        psX[pj][:, :],
                                        wqaH[sl][:, di * 128:(di + 1) * 128],
                                        ybuf[:, dc * T + off: dc * T + off + TH],
                                        start=(dc == 0), stop=(dc == DC - 1))
                                bump(mm, "pe")
                            TT(f)
                            mark(f"{key}m{jh}")

                            if hf == 1:
                                if sw == 1:
                                    def f(v, j=j, jh=jh):
                                        wait(v, "pe", marks[f"{key}m{jh}"]["pe"])
                                        if j == 0 and guard["GH"] is not None:
                                            wait(v, "pe", guard["GH"])
                                        bump(v.tensor_copy(
                                            out=GH[:, j * TH:(j + 1) * TH],
                                            in_=psX[j % 2][:, :]), "dv")
                                    VV(f)
                                    mark(f"{key}c{j}")
                                else:
                                    def f(v, j=j, jh=jh):
                                        wait(v, "pe", marks[f"{key}m{jh}"]["pe"])
                                        if j == 0:
                                            wait(v, "ac",
                                                 marks[f"sig{call}_{h2}"]["ac"])
                                        if j >= 2:
                                            wait(v, "dv", marks[f"{key}h{j - 2}"]["dv"])
                                        bump(v.tensor_copy(out=x2t[j % 2][:, :],
                                                           in_=psX[j % 2][:, :]), "dv")
                                        wait(v, "dv")
                                        bump(v.tensor_tensor(
                                            out=GH[:, j * TH:(j + 1) * TH],
                                            in0=GH[:, j * TH:(j + 1) * TH],
                                            in1=x2t[j % 2][:, :],
                                            op=ALU.mult), "dv")
                                    VV(f)
                                    mark(f"{key}c{j}")
                                    marks[f"{key}h{j}"] = dict(cnt)
                    mark(f"{key}end")
                    guard["psX"] = {"dv": cnt["dv"]}
                    wg_update()
                    return
                for j in range(HJ):
                    p = j % 2
                    if True:
                        def f(s, j=j, p=p, sw=sw, wg=dict(WG)):
                            if j < 2:
                                wguard(s, wg)
                                if call == 0 and h2 == 1 and j == 0:
                                    kh = f"F0s{sw}h0"
                                    for snm in ("sWe", "sWo", "sW2", "sW3"):
                                        wait(s, snm, marks[f"{kh}end"][snm])
                            if j >= 2:
                                wait(s, "pe", marks[f"{key}m{j - 2}"]["pe"])
                            bump(s.dma_start(out=wqa[p][:, :], in_=wscr[j]),
                                 "lWe" if j % 2 == 0 else "lWo", 16)
                        SS(f)
                        mark(f"{key}l{j}")

                    def f(t, j=j, p=p):
                        if quant:
                            wait(t, "dv", marks[f"{key}q{j}"]["dv"])
                        else:
                            lw = "lWe" if j % 2 == 0 else "lWo"
                            wait(t, lw, marks[f"{key}l{j}"][lw])
                        if j == 0:
                            wait(t, "dv", marks[f"{lntag}done"]["dv"])
                            if guard["psX"] is not None:
                                for nm, v in guard["psX"].items():
                                    wait(t, nm, v)
                        if j >= 2:
                            wait(t, "dv", marks[f"{key}c{j - 2}"]["dv"])
                        for dc in range(DC - 1):
                            t.matmul(psX[p][:, :],
                                     wqa[p][:, dc * 128:(dc + 1) * 128],
                                     ybuf[:, dc * T + off: dc * T + off + TH],
                                     start=(dc == 0), stop=False)
                        bump(t.matmul(psX[p][:, :],
                                      wqa[p][:, (DC - 1) * 128:DC * 128],
                                      ybuf[:, (DC - 1) * T + off: (DC - 1) * T + off + TH],
                                      start=False, stop=True), "pe")
                    TT(f)
                    mark(f"{key}m{j}")

                    if sw == 1:
                        def f(v, j=j, p=p):
                            wait(v, "pe", marks[f"{key}m{j}"]["pe"])
                            if j == 0 and guard["GH"] is not None:
                                wait(v, "pe", guard["GH"])
                            bump(v.tensor_copy(out=GH[:, j * TH:(j + 1) * TH],
                                               in_=psX[p][:, :]), "dv")
                        VV(f)
                        mark(f"{key}c{j}")
                    else:
                        def f(v, j=j, p=p):
                            wait(v, "pe", marks[f"{key}m{j}"]["pe"])
                            if j == 0:
                                wait(v, "ac", marks[f"sig{call}_{h2}"]["ac"])
                            if j >= 2:
                                wait(v, "dv", marks[f"{key}h{j - 2}"]["dv"])
                            bump(v.tensor_copy(out=x2t[p][:, :],
                                               in_=psX[p][:, :]), "dv")
                            wait(v, "dv")
                            bump(v.tensor_tensor(out=GH[:, j * TH:(j + 1) * TH],
                                                 in0=GH[:, j * TH:(j + 1) * TH],
                                                 in1=x2t[p][:, :],
                                                 op=ALU.mult), "dv")
                        VV(f)
                        mark(f"{key}c{j}")
                        marks[f"{key}h{j}"] = dict(cnt)
                mark(f"{key}end")
                guard["psX"] = {"dv": cnt["dv"]}
                wg_update()

            def sig_batch(h2):
                def f(a, h2=h2):
                    wait(a, "dv", marks[f"F{call}s1h{h2}c{HJ - 1}"]["dv"])
                    bump(a.activation(out=GH[:, :], in_=GH[:, :],
                                      func=AF.Sigmoid,
                                      scale=halfc[:, 0:1]), "ac")
                AA(f)
                mark(f"sig{call}_{h2}")

            def phase_b(h2):
                off = h2 * TH
                key = f"F{call}B{h2}"
                qmode = (call == 0 and h2 == 0)
                if qmode:
                    QB = HJ // 4

                    def odd_loadb(a, bidx):
                        o_, qq_ = divmod(bidx, 4)
                        sin = bidx % 3
                        if bidx < 3:
                            wait(a, "dv", marks["F0s2h0end"]["dv"])
                        bump(a.dma_start(
                            out=W3I[sin],
                            in_=w3t[o_][:, qq_ * 1408:(qq_ + 1) * 1408]),
                            "lBo", 16)
                        marks[f"{key}l{bidx}"] = dict(cnt)

                    for o in range(DC):
                        for qq2 in range(0, 4, 2):
                          if True:
                            bidx = o * 4 + qq2
                            sin = bidx % 3

                            def f(s, o=o, qq=qq2, bidx=bidx, sin=sin):
                                if bidx < 3:
                                    wait(s, "dv", marks["F0s2h0end"]["dv"])
                                else:
                                    wait(s, "ac",
                                         marks[f"{key}s2_{bidx - 3}"]["ac"])
                                bump(s.dma_start(
                                    out=W3I[sin],
                                    in_=w3t[o][:, qq * 1408:(qq + 1) * 1408]),
                                    "lBe", 16)
                            SS(f)
                            mark(f"{key}l{bidx}")

                          def f(a, o=o, qq2=qq2, key=key):
                            if o == 0 and qq2 == 0:
                                odd_loadb(a, 1)
                            for qq in (qq2, qq2 + 1):
                                bidx = o * 4 + qq
                                p = bidx % 2
                                sin = bidx % 3
                                lb = "lBe" if bidx % 2 == 0 else "lBo"
                                wait(a, lb, marks[f"{key}l{bidx}"][lb])
                                sqp = "sQe" if bidx % 2 == 0 else "sQo"
                                if bidx >= 2:
                                    wait(a, "pe", marks[f"{key}m{bidx - 2}"]["pe"])
                                    wait(a, sqp, marks[f"{key}w{bidx - 2}"][sqp])
                                    wait(a, "dv", marks[f"{key}q{bidx - 2}"]["dv"])
                                bump(a.activation(
                                    out=W3Rq[p][:, :], in_=W3I[sin],
                                    func=AF.Sign, bias=thrv[:, 6:7]), "ac")
                                wait(a, "ac")
                                bump(a.activation(
                                    out=SQB3[p][:, :], in_=W3I[sin],
                                    func=AF.Sign, bias=thrv[:, 2:3]), "ac")
                                marks[f"{key}s2_{bidx}"] = dict(cnt)
                            if o * 4 + qq2 + 3 < 4 * DC:
                                odd_loadb(a, o * 4 + qq2 + 3)
                          AA(f)

                          for qq in (qq2, qq2 + 1):
                            bidx = o * 4 + qq
                            p = bidx % 2

                            def f(v, bidx=bidx, p=p):
                                wait(v, "ac", marks[f"{key}s2_{bidx}"]["ac"])
                                bump(v.tensor_tensor(
                                    out=W3Rq[p][:, :], in0=W3Rq[p][:, :],
                                    in1=SQB3[p][:, :], op=ALU.add), "dv")
                            VV(f)
                            mark(f"{key}q{bidx}")

                            def f(g, o=o, qq=qq, bidx=bidx, p=p):
                                sqp = "sQe" if bidx % 2 == 0 else "sQo"
                                wait(g, "dv", marks[f"{key}q{bidx}"]["dv"])
                                bump(g.dma_start(
                                    out=w3q[o][:, qq * 1408:(qq + 1) * 1408],
                                    in_=W3Rq[p][:, :]), sqp, 16)
                            (SS if bidx % 2 == 0 else AA)(f)
                            mark(f"{key}w{bidx}")

                            def f(t, o=o, qq=qq, bidx=bidx, p=p):
                                wait(t, "dv", marks[f"{key}q{bidx}"]["dv"])
                                if bidx == 0:
                                    wait(t, "dv",
                                         marks[f"F{call}s2h{h2}h{HJ - 1}"]["dv"])
                                    if guard["psO"] is not None:
                                        for nm, v in guard["psO"].items():
                                            wait(t, nm, v)
                                if bidx >= 8 and qq == 0:
                                    wait(t, "dv", marks[f"{key}e{o - 2}"]["dv"])
                                for j in range(QB):
                                    jj = qq * QB + j
                                    mm = t.matmul(
                                        psO[o % 2][:, :],
                                        W3Rq[p][:, j * 128:(j + 1) * 128],
                                        GH[:, jj * TH:(jj + 1) * TH],
                                        start=(jj == 0), stop=(jj == HJ - 1))
                                bump(mm, "pe")
                            TT(f)
                            mark(f"{key}m{bidx}")

                        def f(v, o=o):
                            wait(v, "pe", marks[f"{key}m{o * 4 + 3}"]["pe"])
                            sl = xf[:, o * T + off: o * T + off + TH]
                            bump(v.scalar_tensor_tensor(
                                out=sl, in0=psO[o % 2][:, :], scalar=0.125,
                                in1=sl, op0=ALU.mult, op1=ALU.add), "dv")
                        VV(f)
                        mark(f"{key}e{o}")
                    mark(f"{key}end")
                    guard["psO"] = {"dv": cnt["dv"]}
                    guard["W3R"] = cnt["pe"]
                    guard["GH"] = cnt["pe"]
                    return
                HB = HJ // 2
                for o in range(DC):
                    for hb in range(2):
                        bidx = o * 2 + hb
                        p = bidx % 2

                        def f(s, o=o, hb=hb, bidx=bidx, p=p):
                            if bidx < 2:
                                if call == 0 and h2 == 1:
                                    wait(s, "sQe", marks["F0B0end"]["sQe"])
                                    wait(s, "sQo", marks["F0B0end"]["sQo"])
                                if guard["W3R"] is not None:
                                    wait(s, "pe", guard["W3R"])
                            if bidx >= 2:
                                wait(s, "pe", marks[f"{key}m{bidx - 2}"]["pe"])
                            bump(s.dma_start(
                                out=W3R[p],
                                in_=w3q[o][:, hb * 2816:(hb + 1) * 2816]),
                                "lBe" if bidx % 2 == 0 else "lBo", 16)
                        SS(f)
                        mark(f"{key}l{bidx}")

                        def f(t, o=o, hb=hb, bidx=bidx, p=p):
                            lb = "lBe" if bidx % 2 == 0 else "lBo"
                            wait(t, lb, marks[f"{key}l{bidx}"][lb])
                            if bidx == 0:
                                wait(t, "dv", marks[f"F{call}s2h{h2}h{HJ - 1}"]["dv"])
                                if guard["psO"] is not None:
                                    for nm, v in guard["psO"].items():
                                        wait(t, nm, v)
                            if bidx >= 4 and hb == 0:
                                wait(t, "dv", marks[f"{key}e{o - 2}"]["dv"])
                            for j in range(HB):
                                jj = hb * HB + j
                                mm = t.matmul(psO[o % 2][:, :],
                                              W3R[p][:, j * 128:(j + 1) * 128],
                                              GH[:, jj * TH:(jj + 1) * TH],
                                              start=(jj == 0), stop=(jj == HJ - 1))
                            bump(mm, "pe")
                        TT(f)
                        mark(f"{key}m{bidx}")

                    def f(v, o=o):
                        wait(v, "pe", marks[f"{key}m{o * 2 + 1}"]["pe"])
                        sl = xf[:, o * T + off: o * T + off + TH]
                        bump(v.scalar_tensor_tensor(
                            out=sl, in0=psO[o % 2][:, :], scalar=0.125, in1=sl,
                            op0=ALU.mult, op1=ALU.add), "dv")
                    VV(f)
                    mark(f"{key}e{o}")
                mark(f"{key}end")
                guard["psO"] = {"dv": cnt["dv"]}
                guard["W3R"] = cnt["pe"]
                guard["GH"] = cnt["pe"]

            if call == 0:
                sweep(1, w1t, w1q, 0)
                sig_batch(0)
                sweep(2, w2t, w2q, 0)
                phase_b(0)
                sweep(1, None, w1q, 1)
                sig_batch(1)
                sweep(2, None, w2q, 1)
                phase_b(1)
            else:
                for h2 in range(2):
                    sweep(1, None, w1q, h2)
                    sig_batch(h2)
                    sweep(2, None, w2q, h2)
                    phase_b(h2)
        mark(f"ffn{call}done")

    def emit_attention():
        prev = dict(cnt)
        with nc.psum_tensor("psP0", [128, TH], F32) as psp0, \
             nc.psum_tensor("psP1", [128, TH], F32) as psp1, \
             nc.psum_tensor("psM0", [64, TH], F32) as psm0, \
             nc.psum_tensor("psM1", [64, TH], F32) as psm1, \
             nc.psum_tensor("psTq", [128, 512], BF16) as psTq, \
             nc.psum_tensor("psKV0", [128, 129], F32) as pskv0, \
             nc.psum_tensor("psKV1", [128, 129], F32) as pskv1:
            psP = [psp0, psp1]
            psM = [psm0, psm1]
            psK = psP
            psKV = [pskv0, pskv1]

            def emit_wv_ci(ci):
                dc, hhalf = divmod(ci, 2)
                p = ci % 2
                svp = "sVe" if ci % 2 == 0 else "sVo"

                def f(a, ci=ci, dc=dc, hhalf=hhalf, p=p, svp=svp):
                    if ci == 0:
                        wait(a, "dv", marks["L2done"]["dv"])
                    if ci >= 2:
                        wait(a, "ac", marks[f"wvc{ci - 2}"]["ac"])
                    bump(a.dma_start(out=WVF[p],
                                     in_=wvt[dc][:, hhalf * 1024:(hhalf + 1) * 1024]),
                         "lC", 16)
                    wait(a, "lC")
                    if ci >= 2:
                        wait(a, svp, marks[f"wvs{ci - 2}"][svp])
                    bump(a.copy(out=WVO[p], in_=WVF[p]), "ac")
                AA(f)
                mark(f"wvc{ci}")

                def f(a, ci=ci, dc=dc, hhalf=hhalf, p=p, svp=svp):
                    wait(a, "ac", marks[f"wvc{ci}"]["ac"])
                    bump(a.dma_start(
                        out=wvb_d[dc][:, hhalf * 1024:(hhalf + 1) * 1024],
                        in_=WVO[p]), svp, 16)
                AA(f)
                mark(f"wvs{ci}")

            def proj_head(h, wsrc, bias_s, tmp, pref, is_q, tmp_guard=None):
                p = h % 2

                def f(s, h=h, p=p, wg=dict(WG)):
                    if h < 2:
                        wguard(s, wg)
                    if h >= 2:
                        wait(s, "dv", marks[f"{pref}cast{h - 2}"]["dv"])
                    bump(s.dma_start(out=wfa[p][:, :], in_=wsrc[h]),
                         "lPe" if h % 2 == 0 else "lPo", 16)
                SS(f)
                mark(f"{pref}ld{h}")

                def f(v, h=h, p=p):
                    lp = "lPe" if h % 2 == 0 else "lPo"
                    wait(v, lp, marks[f"{pref}ld{h}"][lp])
                    if h >= 2:
                        wait(v, "pe", marks[f"{pref}mm{h - 2}"]["pe"])
                    bump(v.tensor_copy(out=wqa[p][:, :], in_=wfa[p][:, :]), "dv")
                VV(f)
                mark(f"{pref}cast{h}")

                def f(t, h=h, p=p):
                    wait(t, "dv", marks[f"{pref}cast{h}"]["dv"])
                    if h == 0:
                        wait(t, "dv", marks["L2done"]["dv"])
                        for nm, v in prev.items():
                            wait(t, nm, v)
                    if h >= 1:
                        wait(t, "dv", marks[f"{pref}ev{h - 1}"]["dv"])
                    for n2 in range(2):
                        for dc in range(DC - 1):
                            t.matmul(psP[n2][:, :],
                                     wqa[p][:, dc * 128:(dc + 1) * 128],
                                     ybuf[:, dc * T + n2 * TH: dc * T + n2 * TH + TH],
                                     start=(dc == 0), stop=False)
                        mm = t.matmul(psP[n2][:, :],
                                      wqa[p][:, (DC - 1) * 128:DC * 128],
                                      ybuf[:, (DC - 1) * T + n2 * TH: (DC - 1) * T + n2 * TH + TH],
                                      start=False, stop=True)
                    bump(mm, "pe")
                TT(f)
                mark(f"{pref}mm{h}")

                def f(v, h=h, p=p):
                    wait(v, "pe", marks[f"{pref}mm{h}"]["pe"])
                    if h >= 2:
                        wait(v, "pe", marks[f"{pref}mp{h - 2}"]["pe"])
                        if tmp_guard is not None:
                            wait(v, "pe", marks[f"{tmp_guard}{h - 2}"]["pe"])
                    for n2 in range(2):
                        bump(v.tensor_scalar(out=tmp[p][:, n2 * TH:(n2 + 1) * TH],
                                             in0=psP[n2][:, :],
                                             scalar1=bias_s[:, h:h + 1],
                                             scalar2=None, op0=ALU.add), "dv")
                VV(f)
                mark(f"{pref}ev{h}")

                wmap = wmapq_b if is_q else wmapk_b

                def f(t, h=h, p=p):
                    wait(t, "dv", marks[f"{pref}ev{h}"]["dv"])
                    if h >= 1:
                        wait(t, "ac", marks[f"{pref}el{h - 1}"]["ac"])
                    for n2 in range(2):
                        mm = t.matmul(psM[n2][:, :], wmap[:, :],
                                      tmp[p][:, n2 * TH:(n2 + 1) * TH],
                                      start=True, stop=True)
                    bump(mm, "pe")
                TT(f)
                mark(f"{pref}mp{h}")

                def f(a, h=h, p=p):
                    wait(a, "pe", marks[f"{pref}mp{h}"]["pe"])
                    if h >= 1:
                        wait(a, "dv", marks[f"{pref}qm{h - 1}"]["dv"])
                    for n2 in range(2):
                        bump(a.activation(out=ELU[0][:, n2 * TH:(n2 + 1) * TH],
                                          in_=psM[n2][:, :], func=AF.Relu), "ac")
                        bump(a.activation(out=ELU[1][:, n2 * TH:(n2 + 1) * TH],
                                          in_=psM[n2][:, :], func=AF.Relu,
                                          scale=negone[0:64, 0:1]), "ac")
                    wait(a, "ac")
                    bump(a.activation(out=ELU[1][:, :], in_=ELU[1][:, :],
                                      func=AF.Exp, scale=negone[0:64, 0:1]), "ac")
                AA(f)
                mark(f"{pref}el{h}")

            wg_update()
            for h in range(NH):
                proj_head(h, wqt, bq_s, QTMP, "q", True)

                def f(v, h=h):
                    wait(v, "ac", marks[f"qel{h}"]["ac"])
                    bump(v.tensor_tensor(
                        out=qmb[(h % 2) * 64:(h % 2) * 64 + 64,
                                (h // 2) * T:(h // 2) * T + T],
                        in0=ELU[0][:, :], in1=ELU[1][:, :], op=ALU.add), "dv")
                VV(f)
                mark(f"qqm{h}")

                emit_wv_ci(2 * h)
                emit_wv_ci(2 * h + 1)

            wg_update()
            for h in range(NH):
                proj_head(h, wkt, bk_s, KTMP, "k", False, tmp_guard="ktp")

                def f(v, h=h):
                    wait(v, "ac", marks[f"kel{h}"]["ac"])
                    if h >= 1:
                        wait(v, "pe", marks[f"ktp{h - 1}"]["pe"])
                    bump(v.tensor_tensor(
                        out=KTMP[h % 2][0:64, :],
                        in0=ELU[0][:, :], in1=ELU[1][:, :], op=ALU.add), "dv")
                VV(f)
                mark(f"kqm{h}")

                def f(t, h=h):
                    wait(t, "dv", marks[f"kqm{h}"]["dv"])
                    if h >= 1:
                        wait(t, "dv", marks[f"kmv{h - 1}"]["dv"])
                    for mb in range(7):
                        t.transpose(psTq[:, mb * 64:(mb + 1) * 64],
                                    KTMP[h % 2][0:64, mb * 128:(mb + 1) * 128],
                                    identb[0:64, 0:64])
                    bump(t.transpose(psTq[:, 7 * 64:8 * 64],
                                     KTMP[h % 2][0:64, 7 * 128:8 * 128],
                                     identb[0:64, 0:64]), "pe")
                TT(f)
                mark(f"ktp{h}")

                def f(v, h=h):
                    wait(v, "pe", marks[f"ktp{h}"]["pe"])
                    bump(v.tensor_copy(out=KMB[:, h * 512:(h + 1) * 512],
                                       in_=psTq[:, :]), "dv")
                VV(f)
                mark(f"kmv{h}")

            def f(g):
                wait(g, "pe", marks["L2done"]["pe"])
                wait(g, "dv", marks["L2done"]["dv"])
                vv = VBUF.rearrange("p (i h e) -> p i h e", i=2, e=129)
                bump(g.memset(vv[:, :, :, 128:129], 1.0), "pl")
            GG(f)
            mark("vones")

            NG = 4
            for g2 in range(NG):
                for osl in range(8):
                    sidx = g2 * 8 + osl

                    def f(s, g2=g2, osl=osl, sidx=sidx):
                        if sidx == 0:
                            wait(s, "sVe", marks["wvs31"]["sVe"])
                            wait(s, "sVo", marks["wvs31"]["sVo"])
                        if sidx >= 1:
                            wait(s, "pe", marks[f"vmm{sidx - 1}_1"]["pe"])
                        bump(s.dma_start(
                            out=WVS,
                            in_=wvb_d.rearrange("d p c -> p d c")
                            [:, :, osl * 256:(osl + 1) * 256]), "lV", 16)
                    SS(f)
                    mark(f"vld{sidx}")

                    for i in range(2):
                        mchunk = g2 * 2 + i
                        b = i % 2

                        def f(t, g2=g2, osl=osl, sidx=sidx, i=i, mchunk=mchunk, b=b):
                            if sidx == 0 and i == 0:
                                wait(t, "dv", marks[f"kev{NH - 1}"]["dv"])
                                wait(t, "dv", marks[f"kmv{NH - 1}"]["dv"])
                            if osl == 0 and i == 0 and g2 > 0:
                                wait(t, "pe", marks[f"kvmm{g2 - 1}"]["pe"])
                            if i == 0:
                                wait(t, "lV", marks[f"vld{sidx}"]["lV"])
                            if i >= 2 or sidx > 0:
                                pk = (f"vev{sidx}_{i - 2}" if i >= 2
                                      else f"vev{sidx - 1}_{i}")
                                wait(t, "dv", marks[pk]["dv"])
                            for dc in range(DC - 1):
                                t.matmul(psP[b][:, 0:256],
                                         ybuf[:, dc * T + mchunk * 128: dc * T + (mchunk + 1) * 128],
                                         WVS[:, dc * 256:(dc + 1) * 256],
                                         start=(dc == 0), stop=False)
                            t.matmul(psP[b][:, 0:256],
                                     ybuf[:, (DC - 1) * T + mchunk * 128: (DC - 1) * T + (mchunk + 1) * 128],
                                     WVS[:, (DC - 1) * 256:DC * 256],
                                     start=False, stop=False)
                            bump(t.matmul(psP[b][:, 0:256], onesb_r[:],
                                          bv_b[0:1, osl * 256:(osl + 1) * 256],
                                          start=False, stop=True), "pe")
                        TT(f)
                        mark(f"vmm{sidx}_{i}")

                        def f(v, g2=g2, osl=osl, sidx=sidx, i=i, b=b):
                            wait(v, "pe", marks[f"vmm{sidx}_{i}"]["pe"])
                            if sidx == 0 and i == 0:
                                wait(v, "pl", marks["vones"]["pl"])
                            for hh in range(2):
                                h = osl * 2 + hh
                                bump(v.tensor_copy(
                                    out=VBUF[:, i * (NH * 129) + h * 129:
                                             i * (NH * 129) + h * 129 + 128],
                                    in_=psP[b][:, hh * 128:(hh + 1) * 128]), "dv")
                        VV(f)
                        mark(f"vev{sidx}_{i}")

                for h in range(NH):
                    q3 = h // 6
                    j3 = (h % 6) // 2
                    i3 = h % 2
                    kidx = g2 * NH + h

                    def f(t, g2=g2, h=h, i3=i3, kidx=kidx):
                        if h == 0:
                            wait(t, "dv", marks[f"vev{g2 * 8 + 7}_1"]["dv"])
                        if kidx >= 2:
                            wait(t, "dv", marks[f"kvadd{kidx - 2}"]["dv"])
                        for i in range(2):
                            mchunk = g2 * 2 + i
                            mm = t.matmul(
                                psKV[h % 2][i3 * 64:(i3 + 1) * 64, :],
                                KMB[:, h * 512 + mchunk * 64: h * 512 + (mchunk + 1) * 64],
                                VBUF[:, i * (NH * 129) + h * 129:
                                     i * (NH * 129) + (h + 1) * 129],
                                start=(i == 0), stop=(i == 1))
                        bump(mm, "pe")
                    TT(f)
                    mark(f"kvmm{kidx}")

                    def f(v, g2=g2, h=h, q3=q3, j3=j3, i3=i3, kidx=kidx):
                        wait(v, "pe", marks[f"kvmm{kidx}"]["pe"])
                        dst = kvga[i3 * 64:(i3 + 1) * 64,
                                   q3 * 512 + j3 * 129: q3 * 512 + (j3 + 1) * 129]
                        ps = psKV[h % 2][i3 * 64:(i3 + 1) * 64, :]
                        if g2 == 0:
                            bump(v.tensor_copy(out=dst, in_=ps), "dv")
                        else:
                            bump(v.tensor_tensor(out=dst, in0=dst, in1=ps,
                                                 op=ALU.add), "dv")
                    VV(f)
                    mark(f"kvadd{kidx}")

            mark("kvev")

            def f(g):
                wait(g, "dv", marks["kvev"]["dv"])
                bump(g.dma_start(out=cc_kin.ap(), in_=kvga[:]), "g", 16)
                wait(g, "g")
                bump(g.collective_compute(
                    "AllReduce", ALU.add, replica_groups=[list(range(NCORES))],
                    ins=[cc_kin.ap().opt()], outs=[cc_kout.ap().opt()]), "cc", 1)
                wait(g, "cc")
                bump(g.dma_start(out=kvga[:], in_=cc_kout.ap()), "g", 16)
            GG(f)
            mark("kvcc")

            def f(v):
                wait(v, "g", marks["kvcc"]["g"])
                bump(v.tensor_copy(out=kvgb[:], in_=kvga[:]), "dv")
            VV(f)
            mark("kvcast")

            def kvoff(h):
                q3 = h // 6
                j3 = (h % 6) // 2
                i3 = h % 2
                return q3, j3, i3

            for h in range(NH):
                q3, j3, i3 = kvoff(h)

                def f(t, h=h, q3=q3, j3=j3, i3=i3):
                    if h == 0:
                        wait(t, "dv", marks["kvcast"]["dv"])
                    if h >= 1:
                        wait(t, "dv", marks[f"nev{h - 1}"]["dv"])
                    ks = kvgb[i3 * 64:(i3 + 1) * 64,
                              q3 * 512 + j3 * 129 + 128: q3 * 512 + j3 * 129 + 129]
                    for n2 in range(2):
                        mm = t.matmul(psM[n2][0:1, :], ks,
                                      qmb[i3 * 64:(i3 + 1) * 64,
                                          (h // 2) * T + n2 * TH: (h // 2) * T + (n2 + 1) * TH],
                                      start=True, stop=True)
                    bump(mm, "pe")
                TT(f)
                mark(f"nmm{h}")

                def f(v, h=h):
                    wait(v, "pe", marks[f"nmm{h}"]["pe"])
                    if h >= 2:
                        wait(v, "lN", marks[f"nsth{h - 1}"]["lN"])
                    ro = (h % 2) * T
                    bump(v.tensor_copy(out=rows[0:1, ro:ro + TH],
                                       in_=psM[0][0:1, :]), "dv")
                    bump(v.tensor_copy(out=rows[0:1, ro + TH:ro + T],
                                       in_=psM[1][0:1, :]), "dv")
                VV(f)
                mark(f"nev{h}")

                def f(s, h=h):
                    wait(s, "dv", marks[f"nev{h}"]["dv"])
                    ro = (h % 2) * T
                    bump(s.dma_start(out=nrm_d.ap()[h:h + 1, :],
                                     in_=rows[0:1, ro:ro + T]), "lN", 16)
                SS(f)
                mark(f"nsth{h}")

            NB = rows[:, 1024:1152]
            nflat0 = nrm_d.ap().rearrange("a b -> (a b)")

            def f(s):
                wait(s, "lN", marks[f"nsth{NH - 1}"]["lN"])
                bump(s.dma_start(out=NB, in_=nflat0[:]), "lN", 16)
            SS(f)
            mark("nbl")

            def f(v):
                wait(v, "lN", marks["nbl"]["lN"])
                bump(v.tensor_scalar(out=NB, in0=NB, scalar1=1e-6, scalar2=None,
                                     op0=ALU.add), "dv")
                wait(v, "dv")
                bump(v.reciprocal(out=NB, in_=NB), "dv")
            VV(f)
            mark("rnorm")

            def f(s):
                wait(s, "dv", marks["rnorm"]["dv"])
                bump(s.dma_start(out=nflat0[:], in_=NB), "lN", 16)
            SS(f)
            mark("nst")

            nflat = nrm_d.ap().rearrange("a b -> (a b)")

            for h in range(NH):
                q3, j3, i3 = kvoff(h)
                if h % 2 == 0:
                    def f(s, h=h):
                        if h == 0:
                            wait(s, "lN", marks["nst"]["lN"])
                        else:
                            wait(s, "pe", marks[f"rbc{h - 1}"]["pe"])
                        bump(s.dma_start(out=rows[0:1, 0:2048],
                                         in_=nflat[h * T:(h + 2) * T]), "lN", 16)
                    SS(f)
                    mark(f"nld{h}")

                def f(t, h=h):
                    wait(t, "lN", marks[f"nld{h - h % 2}"]["lN"])
                    if h == 0:
                        wait(t, "dv", marks["rnorm"]["dv"])
                    if h >= 1:
                        wait(t, "dv", marks[f"rnb{h - 1}"]["dv"])
                    for n2 in range(2):
                        mm = t.matmul(psM[n2][:, :],
                                      onesf_r[0:1, 0:64],
                                      rows[0:1, (h % 2) * T + n2 * TH:
                                           (h % 2) * T + (n2 + 1) * TH],
                                      start=True, stop=True)
                    bump(mm, "pe")
                TT(f)
                mark(f"rbc{h}")

                def f(v, h=h, i3=i3):
                    wait(v, "pe", marks[f"rbc{h}"]["pe"])
                    if h >= 1:
                        wait(v, "dv", marks[f"qn{h - 1}"]["dv"])
                    bump(v.tensor_copy(out=RNB2[i3][:, 0:TH], in_=psM[0][:, :]), "dv")
                    bump(v.tensor_copy(out=RNB2[i3][:, TH:T], in_=psM[1][:, :]), "dv")
                VV(f)
                mark(f"rnb{h}")

                def f(v, h=h, i3=i3):
                    wait(v, "dv", marks[f"rnb{h}"]["dv"])
                    sl = qmb[i3 * 64:(i3 + 1) * 64, (h // 2) * T:(h // 2) * T + T]
                    bump(v.tensor_tensor(out=sl, in0=sl, in1=RNB2[i3][:, :],
                                         op=ALU.mult), "dv")
                VV(f)
                mark(f"qn{h}")

                def f(t, h=h, q3=q3, j3=j3, i3=i3):
                    wait(t, "dv", marks[f"qn{h}"]["dv"])
                    if h >= 1:
                        wait(t, "dv", marks[f"aev{h - 1}"]["dv"])
                    kv = kvgb[i3 * 64:(i3 + 1) * 64,
                              q3 * 512 + j3 * 129: q3 * 512 + j3 * 129 + 128]
                    for n2 in range(2):
                        mm = t.matmul(psP[n2][:, :], kv,
                                      qmb[i3 * 64:(i3 + 1) * 64,
                                          (h // 2) * T + n2 * TH:(h // 2) * T + (n2 + 1) * TH],
                                      start=True, stop=True)
                    bump(mm, "pe")
                TT(f)
                mark(f"amm{h}")

                def f(v, h=h):
                    wait(v, "pe", marks[f"amm{h}"]["pe"])
                    for n2 in range(2):
                        bump(v.tensor_copy(
                            out=ybuf[:, h * T + n2 * TH: h * T + (n2 + 1) * TH],
                            in_=psP[n2][:, :]), "dv")
                VV(f)
                mark(f"aev{h}")

            wg_update()
            for o in range(DC):
                p = o % 2

                def f(s, o=o, p=p, wg=dict(WG)):
                    if o < 2:
                        wguard(s, wg)
                    if o >= 2:
                        wait(s, "dv", marks[f"ocast{o - 2}"]["dv"])
                    bump(s.dma_start(out=wfa[p][:, :], in_=wot[o]),
                         "lPe" if o % 2 == 0 else "lPo", 16)
                SS(f)
                mark(f"old{o}")

                def f(v, o=o, p=p):
                    lp = "lPe" if o % 2 == 0 else "lPo"
                    wait(v, lp, marks[f"old{o}"][lp])
                    if o >= 2:
                        wait(v, "pe", marks[f"omm{o - 2}"]["pe"])
                    bump(v.tensor_copy(out=wqa[p][:, :], in_=wfa[p][:, :]), "dv")
                VV(f)
                mark(f"ocast{o}")

                def f(t, o=o, p=p):
                    wait(t, "dv", marks[f"ocast{o}"]["dv"])
                    if o == 0:
                        wait(t, "dv", marks["kvev"]["dv"])
                    if o >= 1:
                        wait(t, "dv", marks[f"oep{o - 1}"]["dv"])
                    for n2 in range(2):
                        for dc in range(DC - 1):
                            t.matmul(psK[n2][:, :],
                                     wqa[p][:, dc * 128:(dc + 1) * 128],
                                     ybuf[:, dc * T + n2 * TH: dc * T + n2 * TH + TH],
                                     start=(dc == 0), stop=False)
                        mm = t.matmul(psK[n2][:, :],
                                      wqa[p][:, (DC - 1) * 128:DC * 128],
                                      ybuf[:, (DC - 1) * T + n2 * TH: (DC - 1) * T + n2 * TH + TH],
                                      start=False, stop=True)
                    bump(mm, "pe")
                TT(f)
                mark(f"omm{o}")

                def f(v, o=o, p=p):
                    wait(v, "pe", marks[f"omm{o}"]["pe"])
                    for n2 in range(2):
                        sl = xf[:, o * T + n2 * TH: o * T + (n2 + 1) * TH]
                        bump(v.scalar_tensor_tensor(
                            out=sl, in0=psK[n2][:, :], scalar=bo_s[:, o:o + 1],
                            in1=sl, op0=ALU.add, op1=ALU.add), "dv")
                VV(f)
                mark(f"oep{o}")
        wg_update()
        mark("attdone")

    def emit_output(src):
        with nc.psum_tensor("psT2a", [128, TH], F32) as pst0, \
             nc.psum_tensor("psT2b", [128, TH], F32) as pst1:
            psT2 = [pst0, pst1]
            for m in range(8):
                for qq in range(4):
                    b = qq % 2

                    def f(t, m=m, qq=qq, b=b):
                        if m == 0 and qq == 0:
                            for nm, v in marks[src].items():
                                wait(t, nm, v)
                        if m > 0 or qq >= 2:
                            pk = (f"oev{m}_{qq - 2}" if qq >= 2
                                  else f"oev{m - 1}_{qq + 2}")
                            wait(t, "dv", marks[pk]["dv"])
                        for i in range(4):
                            dc = qq * 4 + i
                            mm = t.transpose(psT2[b][:, i * 128:(i + 1) * 128],
                                             xf[:, dc * T + m * 128: dc * T + (m + 1) * 128],
                                             identf[:])
                        bump(mm, "pe")
                    TT(f)
                    mark(f"otp{m}_{qq}")

                    def f(v, m=m, qq=qq, b=b):
                        wait(v, "pe", marks[f"otp{m}_{qq}"]["pe"])
                        if m > 0:
                            wait(v, "sO", marks[f"ost{m - 1}"]["sO"])
                        bump(v.tensor_copy(out=XOUT[:, qq * 512:(qq + 1) * 512],
                                           in_=psT2[b][:, :]), "dv")
                    VV(f)
                    mark(f"oev{m}_{qq}")

                def f(s, m=m):
                    wait(s, "dv", marks[f"oev{m}_3"]["dv"])
                    bump(s.dma_start(out=out[m * 128:(m + 1) * 128, :], in_=XOUT),
                         "sO", 16)
                SS(f)
                mark(f"ost{m}")

            def f(s):
                wait(s, "sO", cnt["sO"])
            SS(f)

    emit_ln("L1", marks["xf_done"]["dv"])
    emit_ffn(0, "L1")
    if stage >= 2:
        emit_ln("L2", marks["ffn0done"]["dv"])
        emit_attention()
    if stage >= 3:
        emit_ln("L3", marks["attdone"]["dv"])
        emit_ffn(1, "L3")
        emit_output("ffn1done")
    elif stage == 2:
        emit_output("attdone")
    else:
        emit_output("ffn0done")

    st.close()
    return nc


def host_prep(inp):
    c = np.ascontiguousarray
    w1 = np.asarray(inp["L0_ffn_w1"]); w2 = np.asarray(inp["L0_ffn_w2"])
    w3 = np.asarray(inp["L0_ffn_w3"])
    w1t = c(w1.T).reshape(16, 128, 44, 128).transpose(2, 1, 0, 3)
    w2t = c(w2.T).reshape(16, 128, 44, 128).transpose(2, 1, 0, 3)
    w3t = c(w3.T).reshape(44, 128, 16, 128).transpose(2, 1, 0, 3)
    w1t = c(w1t.reshape(44, 128, 2048))
    w2t = c(w2t.reshape(44, 128, 2048))
    w3t = c(w3t.reshape(16, 128, 5632))
    wq = np.asarray(inp["L0_attn_wq"]); wk = np.asarray(inp["L0_attn_wk"])
    wv = np.asarray(inp["L0_attn_wv"]); wo = np.asarray(inp["L0_attn_wo"])
    mk = lambda w: c(c(w.T).reshape(16, 128, 16, 128).transpose(2, 1, 0, 3)
                     .reshape(16, 128, 2048))
    base = {
        "w1t": w1t, "w2t": w2t, "w3t": w3t,
        "wqt": mk(wq), "wkt": mk(wk), "wot": mk(wo),
        "wvt": c(wv.T.reshape(16, 128, 2048)),
        "wmapqt": c(np.asarray(inp["L0_attn_wmap_q"]).T),
        "wmapkt": c(np.asarray(inp["L0_attn_wmap_k"]).T),
        "bqt": c(np.asarray(inp["L0_attn_bq"]).reshape(16, 128).T),
        "bkt": c(np.asarray(inp["L0_attn_bk"]).reshape(16, 128).T),
        "bot": c(np.asarray(inp["L0_attn_bo"]).reshape(16, 128).T),
        "bvr": c(np.asarray(inp["L0_attn_bv"]).reshape(1, 2048)),
    }
    g1 = w1t.reshape(NCORES, 128, GSL)
    g2 = w2t.reshape(NCORES, 128, GSL)
    g3 = w3t.reshape(NCORES, 128, GSL)
    x = np.asarray(inp["L0_attn_hidden_states"]).reshape(8192, 2048)
    maps = []
    for ci in range(NCORES):
        m = dict(base)
        m["xin"] = c(x[ci * T:(ci + 1) * T, :])
        m["gsl"] = c(np.stack([g1[ci], g2[ci], g3[ci]]))
        maps.append(m)
    return maps


def kernel(**inputs):
    maps = host_prep(inputs)
    nc = build(STAGE)
    res = run_bass_kernel_spmd(nc, maps, core_ids=list(range(NCORES)))
    outs = [res.results[i]["out"] for i in range(NCORES)]
    return np.concatenate(outs, 0).reshape(1, 8192, 2048)


if __name__ == "__main__":
    import sys
    stg = int(sys.argv[1]) if len(sys.argv) > 1 else STAGE
    nc = build(stg)
    n = sum(1 for _ in nc.all_instructions())
    print(f"build ok (stage {stg}), instructions: {n}")


# version 17 (final)
# speedup vs baseline: 1.2749x; 1.0425x over previous
import contextlib
import numpy as np

import concourse.bass as bass
import concourse.mybir as mybir
from concourse.bass_utils import run_bass_kernel_spmd

F32 = mybir.dt.float32
BF16 = mybir.dt.bfloat16
AF = mybir.ActivationFunctionType
ALU = mybir.AluOpType

NCORES = 8
T = 1024
TH = 512
D = 2048
DC = 16
HID = 5632
HJ = 44
NH = 16
MM = 64
NW = float(HID * D)
GSL = HJ * 128 * 2048 // NCORES // 128

STAGE = 3


def build(stage=STAGE):
    nc = bass.Bass("TRN2", target_bir_lowering=False,
                   dynamic_dma_scratch_size=2048)

    dt_in = lambda n, s: nc.dram_tensor(n, s, F32, kind="ExternalInput").ap()
    xin = dt_in("xin", [T, D])
    w1t = dt_in("w1t", [HJ, 128, 2048])
    w2t = dt_in("w2t", [HJ, 128, 2048])
    w3t = dt_in("w3t", [DC, 128, HID])
    gsl = dt_in("gsl", [3, 128, GSL])
    wqt = dt_in("wqt", [NH, 128, 2048])
    wkt = dt_in("wkt", [NH, 128, 2048])
    wvt = dt_in("wvt", [DC, 128, 2048])
    wot = dt_in("wot", [DC, 128, 2048])
    wmapqt = dt_in("wmapqt", [128, MM])
    wmapkt = dt_in("wmapkt", [128, MM])
    bqt = dt_in("bqt", [128, NH])
    bkt = dt_in("bkt", [128, NH])
    bvr = dt_in("bvr", [1, D])
    bot = dt_in("bot", [128, NH])
    out = nc.dram_tensor("out", [T, D], F32, kind="ExternalOutput").ap()

    w1q = nc.dram_tensor("w1q", [HJ, 128, 2048], BF16).ap()
    w2q = nc.dram_tensor("w2q", [HJ, 128, 2048], BF16).ap()
    w3q = nc.dram_tensor("w3q", [DC, 128, HID], BF16).ap()
    wvb_d = nc.dram_tensor("wvb_d", [DC, 128, 2048], BF16).ap()
    cc_gin = nc.dram_tensor("cc_gin", [1, 4], F32)
    cc_gout = nc.dram_tensor("cc_gout", [1, 4], F32)
    cc_kin = nc.dram_tensor("cc_kin", [128, 1536], F32)
    cc_kout = nc.dram_tensor("cc_kout", [128, 1536], F32)
    nrm_d = nc.dram_tensor("nrm_d", [NH, T], F32)

    st = contextlib.ExitStack()
    sb = lambda name, shape, dt: st.enter_context(nc.sbuf_tensor(name, shape, dt))

    xf = sb("xf", [128, DC * T], F32)
    ybuf = sb("ybuf", [128, DC * T], BF16)
    wfa = [sb("wfa0", [128, 2048], F32), sb("wfa1", [128, 2048], F32)]
    wqa = [sb("wqa0", [128, 2048], BF16), sb("wqa1", [128, 2048], BF16)]
    AR = 18960
    arena = sb("arena", [128, AR], F32)
    x2t = [sb("x2t0", [128, TH], BF16), sb("x2t1", [128, TH], BF16)]
    sqt = sb("sqt", [128, TH], BF16)
    xbt = [sb("xbt0", [128, TH], BF16), sb("xbt1", [128, TH], BF16)]
    nrm = sb("nrm", [NH, T], F32)
    onesb_c = sb("onesb_c", [128, 1], BF16)
    onesb_r = sb("onesb_r", [1, 128], BF16)
    onesf_r = sb("onesf_r", [1, 128], F32)
    onesf_c = sb("onesf_c", [128, 1], F32)
    thrv = sb("thrv", [128, 8], F32)
    gpart = sb("gpart", [128, 2], F32)
    gprow = sb("gprow", [1, 4], F32)
    identf = sb("identf", [128, 128], F32)
    identb = sb("identb", [128, 128], BF16)
    wmapq_b = sb("wmapq_b", [128, MM], BF16)
    wmapk_b = sb("wmapk_b", [128, MM], BF16)
    wmap_f = sb("wmap_f", [128, MM], F32)
    bq_s = sb("bq_s", [128, NH], F32)
    bk_s = sb("bk_s", [128, NH], F32)
    bo_s = sb("bo_s", [128, NH], F32)
    negone = sb("negone", [128, 1], F32)
    halfc = sb("halfc", [128, 1], F32)
    qmb = sb("qmb", [128, (NH // 2) * T], BF16)
    bv_f = None
    bv_b = None

    def cf(c0, n):
        assert c0 + n <= AR
        return arena[:, c0:c0 + n]

    def cb(c0, nb):
        assert c0 + nb // 2 <= AR
        return arena[:, c0:c0 + nb // 2].bitcast(BF16)

    GH = cb(0, HJ * TH)
    SQW = [cb(11264, 2048), cb(12288, 2048)]
    G2B = [cf(13376, 2048), cf(15424, 2048)]
    W3R = [cb(13376, 2816), cb(14784, 2816)]
    rows = cf(16192, 2048)
    W3I = [cf(11264, 1408), cf(16192, 1408), qmb[:, 0:2816].bitcast(F32)]
    W3Rq = [W3R[0][:, 0:1408], W3R[1][:, 0:1408]]
    SQB3 = [W3R[0][:, 1408:2816], W3R[1][:, 1408:2816]]
    XB = cb(0, 16 * TH)
    SQ = cb(4096, 16 * TH)
    XTOK = [cf(c, 2048) for c in (0, 2048, 4096, 6144)]
    GAMS = cf(8192, GSL // 2)
    XOUT = cf(0, 2048)
    VBUF = cb(0, 2 * NH * 129)
    KMB = cb(2064, NH * 512)
    KTMP = [cb(6160, T), cb(6672, T)]
    WVS = [cb(7184, DC * 256), cb(15888, DC * 256)]
    QTMP = [cb(9232, T), cb(9744, T)]
    ELU = [cf(10256, T)[0:64, :], cf(11280, T)[0:64, :]]
    kvga = cf(12304, 1536)
    kvgb = cb(13840, 1536)
    RNBT = cb(15376, T)
    RNB2 = [RNBT[0:64, :], RNBT[64:128, :]]
    WVF = [cf(15888, 1024), cf(16912, 1024)]
    WVO = [cb(17936, 1024), cb(18448, 1024)]

    bv_f = wfa[0][0:1, 0:D]
    bv_b = nrm[0:1, :].bitcast(BF16)

    sems = {}
    for nm in ("lA", "lX", "lWe", "lWo", "lBe", "lBo", "lPe", "lPo",
               "lV", "lN", "sO", "lG", "l3e", "l3o", "lC",
               "sWe", "sWo", "sW2", "sW3", "sQe", "sQo", "sVe", "sVo",
               "g", "cc", "pe", "ac", "dv", "pl", "gc"):
        sems[nm] = st.enter_context(nc.semaphore("s_" + nm))
    cnt = {k: 0 for k in sems}

    def bump(inst, nm, k=1):
        inst.then_inc(sems[nm], k)
        cnt[nm] += k
        return cnt[nm]

    def wait(eng, nm, v=None):
        eng.wait_ge(sems[nm], cnt[nm] if v is None else v)

    block = st.enter_context(nc.Block())
    TT = block.tensor
    VV = block.vector
    AA = block.scalar
    GG = block.gpsimd
    SS = block.sync

    marks = {}

    def mark(key):
        marks[key] = dict(cnt)

    WG = {"dv": 0, "pe": 0, "ac": 0, "sWe": 0, "sWo": 0, "sW2": 0, "sW3": 0}

    def wguard(eng, wg):
        for nm, v in wg.items():
            if v:
                eng.wait_ge(sems[nm], v)

    def wg_update():
        for nm in WG:
            WG[nm] = cnt[nm]

    def f(g):
        for tile, val in ((identf, 0.0), (identb, 0.0), (onesb_c, 1.0),
                          (onesb_r, 1.0), (onesf_r, 1.0), (onesf_c, 1.0),
                          (negone, -1.0), (halfc, 0.5)):
            bump(g.memset(tile[:], val), "pl")
        wait(g, "pl")
        bump(g.affine_select(out=identf[:], in_=identf[:], compare_op=ALU.not_equal,
                             fill=1.0, base=0, pattern=[[-1, 128]],
                             channel_multiplier=1), "pl")
        bump(g.affine_select(out=identb[:], in_=identb[:], compare_op=ALU.not_equal,
                             fill=1.0, base=0, pattern=[[-1, 128]],
                             channel_multiplier=1), "pl")
    GG(f)
    mark("consts")

    def f(s):
        for dst, src in ((bq_s[:], bqt), (bk_s[:], bkt), (bo_s[:], bot),
                         (bv_f[:], bvr), (wmap_f[:], wmapqt)):
            bump(s.dma_start(out=dst, in_=src), "lA", 16)
    SS(f)
    mark("smallld")

    def f(a):
        wait(a, "lA", marks["smallld"]["lA"])
        bump(a.copy(out=wmapq_b[:], in_=wmap_f[:]), "ac")
        bump(a.copy(out=bv_b[:], in_=bv_f[:]), "ac")
    AA(f)
    mark("wmapq_cast")

    def f(s):
        wait(s, "ac", marks["wmapq_cast"]["ac"])
        bump(s.dma_start(out=wmap_f[:], in_=wmapkt), "lA", 16)
    SS(f)
    mark("wmapk_ld")

    def f(a):
        wait(a, "lA", marks["wmapk_ld"]["lA"])
        bump(a.copy(out=wmapk_b[:], in_=wmap_f[:]), "ac")
    AA(f)
    mark("wmapk_cast")

    with nc.psum_tensor("ps_g", [1, 4], F32) as ps_g, \
         nc.psum_tensor("ps_thr", [128, 8], F32) as ps_thr, \
         nc.psum_tensor("ps_xt0", [128, 512], F32) as ps_xt0, \
         nc.psum_tensor("ps_xt1", [128, 512], F32) as ps_xt1:
        ps_xt = [ps_xt0, ps_xt1]

        GH2 = GSL // 2
        for w in range(3):
            for hf in range(2):
                def f(a, w=w, hf=hf):
                    if w + hf > 0:
                        pw, ph = (w, hf - 1) if hf else (w - 1, 1)
                        wait(a, "dv", marks[f"gred{pw}_{ph}"]["dv"])
                    bump(a.dma_start(out=GAMS, in_=gsl[w][:, hf * GH2:(hf + 1) * GH2]),
                         "lG", 16)
                AA(f)
                mark(f"gld{w}_{hf}")

                def f(v, w=w, hf=hf):
                    wait(v, "lG", marks[f"gld{w}_{hf}"]["lG"])
                    if hf == 0 and w > 0:
                        wait(v, "pe", marks[f"gtot{w - 1}"]["pe"])
                    bump(v.reduce_sum(out=gpart[:, hf:hf + 1], in_=GAMS,
                                      axis=mybir.AxisListType.X,
                                      apply_absolute_value=True), "dv")
                VV(f)
                mark(f"gred{w}_{hf}")

            def f(t, w=w):
                wait(t, "dv", marks[f"gred{w}_1"]["dv"])
                if w == 0:
                    wait(t, "pl", marks["consts"]["pl"])
                bump(t.matmul(ps_g[0:1, w:w + 1], gpart[:, 0:1], onesf_c[:],
                              start=True, stop=False), "pe")
                bump(t.matmul(ps_g[0:1, w:w + 1], gpart[:, 1:2], onesf_c[:],
                              start=False, stop=True), "pe")
            TT(f)
            mark(f"gtot{w}")

        def f(v):
            bump(v.memset(gprow[:], 0.0), "dv")
            wait(v, "pe", marks["gtot2"]["pe"])
            wait(v, "dv")
            bump(v.tensor_copy(out=gprow[0:1, 0:3], in_=ps_g[0:1, 0:3]), "dv")
        VV(f)
        mark("gpack")

        def f(g):
            wait(g, "dv", marks["gpack"]["dv"])
            bump(g.dma_start(out=cc_gin.ap(), in_=gprow[:]), "g", 16)
            wait(g, "g")
            bump(g.collective_compute(
                "AllReduce", ALU.add, replica_groups=[list(range(NCORES))],
                ins=[cc_gin.ap().opt()], outs=[cc_gout.ap().opt()]), "cc", 1)
            wait(g, "cc")
            bump(g.dma_start(out=gprow[:], in_=cc_gout.ap()), "g", 16)
        GG(f)
        mark("gcc")

        def f(v):
            wait(v, "g", marks["gcc"]["g"])
            bump(v.tensor_scalar(out=rows[0:1, 0:4], in0=gprow[:],
                                 scalar1=0.5 / NW, scalar2=0.5e-8,
                                 op0=ALU.mult, op1=ALU.add), "dv")
            bump(v.tensor_scalar(out=rows[0:1, 4:8], in0=gprow[:],
                                 scalar1=-0.5 / NW, scalar2=-0.5e-8,
                                 op0=ALU.mult, op1=ALU.add), "dv")
        VV(f)
        mark("throw")

        def f(t):
            wait(t, "dv", marks["throw"]["dv"])
            bump(t.matmul(ps_thr[:, :], onesf_r[:], rows[0:1, 0:8],
                          start=True, stop=True), "pe")
        TT(f)
        mark("thrbc")

        def f(v):
            wait(v, "pe", marks["thrbc"]["pe"])
            bump(v.tensor_copy(out=thrv[:], in_=ps_thr[:, :]), "dv")
        VV(f)
        mark("thr")
        THR_DV = cnt["dv"]

        for grp in range(2):
            def f(s, grp=grp):
                if grp == 1:
                    wait(s, "pe", marks["xt0"]["pe"])
                for i in range(4):
                    m = grp * 4 + i
                    bump(s.dma_start(out=XTOK[i],
                                     in_=xin[m * 128:(m + 1) * 128, :]), "lX", 16)
            SS(f)
            mark(f"xld{grp}")

            for dc in range(DC):
                b = dc % 2

                def f(t, grp=grp, dc=dc, b=b):
                    if dc == 0:
                        wait(t, "lX", marks[f"xld{grp}"]["lX"])
                        if grp == 0:
                            wait(t, "pl", marks["consts"]["pl"])
                    if dc >= 2 or grp == 1:
                        pk = f"xtev{grp}_{dc - 2}" if dc >= 2 else f"xtev0_{DC - 2 + dc}"
                        wait(t, "dv", marks[pk]["dv"])
                    for i in range(3):
                        t.transpose(ps_xt[b][:, i * 128:(i + 1) * 128],
                                    XTOK[i][:, dc * 128:(dc + 1) * 128], identf[:])
                    bump(t.transpose(ps_xt[b][:, 3 * 128:4 * 128],
                                     XTOK[3][:, dc * 128:(dc + 1) * 128],
                                     identf[:]), "pe")
                TT(f)
                mark(f"xtp{grp}_{dc}")

                def f(v, grp=grp, dc=dc, b=b):
                    wait(v, "pe", marks[f"xtp{grp}_{dc}"]["pe"])
                    bump(v.tensor_copy(
                        out=xf[:, dc * T + grp * 512: dc * T + grp * 512 + 512],
                        in_=ps_xt[b][:, :]), "dv")
                VV(f)
                mark(f"xtev{grp}_{dc}")
            mark(f"xt{grp}")
    mark("xf_done")

    def emit_ln(tag, src_dv):
        with nc.psum_tensor("psS_" + tag, [1, TH], F32) as psS, \
             nc.psum_tensor("psQ_" + tag, [1, TH], F32) as psQ, \
             nc.psum_tensor("psB1_" + tag, [128, TH], F32) as psB1, \
             nc.psum_tensor("psB2_" + tag, [128, TH], F32) as psB2:
            for h2 in range(2):
                off = h2 * TH

                def f(v, off=off):
                    if off == 0:
                        wait(v, "dv", src_dv)
                    else:
                        wait(v, "pe", marks[f"{tag}mq_0"]["pe"])
                    for dc in range(DC):
                        bump(v.tensor_copy(out=XB[:, dc * TH:(dc + 1) * TH],
                                           in_=xf[:, dc * T + off: dc * T + off + TH]),
                             "dv")
                VV(f)
                mark(f"{tag}xb_{off}")

                def f(a, off=off):
                    wait(a, "dv", marks[f"{tag}xb_{off}"]["dv"])
                    bump(a.activation(out=SQ[:, :], in_=XB[:, :],
                                      func=AF.Square), "ac")
                AA(f)
                mark(f"{tag}sq_{off}")

                def f(t, off=off):
                    wait(t, "ac", marks[f"{tag}sq_{off}"]["ac"])
                    if off:
                        wait(t, "dv", marks[f"{tag}var_0"]["dv"])
                    for dc in range(DC):
                        t.matmul(psS[0:1, :], onesb_c[:], XB[:, dc * TH:(dc + 1) * TH],
                                 start=(dc == 0), stop=(dc == DC - 1))
                    for dc in range(DC - 1):
                        t.matmul(psQ[0:1, :], onesb_c[:], SQ[:, dc * TH:(dc + 1) * TH],
                                 start=(dc == 0), stop=False)
                    bump(t.matmul(psQ[0:1, :], onesb_c[:],
                                  SQ[:, (DC - 1) * TH:DC * TH],
                                  start=False, stop=True), "pe")
                TT(f)
                mark(f"{tag}mq_{off}")

                def f(v, off=off):
                    wait(v, "pe", marks[f"{tag}mq_{off}"]["pe"])
                    r = rows
                    bump(v.tensor_scalar(out=r[0:1, 0:TH], in0=psS[0:1, :],
                                         scalar1=1.0 / D, scalar2=None,
                                         op0=ALU.mult), "dv")
                    bump(v.tensor_scalar(out=r[0:1, TH:2 * TH], in0=psQ[0:1, :],
                                         scalar1=1.0 / D, scalar2=None,
                                         op0=ALU.mult), "dv")
                    wait(v, "dv")
                    bump(v.tensor_tensor(out=r[0:1, 2 * TH:3 * TH], in0=r[0:1, 0:TH],
                                         in1=r[0:1, 0:TH], op=ALU.mult), "dv")
                    wait(v, "dv")
                    bump(v.tensor_scalar(out=r[0:1, 2 * TH:3 * TH],
                                         in0=r[0:1, 2 * TH:3 * TH],
                                         scalar1=-1.0, scalar2=1e-5,
                                         op0=ALU.mult, op1=ALU.add), "dv")
                    wait(v, "dv")
                    bump(v.tensor_tensor(out=r[0:1, 2 * TH:3 * TH],
                                         in0=r[0:1, TH:2 * TH],
                                         in1=r[0:1, 2 * TH:3 * TH],
                                         op=ALU.add), "dv")
                VV(f)
                mark(f"{tag}var_{off}")

                def f(a, off=off):
                    wait(a, "dv", marks[f"{tag}var_{off}"]["dv"])
                    bump(a.activation(out=rows[0:1, 3 * TH:4 * TH],
                                      in_=rows[0:1, 2 * TH:3 * TH],
                                      func=AF.Sqrt), "ac")
                AA(f)
                mark(f"{tag}sd_{off}")

                def f(v, off=off):
                    wait(v, "ac", marks[f"{tag}sd_{off}"]["ac"])
                    r = rows
                    bump(v.reciprocal(out=r[0:1, TH:2 * TH],
                                      in_=r[0:1, 3 * TH:4 * TH]), "dv")
                    wait(v, "dv")
                    bump(v.tensor_tensor(out=r[0:1, 2 * TH:3 * TH], in0=r[0:1, 0:TH],
                                         in1=r[0:1, TH:2 * TH],
                                         op=ALU.mult), "dv")
                    wait(v, "dv")
                    bump(v.tensor_scalar(out=r[0:1, 2 * TH:3 * TH],
                                         in0=r[0:1, 2 * TH:3 * TH],
                                         scalar1=-1.0, scalar2=None,
                                         op0=ALU.mult), "dv")
                VV(f)
                mark(f"{tag}rs_{off}")

                def f(t, off=off):
                    wait(t, "dv", marks[f"{tag}rs_{off}"]["dv"])
                    bump(t.matmul(psB1[:, :], onesf_r[:], rows[0:1, TH:2 * TH],
                                  start=True, stop=True), "pe")
                    bump(t.matmul(psB2[:, :], onesf_r[:], rows[0:1, 2 * TH:3 * TH],
                                  start=True, stop=True), "pe")
                TT(f)
                mark(f"{tag}bc_{off}")

                for dc in range(DC):
                    def f(v, dc=dc, off=off):
                        if dc == 0:
                            wait(v, "pe", marks[f"{tag}bc_{off}"]["pe"])
                        else:
                            wait(v, "dv")
                        bump(v.tensor_tensor(out=G2B[0][:, 0:TH],
                                             in0=xf[:, dc * T + off: dc * T + off + TH],
                                             in1=psB1[:, :], op=ALU.mult), "dv")
                        wait(v, "dv")
                        bump(v.tensor_tensor(
                            out=ybuf[:, dc * T + off: dc * T + off + TH],
                            in0=G2B[0][:, 0:TH], in1=psB2[:, :], op=ALU.add), "dv")
                    VV(f)
                    mark(f"{tag}y{dc}_{off}")
        mark(f"{tag}done")

    def emit_ffn(call, lntag):
        with nc.psum_tensor(f"psX{call}a", [128, TH], F32) as psxa, \
             nc.psum_tensor(f"psX{call}b", [128, TH], F32) as psxb, \
             nc.psum_tensor(f"psO{call}a", [128, TH], F32) as psoa, \
             nc.psum_tensor(f"psO{call}b", [128, TH], F32) as psob:
            psX = [psxa, psxb]
            psO = [psoa, psob]
            guard = {"psX": None, "psO": None, "GH": None, "W3R": None}

            def sweep(sw, wsrc, wscr, h2):
                off = h2 * TH
                quant = (call == 0 and h2 == 0)
                key = f"F{call}s{sw}h{h2}"
                widx = 0 if sw == 1 else 1
                if quant:
                    wfaH = [wfa[s // 2][:, (s % 2) * 1024:(s % 2 + 1) * 1024]
                            for s in range(4)]
                    wqaH = [wqa[s // 2][:, (s % 2) * 1024:(s % 2 + 1) * 1024]
                            for s in range(4)]
                    SQWH = [SQW[s // 2][:, (s % 2) * 1024:(s % 2 + 1) * 1024]
                            for s in range(4)]
                    SLS = ("sWe", "sWo", "sW2", "sW3")

                    def odd_load(a, jh, sw, wg):
                        j, hf = jh // 2, jh % 2
                        sl = jh % 4
                        if jh < 4:
                            wguard(a, wg)
                            if sw == 2:
                                wait(a, "ac", marks["F0s1h0end"]["ac"])
                        bump(a.dma_start(
                            out=wfaH[sl],
                            in_=wsrc[j][:, hf * 1024:(hf + 1) * 1024]),
                            "lWo", 16)
                        marks[f"{key}l{jh}"] = dict(cnt)

                    for jh2 in range(0, 2 * HJ, 2):
                        jh = jh2
                        j, hf = jh // 2, jh % 2
                        sl = jh % 4

                        def f(s, jh=jh, j=j, hf=hf, sl=sl, sw=sw,
                              wg=dict(WG)):
                            if jh < 4:
                                wguard(s, wg)
                                if sw == 2:
                                    wait(s, "ac", marks["F0s1h0end"]["ac"])
                            else:
                                wait(s, "ac", marks[f"{key}s2_{jh - 4}"]["ac"])
                            bump(s.dma_start(
                                out=wfaH[sl],
                                in_=wsrc[j][:, hf * 1024:(hf + 1) * 1024]),
                                "lWe", 16)
                        SS(f)
                        mark(f"{key}l{jh}")

                        def f(a, jh2=jh2, sw=sw, widx=widx, key=key,
                              wg=dict(WG)):
                            if jh2 == 0:
                                odd_load(a, 1, sw, wg)
                                odd_load(a, 3, sw, wg)
                            for jh in (jh2, jh2 + 1):
                                sl = jh % 4
                                lw = "lWe" if jh % 2 == 0 else "lWo"
                                sws = SLS[sl]
                                wait(a, lw, marks[f"{key}l{jh}"][lw])
                                wait(a, "dv", THR_DV)
                                if jh >= 4:
                                    wait(a, "pe", marks[f"{key}m{jh - 4}"]["pe"])
                                    wait(a, sws, marks[f"{key}w{jh - 4}"][sws])
                                    wait(a, "dv", marks[f"{key}q{jh - 4}"]["dv"])
                                elif sw == 2:
                                    wait(a, "pe", marks["F0s1h0end"]["pe"])
                                    wait(a, sws, marks["F0s1h0end"][sws])
                                    wait(a, "dv", marks["F0s1h0end"]["dv"])
                                bump(a.activation(
                                    out=wqaH[sl][:, :], in_=wfaH[sl][:, :],
                                    func=AF.Sign,
                                    bias=thrv[:, 4 + widx:5 + widx]), "ac")
                                wait(a, "ac")
                                bump(a.activation(
                                    out=SQWH[sl][:, :], in_=wfaH[sl][:, :],
                                    func=AF.Sign,
                                    bias=thrv[:, widx:widx + 1]), "ac")
                                marks[f"{key}s2_{jh}"] = dict(cnt)
                            if jh2 + 5 < 2 * HJ:
                                odd_load(a, jh2 + 5, sw, wg)
                        AA(f)

                        for jh in (jh2, jh2 + 1):
                            j, hf = jh // 2, jh % 2
                            sl = jh % 4
                            sws = SLS[sl]

                            def f(v, jh=jh, sl=sl):
                                wait(v, "ac", marks[f"{key}s2_{jh}"]["ac"])
                                bump(v.tensor_tensor(
                                    out=wqaH[sl][:, :], in0=wqaH[sl][:, :],
                                    in1=SQWH[sl][:, :], op=ALU.add), "dv")
                            VV(f)
                            mark(f"{key}q{jh}")

                            def f(g, jh=jh, j=j, hf=hf, sl=sl, sws=sws):
                                wait(g, "dv", marks[f"{key}q{jh}"]["dv"])
                                bump(g.dma_start(
                                    out=wscr[j][:, hf * 1024:(hf + 1) * 1024],
                                    in_=wqaH[sl][:, :]), sws, 16)
                            (SS if jh % 2 == 0 else AA)(f)
                            mark(f"{key}w{jh}")

                            def f(t, jh=jh, j=j, hf=hf, sl=sl):
                                wait(t, "dv", marks[f"{key}q{jh}"]["dv"])
                                if jh == 0:
                                    wait(t, "dv", marks[f"{lntag}done"]["dv"])
                                    if guard["psX"] is not None:
                                        for nm, v in guard["psX"].items():
                                            wait(t, nm, v)
                                if hf == 0 and j >= 2:
                                    wait(t, "dv", marks[f"{key}c{j - 2}"]["dv"])
                                pj = j % 2
                                for di in range(8):
                                    dc = hf * 8 + di
                                    mm = t.matmul(
                                        psX[pj][:, :],
                                        wqaH[sl][:, di * 128:(di + 1) * 128],
                                        ybuf[:, dc * T + off: dc * T + off + TH],
                                        start=(dc == 0), stop=(dc == DC - 1))
                                bump(mm, "pe")
                            TT(f)
                            mark(f"{key}m{jh}")

                            if hf == 1:
                                if sw == 1:
                                    def f(v, j=j, jh=jh):
                                        wait(v, "pe", marks[f"{key}m{jh}"]["pe"])
                                        if j == 0 and guard["GH"] is not None:
                                            wait(v, "pe", guard["GH"])
                                        bump(v.tensor_copy(
                                            out=GH[:, j * TH:(j + 1) * TH],
                                            in_=psX[j % 2][:, :]), "dv")
                                    VV(f)
                                    mark(f"{key}c{j}")
                                else:
                                    def f(v, j=j, jh=jh):
                                        wait(v, "pe", marks[f"{key}m{jh}"]["pe"])
                                        if j == 0:
                                            wait(v, "ac",
                                                 marks[f"sig{call}_{h2}"]["ac"])
                                        if j >= 2:
                                            wait(v, "dv", marks[f"{key}h{j - 2}"]["dv"])
                                        bump(v.tensor_copy(out=x2t[j % 2][:, :],
                                                           in_=psX[j % 2][:, :]), "dv")
                                        wait(v, "dv")
                                        bump(v.tensor_tensor(
                                            out=GH[:, j * TH:(j + 1) * TH],
                                            in0=GH[:, j * TH:(j + 1) * TH],
                                            in1=x2t[j % 2][:, :],
                                            op=ALU.mult), "dv")
                                    VV(f)
                                    mark(f"{key}c{j}")
                                    marks[f"{key}h{j}"] = dict(cnt)
                    mark(f"{key}end")
                    guard["psX"] = {"dv": cnt["dv"]}
                    wg_update()
                    return
                for j in range(HJ):
                    p = j % 2
                    if True:
                        def f(s, j=j, p=p, sw=sw, wg=dict(WG)):
                            if j < 2:
                                wguard(s, wg)
                                if call == 0 and h2 == 1 and j == 0:
                                    kh = f"F0s{sw}h0"
                                    for snm in ("sWe", "sWo", "sW2", "sW3"):
                                        wait(s, snm, marks[f"{kh}end"][snm])
                            if j >= 2:
                                wait(s, "pe", marks[f"{key}m{j - 2}"]["pe"])
                            bump(s.dma_start(out=wqa[p][:, :], in_=wscr[j]),
                                 "lWe" if j % 2 == 0 else "lWo", 16)
                        SS(f)
                        mark(f"{key}l{j}")

                    def f(t, j=j, p=p):
                        if quant:
                            wait(t, "dv", marks[f"{key}q{j}"]["dv"])
                        else:
                            lw = "lWe" if j % 2 == 0 else "lWo"
                            wait(t, lw, marks[f"{key}l{j}"][lw])
                        if j == 0:
                            wait(t, "dv", marks[f"{lntag}done"]["dv"])
                            if guard["psX"] is not None:
                                for nm, v in guard["psX"].items():
                                    wait(t, nm, v)
                        if j >= 2:
                            wait(t, "dv", marks[f"{key}c{j - 2}"]["dv"])
                        for dc in range(DC - 1):
                            t.matmul(psX[p][:, :],
                                     wqa[p][:, dc * 128:(dc + 1) * 128],
                                     ybuf[:, dc * T + off: dc * T + off + TH],
                                     start=(dc == 0), stop=False)
                        bump(t.matmul(psX[p][:, :],
                                      wqa[p][:, (DC - 1) * 128:DC * 128],
                                      ybuf[:, (DC - 1) * T + off: (DC - 1) * T + off + TH],
                                      start=False, stop=True), "pe")
                    TT(f)
                    mark(f"{key}m{j}")

                    if sw == 1:
                        def f(v, j=j, p=p):
                            wait(v, "pe", marks[f"{key}m{j}"]["pe"])
                            if j == 0 and guard["GH"] is not None:
                                wait(v, "pe", guard["GH"])
                            bump(v.tensor_copy(out=GH[:, j * TH:(j + 1) * TH],
                                               in_=psX[p][:, :]), "dv")
                        VV(f)
                        mark(f"{key}c{j}")
                    else:
                        def f(v, j=j, p=p):
                            wait(v, "pe", marks[f"{key}m{j}"]["pe"])
                            if j == 0:
                                wait(v, "ac", marks[f"sig{call}_{h2}"]["ac"])
                            if j >= 2:
                                wait(v, "dv", marks[f"{key}h{j - 2}"]["dv"])
                            bump(v.tensor_copy(out=x2t[p][:, :],
                                               in_=psX[p][:, :]), "dv")
                            wait(v, "dv")
                            bump(v.tensor_tensor(out=GH[:, j * TH:(j + 1) * TH],
                                                 in0=GH[:, j * TH:(j + 1) * TH],
                                                 in1=x2t[p][:, :],
                                                 op=ALU.mult), "dv")
                        VV(f)
                        mark(f"{key}c{j}")
                        marks[f"{key}h{j}"] = dict(cnt)
                mark(f"{key}end")
                guard["psX"] = {"dv": cnt["dv"]}
                wg_update()

            def sig_batch(h2):
                def f(a, h2=h2):
                    wait(a, "dv", marks[f"F{call}s1h{h2}c{HJ - 1}"]["dv"])
                    bump(a.activation(out=GH[:, :], in_=GH[:, :],
                                      func=AF.Sigmoid,
                                      scale=halfc[:, 0:1]), "ac")
                AA(f)
                mark(f"sig{call}_{h2}")

            def phase_b(h2):
                off = h2 * TH
                key = f"F{call}B{h2}"
                qmode = (call == 0 and h2 == 0)
                if qmode:
                    QB = HJ // 4

                    def odd_loadb(a, bidx):
                        o_, qq_ = divmod(bidx, 4)
                        sin = bidx % 3
                        if bidx < 3:
                            wait(a, "dv", marks["F0s2h0end"]["dv"])
                        bump(a.dma_start(
                            out=W3I[sin],
                            in_=w3t[o_][:, qq_ * 1408:(qq_ + 1) * 1408]),
                            "lBo", 16)
                        marks[f"{key}l{bidx}"] = dict(cnt)

                    for o in range(DC):
                        for qq2 in range(0, 4, 2):
                          if True:
                            bidx = o * 4 + qq2
                            sin = bidx % 3

                            def f(s, o=o, qq=qq2, bidx=bidx, sin=sin):
                                if bidx < 3:
                                    wait(s, "dv", marks["F0s2h0end"]["dv"])
                                else:
                                    wait(s, "ac",
                                         marks[f"{key}s2_{bidx - 3}"]["ac"])
                                bump(s.dma_start(
                                    out=W3I[sin],
                                    in_=w3t[o][:, qq * 1408:(qq + 1) * 1408]),
                                    "lBe", 16)
                            SS(f)
                            mark(f"{key}l{bidx}")

                          def f(a, o=o, qq2=qq2, key=key):
                            if o == 0 and qq2 == 0:
                                odd_loadb(a, 1)
                            for qq in (qq2, qq2 + 1):
                                bidx = o * 4 + qq
                                p = bidx % 2
                                sin = bidx % 3
                                lb = "lBe" if bidx % 2 == 0 else "lBo"
                                wait(a, lb, marks[f"{key}l{bidx}"][lb])
                                sqp = "sQe" if bidx % 2 == 0 else "sQo"
                                if bidx >= 2:
                                    wait(a, "pe", marks[f"{key}m{bidx - 2}"]["pe"])
                                    wait(a, sqp, marks[f"{key}w{bidx - 2}"][sqp])
                                    wait(a, "dv", marks[f"{key}q{bidx - 2}"]["dv"])
                                bump(a.activation(
                                    out=W3Rq[p][:, :], in_=W3I[sin],
                                    func=AF.Sign, bias=thrv[:, 6:7]), "ac")
                                wait(a, "ac")
                                bump(a.activation(
                                    out=SQB3[p][:, :], in_=W3I[sin],
                                    func=AF.Sign, bias=thrv[:, 2:3]), "ac")
                                marks[f"{key}s2_{bidx}"] = dict(cnt)
                            if o * 4 + qq2 + 3 < 4 * DC:
                                odd_loadb(a, o * 4 + qq2 + 3)
                          AA(f)

                          for qq in (qq2, qq2 + 1):
                            bidx = o * 4 + qq
                            p = bidx % 2

                            def f(v, bidx=bidx, p=p):
                                wait(v, "ac", marks[f"{key}s2_{bidx}"]["ac"])
                                bump(v.tensor_tensor(
                                    out=W3Rq[p][:, :], in0=W3Rq[p][:, :],
                                    in1=SQB3[p][:, :], op=ALU.add), "dv")
                            VV(f)
                            mark(f"{key}q{bidx}")

                            def f(g, o=o, qq=qq, bidx=bidx, p=p):
                                sqp = "sQe" if bidx % 2 == 0 else "sQo"
                                wait(g, "dv", marks[f"{key}q{bidx}"]["dv"])
                                bump(g.dma_start(
                                    out=w3q[o][:, qq * 1408:(qq + 1) * 1408],
                                    in_=W3Rq[p][:, :]), sqp, 16)
                            (SS if bidx % 2 == 0 else AA)(f)
                            mark(f"{key}w{bidx}")

                            def f(t, o=o, qq=qq, bidx=bidx, p=p):
                                wait(t, "dv", marks[f"{key}q{bidx}"]["dv"])
                                if bidx == 0:
                                    wait(t, "dv",
                                         marks[f"F{call}s2h{h2}h{HJ - 1}"]["dv"])
                                    if guard["psO"] is not None:
                                        for nm, v in guard["psO"].items():
                                            wait(t, nm, v)
                                if bidx >= 8 and qq == 0:
                                    wait(t, "dv", marks[f"{key}e{o - 2}"]["dv"])
                                for j in range(QB):
                                    jj = qq * QB + j
                                    mm = t.matmul(
                                        psO[o % 2][:, :],
                                        W3Rq[p][:, j * 128:(j + 1) * 128],
                                        GH[:, jj * TH:(jj + 1) * TH],
                                        start=(jj == 0), stop=(jj == HJ - 1))
                                bump(mm, "pe")
                            TT(f)
                            mark(f"{key}m{bidx}")

                        def f(v, o=o):
                            wait(v, "pe", marks[f"{key}m{o * 4 + 3}"]["pe"])
                            sl = xf[:, o * T + off: o * T + off + TH]
                            bump(v.scalar_tensor_tensor(
                                out=sl, in0=psO[o % 2][:, :], scalar=0.125,
                                in1=sl, op0=ALU.mult, op1=ALU.add), "dv")
                        VV(f)
                        mark(f"{key}e{o}")
                    mark(f"{key}end")
                    guard["psO"] = {"dv": cnt["dv"]}
                    guard["W3R"] = cnt["pe"]
                    guard["GH"] = cnt["pe"]
                    return
                HB = HJ // 2
                for o in range(DC):
                    for hb in range(2):
                        bidx = o * 2 + hb
                        p = bidx % 2

                        def f(s, o=o, hb=hb, bidx=bidx, p=p):
                            if bidx < 2:
                                if call == 0 and h2 == 1:
                                    wait(s, "sQe", marks["F0B0end"]["sQe"])
                                    wait(s, "sQo", marks["F0B0end"]["sQo"])
                                if guard["W3R"] is not None:
                                    wait(s, "pe", guard["W3R"])
                            if bidx >= 2:
                                wait(s, "pe", marks[f"{key}m{bidx - 2}"]["pe"])
                            bump(s.dma_start(
                                out=W3R[p],
                                in_=w3q[o][:, hb * 2816:(hb + 1) * 2816]),
                                "lBe" if bidx % 2 == 0 else "lBo", 16)
                        SS(f)
                        mark(f"{key}l{bidx}")

                        def f(t, o=o, hb=hb, bidx=bidx, p=p):
                            lb = "lBe" if bidx % 2 == 0 else "lBo"
                            wait(t, lb, marks[f"{key}l{bidx}"][lb])
                            if bidx == 0:
                                wait(t, "dv", marks[f"F{call}s2h{h2}h{HJ - 1}"]["dv"])
                                if guard["psO"] is not None:
                                    for nm, v in guard["psO"].items():
                                        wait(t, nm, v)
                            if bidx >= 4 and hb == 0:
                                wait(t, "dv", marks[f"{key}e{o - 2}"]["dv"])
                            for j in range(HB):
                                jj = hb * HB + j
                                mm = t.matmul(psO[o % 2][:, :],
                                              W3R[p][:, j * 128:(j + 1) * 128],
                                              GH[:, jj * TH:(jj + 1) * TH],
                                              start=(jj == 0), stop=(jj == HJ - 1))
                            bump(mm, "pe")
                        TT(f)
                        mark(f"{key}m{bidx}")

                    def f(v, o=o):
                        wait(v, "pe", marks[f"{key}m{o * 2 + 1}"]["pe"])
                        sl = xf[:, o * T + off: o * T + off + TH]
                        bump(v.scalar_tensor_tensor(
                            out=sl, in0=psO[o % 2][:, :], scalar=0.125, in1=sl,
                            op0=ALU.mult, op1=ALU.add), "dv")
                    VV(f)
                    mark(f"{key}e{o}")
                mark(f"{key}end")
                guard["psO"] = {"dv": cnt["dv"]}
                guard["W3R"] = cnt["pe"]
                guard["GH"] = cnt["pe"]

            if call == 0:
                sweep(1, w1t, w1q, 0)
                sig_batch(0)
                sweep(2, w2t, w2q, 0)
                phase_b(0)
                sweep(1, None, w1q, 1)
                sig_batch(1)
                sweep(2, None, w2q, 1)
                phase_b(1)
            else:
                for h2 in range(2):
                    sweep(1, None, w1q, h2)
                    sig_batch(h2)
                    sweep(2, None, w2q, h2)
                    phase_b(h2)
        mark(f"ffn{call}done")

    def emit_attention():
        prev = dict(cnt)
        with nc.psum_tensor("psP0", [128, TH], F32) as psp0, \
             nc.psum_tensor("psP1", [128, TH], F32) as psp1, \
             nc.psum_tensor("psM0", [64, TH], F32) as psm0, \
             nc.psum_tensor("psM1", [64, TH], F32) as psm1, \
             nc.psum_tensor("psTq", [128, 512], BF16) as psTq, \
             nc.psum_tensor("psKV0", [128, 129], F32) as pskv0, \
             nc.psum_tensor("psKV1", [128, 129], F32) as pskv1:
            psP = [psp0, psp1]
            psM = [psm0, psm1]
            psK = psP
            psKV = [pskv0, pskv1]

            def emit_wv_ci(ci):
                dc, hhalf = divmod(ci, 2)
                p = ci % 2
                svp = "sVe" if ci % 2 == 0 else "sVo"

                def f(a, ci=ci, dc=dc, hhalf=hhalf, p=p, svp=svp):
                    if ci == 0:
                        wait(a, "dv", marks["L2done"]["dv"])
                    if ci >= 2:
                        wait(a, "ac", marks[f"wvc{ci - 2}"]["ac"])
                    bump(a.dma_start(out=WVF[p],
                                     in_=wvt[dc][:, hhalf * 1024:(hhalf + 1) * 1024]),
                         "lC", 16)
                    wait(a, "lC")
                    if ci >= 2:
                        wait(a, svp, marks[f"wvs{ci - 2}"][svp])
                    bump(a.copy(out=WVO[p], in_=WVF[p]), "ac")
                AA(f)
                mark(f"wvc{ci}")

                def f(a, ci=ci, dc=dc, hhalf=hhalf, p=p, svp=svp):
                    wait(a, "ac", marks[f"wvc{ci}"]["ac"])
                    bump(a.dma_start(
                        out=wvb_d[dc][:, hhalf * 1024:(hhalf + 1) * 1024],
                        in_=WVO[p]), svp, 16)
                AA(f)
                mark(f"wvs{ci}")

            def proj_head(h, wsrc, bias_s, tmp, pref, is_q, tmp_guard=None):
                p = h % 2

                def f(s, h=h, p=p, wg=dict(WG)):
                    if h < 2:
                        wguard(s, wg)
                    if h >= 2:
                        wait(s, "dv", marks[f"{pref}cast{h - 2}"]["dv"])
                    bump(s.dma_start(out=wfa[p][:, :], in_=wsrc[h]),
                         "lPe" if h % 2 == 0 else "lPo", 16)
                SS(f)
                mark(f"{pref}ld{h}")

                def f(v, h=h, p=p):
                    lp = "lPe" if h % 2 == 0 else "lPo"
                    wait(v, lp, marks[f"{pref}ld{h}"][lp])
                    if h >= 2:
                        wait(v, "pe", marks[f"{pref}mm{h - 2}"]["pe"])
                    bump(v.tensor_copy(out=wqa[p][:, :], in_=wfa[p][:, :]), "dv")
                VV(f)
                mark(f"{pref}cast{h}")

                def f(t, h=h, p=p):
                    wait(t, "dv", marks[f"{pref}cast{h}"]["dv"])
                    if h == 0:
                        wait(t, "dv", marks["L2done"]["dv"])
                        for nm, v in prev.items():
                            wait(t, nm, v)
                    if h >= 1:
                        wait(t, "dv", marks[f"{pref}ev{h - 1}"]["dv"])
                    for n2 in range(2):
                        for dc in range(DC - 1):
                            t.matmul(psP[n2][:, :],
                                     wqa[p][:, dc * 128:(dc + 1) * 128],
                                     ybuf[:, dc * T + n2 * TH: dc * T + n2 * TH + TH],
                                     start=(dc == 0), stop=False)
                        mm = t.matmul(psP[n2][:, :],
                                      wqa[p][:, (DC - 1) * 128:DC * 128],
                                      ybuf[:, (DC - 1) * T + n2 * TH: (DC - 1) * T + n2 * TH + TH],
                                      start=False, stop=True)
                    bump(mm, "pe")
                TT(f)
                mark(f"{pref}mm{h}")

                def f(v, h=h, p=p):
                    wait(v, "pe", marks[f"{pref}mm{h}"]["pe"])
                    if h >= 2:
                        wait(v, "pe", marks[f"{pref}mp{h - 2}"]["pe"])
                        if tmp_guard is not None:
                            wait(v, "pe", marks[f"{tmp_guard}{h - 2}"]["pe"])
                    for n2 in range(2):
                        bump(v.tensor_scalar(out=tmp[p][:, n2 * TH:(n2 + 1) * TH],
                                             in0=psP[n2][:, :],
                                             scalar1=bias_s[:, h:h + 1],
                                             scalar2=None, op0=ALU.add), "dv")
                VV(f)
                mark(f"{pref}ev{h}")

                wmap = wmapq_b if is_q else wmapk_b

                def f(t, h=h, p=p):
                    wait(t, "dv", marks[f"{pref}ev{h}"]["dv"])
                    if h >= 1:
                        wait(t, "ac", marks[f"{pref}el{h - 1}"]["ac"])
                    for n2 in range(2):
                        mm = t.matmul(psM[n2][:, :], wmap[:, :],
                                      tmp[p][:, n2 * TH:(n2 + 1) * TH],
                                      start=True, stop=True)
                    bump(mm, "pe")
                TT(f)
                mark(f"{pref}mp{h}")

                def f(a, h=h, p=p):
                    wait(a, "pe", marks[f"{pref}mp{h}"]["pe"])
                    if h >= 1:
                        wait(a, "dv", marks[f"{pref}qm{h - 1}"]["dv"])
                    for n2 in range(2):
                        bump(a.activation(out=ELU[0][:, n2 * TH:(n2 + 1) * TH],
                                          in_=psM[n2][:, :], func=AF.Relu), "ac")
                        bump(a.activation(out=ELU[1][:, n2 * TH:(n2 + 1) * TH],
                                          in_=psM[n2][:, :], func=AF.Relu,
                                          scale=negone[0:64, 0:1]), "ac")
                    wait(a, "ac")
                    bump(a.activation(out=ELU[1][:, :], in_=ELU[1][:, :],
                                      func=AF.Exp, scale=negone[0:64, 0:1]), "ac")
                AA(f)
                mark(f"{pref}el{h}")

            wg_update()
            for h in range(NH):
                proj_head(h, wqt, bq_s, QTMP, "q", True)

                def f(v, h=h):
                    wait(v, "ac", marks[f"qel{h}"]["ac"])
                    bump(v.tensor_tensor(
                        out=qmb[(h % 2) * 64:(h % 2) * 64 + 64,
                                (h // 2) * T:(h // 2) * T + T],
                        in0=ELU[0][:, :], in1=ELU[1][:, :], op=ALU.add), "dv")
                VV(f)
                mark(f"qqm{h}")

                emit_wv_ci(2 * h)
                emit_wv_ci(2 * h + 1)

            wg_update()
            for h in range(NH):
                proj_head(h, wkt, bk_s, KTMP, "k", False, tmp_guard="ktp")

                def f(v, h=h):
                    wait(v, "ac", marks[f"kel{h}"]["ac"])
                    if h >= 1:
                        wait(v, "pe", marks[f"ktp{h - 1}"]["pe"])
                    bump(v.tensor_tensor(
                        out=KTMP[h % 2][0:64, :],
                        in0=ELU[0][:, :], in1=ELU[1][:, :], op=ALU.add), "dv")
                VV(f)
                mark(f"kqm{h}")

                def f(t, h=h):
                    wait(t, "dv", marks[f"kqm{h}"]["dv"])
                    if h >= 1:
                        wait(t, "dv", marks[f"kmv{h - 1}"]["dv"])
                    for mb in range(7):
                        t.transpose(psTq[:, mb * 64:(mb + 1) * 64],
                                    KTMP[h % 2][0:64, mb * 128:(mb + 1) * 128],
                                    identb[0:64, 0:64])
                    bump(t.transpose(psTq[:, 7 * 64:8 * 64],
                                     KTMP[h % 2][0:64, 7 * 128:8 * 128],
                                     identb[0:64, 0:64]), "pe")
                TT(f)
                mark(f"ktp{h}")

                def f(v, h=h):
                    wait(v, "pe", marks[f"ktp{h}"]["pe"])
                    bump(v.tensor_copy(out=KMB[:, h * 512:(h + 1) * 512],
                                       in_=psTq[:, :]), "dv")
                VV(f)
                mark(f"kmv{h}")

            def f(g):
                wait(g, "pe", marks["L2done"]["pe"])
                wait(g, "dv", marks["L2done"]["dv"])
                vv = VBUF.rearrange("p (i h e) -> p i h e", i=2, e=129)
                bump(g.memset(vv[:, :, :, 128:129], 1.0), "pl")
            GG(f)
            mark("vones")

            NG = 4
            for g2 in range(NG):
                for osl in range(8):
                    sidx = g2 * 8 + osl

                    def f(s, g2=g2, osl=osl, sidx=sidx):
                        if sidx < 2:
                            wait(s, "sVe", marks["wvs31"]["sVe"])
                            wait(s, "sVo", marks["wvs31"]["sVo"])
                        if sidx >= 2:
                            wait(s, "pe", marks[f"vmm{sidx - 2}_1"]["pe"])
                        bump(s.dma_start(
                            out=WVS[sidx % 2],
                            in_=wvb_d.rearrange("d p c -> p d c")
                            [:, :, osl * 256:(osl + 1) * 256]), "lV", 16)
                    SS(f)
                    mark(f"vld{sidx}")

                    for i in range(2):
                        mchunk = g2 * 2 + i
                        b = i % 2

                        def f(t, g2=g2, osl=osl, sidx=sidx, i=i, mchunk=mchunk, b=b):
                            if sidx == 0 and i == 0:
                                wait(t, "dv", marks[f"kev{NH - 1}"]["dv"])
                                wait(t, "dv", marks[f"kmv{NH - 1}"]["dv"])
                            if osl == 0 and i == 0 and g2 > 0:
                                wait(t, "pe", marks[f"kvmm{g2 - 1}"]["pe"])
                            if i == 0:
                                wait(t, "lV", marks[f"vld{sidx}"]["lV"])
                            if i >= 2 or sidx > 0:
                                pk = (f"vev{sidx}_{i - 2}" if i >= 2
                                      else f"vev{sidx - 1}_{i}")
                                wait(t, "dv", marks[pk]["dv"])
                            wsl = WVS[sidx % 2]
                            for dc in range(DC - 1):
                                t.matmul(psP[b][:, 0:256],
                                         ybuf[:, dc * T + mchunk * 128: dc * T + (mchunk + 1) * 128],
                                         wsl[:, dc * 256:(dc + 1) * 256],
                                         start=(dc == 0), stop=False)
                            t.matmul(psP[b][:, 0:256],
                                     ybuf[:, (DC - 1) * T + mchunk * 128: (DC - 1) * T + (mchunk + 1) * 128],
                                     wsl[:, (DC - 1) * 256:DC * 256],
                                     start=False, stop=False)
                            bump(t.matmul(psP[b][:, 0:256], onesb_r[:],
                                          bv_b[0:1, osl * 256:(osl + 1) * 256],
                                          start=False, stop=True), "pe")
                        TT(f)
                        mark(f"vmm{sidx}_{i}")

                        def f(v, g2=g2, osl=osl, sidx=sidx, i=i, b=b):
                            wait(v, "pe", marks[f"vmm{sidx}_{i}"]["pe"])
                            if sidx == 0 and i == 0:
                                wait(v, "pl", marks["vones"]["pl"])
                            for hh in range(2):
                                h = osl * 2 + hh
                                bump(v.tensor_copy(
                                    out=VBUF[:, i * (NH * 129) + h * 129:
                                             i * (NH * 129) + h * 129 + 128],
                                    in_=psP[b][:, hh * 128:(hh + 1) * 128]), "dv")
                        VV(f)
                        mark(f"vev{sidx}_{i}")

                for h in range(NH):
                    q3 = h // 6
                    j3 = (h % 6) // 2
                    i3 = h % 2
                    kidx = g2 * NH + h

                    def f(t, g2=g2, h=h, i3=i3, kidx=kidx):
                        if h == 0:
                            wait(t, "dv", marks[f"vev{g2 * 8 + 7}_1"]["dv"])
                        if kidx >= 2:
                            wait(t, "dv", marks[f"kvadd{kidx - 2}"]["dv"])
                        for i in range(2):
                            mchunk = g2 * 2 + i
                            mm = t.matmul(
                                psKV[h % 2][i3 * 64:(i3 + 1) * 64, :],
                                KMB[:, h * 512 + mchunk * 64: h * 512 + (mchunk + 1) * 64],
                                VBUF[:, i * (NH * 129) + h * 129:
                                     i * (NH * 129) + (h + 1) * 129],
                                start=(i == 0), stop=(i == 1))
                        bump(mm, "pe")
                    TT(f)
                    mark(f"kvmm{kidx}")

                    def f(v, g2=g2, h=h, q3=q3, j3=j3, i3=i3, kidx=kidx):
                        wait(v, "pe", marks[f"kvmm{kidx}"]["pe"])
                        dst = kvga[i3 * 64:(i3 + 1) * 64,
                                   q3 * 512 + j3 * 129: q3 * 512 + (j3 + 1) * 129]
                        ps = psKV[h % 2][i3 * 64:(i3 + 1) * 64, :]
                        if g2 == 0:
                            bump(v.tensor_copy(out=dst, in_=ps), "dv")
                        else:
                            bump(v.tensor_tensor(out=dst, in0=dst, in1=ps,
                                                 op=ALU.add), "dv")
                    VV(f)
                    mark(f"kvadd{kidx}")

            mark("kvev")

            def f(g):
                wait(g, "dv", marks["kvev"]["dv"])
                bump(g.dma_start(out=cc_kin.ap(), in_=kvga[:]), "g", 16)
                wait(g, "g")
                bump(g.collective_compute(
                    "AllReduce", ALU.add, replica_groups=[list(range(NCORES))],
                    ins=[cc_kin.ap().opt()], outs=[cc_kout.ap().opt()]), "cc", 1)
                wait(g, "cc")
                bump(g.dma_start(out=kvga[:], in_=cc_kout.ap()), "g", 16)
            GG(f)
            mark("kvcc")

            def f(v):
                wait(v, "g", marks["kvcc"]["g"])
                bump(v.tensor_copy(out=kvgb[:], in_=kvga[:]), "dv")
            VV(f)
            mark("kvcast")

            def kvoff(h):
                q3 = h // 6
                j3 = (h % 6) // 2
                i3 = h % 2
                return q3, j3, i3

            for h in range(NH):
                q3, j3, i3 = kvoff(h)

                def f(t, h=h, q3=q3, j3=j3, i3=i3):
                    if h == 0:
                        wait(t, "dv", marks["kvcast"]["dv"])
                    if h >= 1:
                        wait(t, "dv", marks[f"nev{h - 1}"]["dv"])
                    ks = kvgb[i3 * 64:(i3 + 1) * 64,
                              q3 * 512 + j3 * 129 + 128: q3 * 512 + j3 * 129 + 129]
                    for n2 in range(2):
                        mm = t.matmul(psM[n2][0:1, :], ks,
                                      qmb[i3 * 64:(i3 + 1) * 64,
                                          (h // 2) * T + n2 * TH: (h // 2) * T + (n2 + 1) * TH],
                                      start=True, stop=True)
                    bump(mm, "pe")
                TT(f)
                mark(f"nmm{h}")

                def f(v, h=h):
                    wait(v, "pe", marks[f"nmm{h}"]["pe"])
                    if h >= 2:
                        wait(v, "lN", marks[f"nsth{h - 1}"]["lN"])
                    ro = (h % 2) * T
                    bump(v.tensor_copy(out=rows[0:1, ro:ro + TH],
                                       in_=psM[0][0:1, :]), "dv")
                    bump(v.tensor_copy(out=rows[0:1, ro + TH:ro + T],
                                       in_=psM[1][0:1, :]), "dv")
                VV(f)
                mark(f"nev{h}")

                def f(s, h=h):
                    wait(s, "dv", marks[f"nev{h}"]["dv"])
                    ro = (h % 2) * T
                    bump(s.dma_start(out=nrm_d.ap()[h:h + 1, :],
                                     in_=rows[0:1, ro:ro + T]), "lN", 16)
                SS(f)
                mark(f"nsth{h}")

            NB = rows[:, 1024:1152]
            nflat0 = nrm_d.ap().rearrange("a b -> (a b)")

            def f(s):
                wait(s, "lN", marks[f"nsth{NH - 1}"]["lN"])
                bump(s.dma_start(out=NB, in_=nflat0[:]), "lN", 16)
            SS(f)
            mark("nbl")

            def f(v):
                wait(v, "lN", marks["nbl"]["lN"])
                bump(v.tensor_scalar(out=NB, in0=NB, scalar1=1e-6, scalar2=None,
                                     op0=ALU.add), "dv")
                wait(v, "dv")
                bump(v.reciprocal(out=NB, in_=NB), "dv")
            VV(f)
            mark("rnorm")

            def f(s):
                wait(s, "dv", marks["rnorm"]["dv"])
                bump(s.dma_start(out=nflat0[:], in_=NB), "lN", 16)
            SS(f)
            mark("nst")

            nflat = nrm_d.ap().rearrange("a b -> (a b)")

            for h in range(NH):
                q3, j3, i3 = kvoff(h)
                if h % 2 == 0:
                    def f(s, h=h):
                        if h == 0:
                            wait(s, "lN", marks["nst"]["lN"])
                        else:
                            wait(s, "pe", marks[f"rbc{h - 1}"]["pe"])
                        bump(s.dma_start(out=rows[0:1, 0:2048],
                                         in_=nflat[h * T:(h + 2) * T]), "lN", 16)
                    SS(f)
                    mark(f"nld{h}")

                def f(t, h=h):
                    wait(t, "lN", marks[f"nld{h - h % 2}"]["lN"])
                    if h == 0:
                        wait(t, "dv", marks["rnorm"]["dv"])
                    if h >= 1:
                        wait(t, "dv", marks[f"rnb{h - 1}"]["dv"])
                    for n2 in range(2):
                        mm = t.matmul(psM[n2][:, :],
                                      onesf_r[0:1, 0:64],
                                      rows[0:1, (h % 2) * T + n2 * TH:
                                           (h % 2) * T + (n2 + 1) * TH],
                                      start=True, stop=True)
                    bump(mm, "pe")
                TT(f)
                mark(f"rbc{h}")

                def f(v, h=h, i3=i3):
                    wait(v, "pe", marks[f"rbc{h}"]["pe"])
                    if h >= 1:
                        wait(v, "dv", marks[f"qn{h - 1}"]["dv"])
                    bump(v.tensor_copy(out=RNB2[i3][:, 0:TH], in_=psM[0][:, :]), "dv")
                    bump(v.tensor_copy(out=RNB2[i3][:, TH:T], in_=psM[1][:, :]), "dv")
                VV(f)
                mark(f"rnb{h}")

                def f(v, h=h, i3=i3):
                    wait(v, "dv", marks[f"rnb{h}"]["dv"])
                    sl = qmb[i3 * 64:(i3 + 1) * 64, (h // 2) * T:(h // 2) * T + T]
                    bump(v.tensor_tensor(out=sl, in0=sl, in1=RNB2[i3][:, :],
                                         op=ALU.mult), "dv")
                VV(f)
                mark(f"qn{h}")

                def f(t, h=h, q3=q3, j3=j3, i3=i3):
                    wait(t, "dv", marks[f"qn{h}"]["dv"])
                    if h >= 1:
                        wait(t, "dv", marks[f"aev{h - 1}"]["dv"])
                    kv = kvgb[i3 * 64:(i3 + 1) * 64,
                              q3 * 512 + j3 * 129: q3 * 512 + j3 * 129 + 128]
                    for n2 in range(2):
                        mm = t.matmul(psP[n2][:, :], kv,
                                      qmb[i3 * 64:(i3 + 1) * 64,
                                          (h // 2) * T + n2 * TH:(h // 2) * T + (n2 + 1) * TH],
                                      start=True, stop=True)
                    bump(mm, "pe")
                TT(f)
                mark(f"amm{h}")

                def f(v, h=h):
                    wait(v, "pe", marks[f"amm{h}"]["pe"])
                    for n2 in range(2):
                        bump(v.tensor_copy(
                            out=ybuf[:, h * T + n2 * TH: h * T + (n2 + 1) * TH],
                            in_=psP[n2][:, :]), "dv")
                VV(f)
                mark(f"aev{h}")

            wg_update()
            for o in range(DC):
                p = o % 2

                def f(s, o=o, p=p, wg=dict(WG)):
                    if o < 2:
                        wguard(s, wg)
                    if o >= 2:
                        wait(s, "dv", marks[f"ocast{o - 2}"]["dv"])
                    bump(s.dma_start(out=wfa[p][:, :], in_=wot[o]),
                         "lPe" if o % 2 == 0 else "lPo", 16)
                SS(f)
                mark(f"old{o}")

                def f(v, o=o, p=p):
                    lp = "lPe" if o % 2 == 0 else "lPo"
                    wait(v, lp, marks[f"old{o}"][lp])
                    if o >= 2:
                        wait(v, "pe", marks[f"omm{o - 2}"]["pe"])
                    bump(v.tensor_copy(out=wqa[p][:, :], in_=wfa[p][:, :]), "dv")
                VV(f)
                mark(f"ocast{o}")

                def f(t, o=o, p=p):
                    wait(t, "dv", marks[f"ocast{o}"]["dv"])
                    if o == 0:
                        wait(t, "dv", marks["kvev"]["dv"])
                    if o >= 1:
                        wait(t, "dv", marks[f"oep{o - 1}"]["dv"])
                    for n2 in range(2):
                        for dc in range(DC - 1):
                            t.matmul(psK[n2][:, :],
                                     wqa[p][:, dc * 128:(dc + 1) * 128],
                                     ybuf[:, dc * T + n2 * TH: dc * T + n2 * TH + TH],
                                     start=(dc == 0), stop=False)
                        mm = t.matmul(psK[n2][:, :],
                                      wqa[p][:, (DC - 1) * 128:DC * 128],
                                      ybuf[:, (DC - 1) * T + n2 * TH: (DC - 1) * T + n2 * TH + TH],
                                      start=False, stop=True)
                    bump(mm, "pe")
                TT(f)
                mark(f"omm{o}")

                def f(v, o=o, p=p):
                    wait(v, "pe", marks[f"omm{o}"]["pe"])
                    for n2 in range(2):
                        sl = xf[:, o * T + n2 * TH: o * T + (n2 + 1) * TH]
                        bump(v.scalar_tensor_tensor(
                            out=sl, in0=psK[n2][:, :], scalar=bo_s[:, o:o + 1],
                            in1=sl, op0=ALU.add, op1=ALU.add), "dv")
                VV(f)
                mark(f"oep{o}")
        wg_update()
        mark("attdone")

    def emit_output(src):
        with nc.psum_tensor("psT2a", [128, TH], F32) as pst0, \
             nc.psum_tensor("psT2b", [128, TH], F32) as pst1:
            psT2 = [pst0, pst1]
            for m in range(8):
                for qq in range(4):
                    b = qq % 2

                    def f(t, m=m, qq=qq, b=b):
                        if m == 0 and qq == 0:
                            for nm, v in marks[src].items():
                                wait(t, nm, v)
                        if m > 0 or qq >= 2:
                            pk = (f"oev{m}_{qq - 2}" if qq >= 2
                                  else f"oev{m - 1}_{qq + 2}")
                            wait(t, "dv", marks[pk]["dv"])
                        for i in range(4):
                            dc = qq * 4 + i
                            mm = t.transpose(psT2[b][:, i * 128:(i + 1) * 128],
                                             xf[:, dc * T + m * 128: dc * T + (m + 1) * 128],
                                             identf[:])
                        bump(mm, "pe")
                    TT(f)
                    mark(f"otp{m}_{qq}")

                    def f(v, m=m, qq=qq, b=b):
                        wait(v, "pe", marks[f"otp{m}_{qq}"]["pe"])
                        if m > 0:
                            wait(v, "sO", marks[f"ost{m - 1}"]["sO"])
                        bump(v.tensor_copy(out=XOUT[:, qq * 512:(qq + 1) * 512],
                                           in_=psT2[b][:, :]), "dv")
                    VV(f)
                    mark(f"oev{m}_{qq}")

                def f(s, m=m):
                    wait(s, "dv", marks[f"oev{m}_3"]["dv"])
                    bump(s.dma_start(out=out[m * 128:(m + 1) * 128, :], in_=XOUT),
                         "sO", 16)
                SS(f)
                mark(f"ost{m}")

            def f(s):
                wait(s, "sO", cnt["sO"])
            SS(f)

    emit_ln("L1", marks["xf_done"]["dv"])
    emit_ffn(0, "L1")
    if stage >= 2:
        emit_ln("L2", marks["ffn0done"]["dv"])
        emit_attention()
    if stage >= 3:
        emit_ln("L3", marks["attdone"]["dv"])
        emit_ffn(1, "L3")
        emit_output("ffn1done")
    elif stage == 2:
        emit_output("attdone")
    else:
        emit_output("ffn0done")

    st.close()
    return nc


def host_prep(inp):
    c = np.ascontiguousarray
    w1 = np.asarray(inp["L0_ffn_w1"]); w2 = np.asarray(inp["L0_ffn_w2"])
    w3 = np.asarray(inp["L0_ffn_w3"])
    w1t = c(w1.T).reshape(16, 128, 44, 128).transpose(2, 1, 0, 3)
    w2t = c(w2.T).reshape(16, 128, 44, 128).transpose(2, 1, 0, 3)
    w3t = c(w3.T).reshape(44, 128, 16, 128).transpose(2, 1, 0, 3)
    w1t = c(w1t.reshape(44, 128, 2048))
    w2t = c(w2t.reshape(44, 128, 2048))
    w3t = c(w3t.reshape(16, 128, 5632))
    wq = np.asarray(inp["L0_attn_wq"]); wk = np.asarray(inp["L0_attn_wk"])
    wv = np.asarray(inp["L0_attn_wv"]); wo = np.asarray(inp["L0_attn_wo"])
    mk = lambda w: c(c(w.T).reshape(16, 128, 16, 128).transpose(2, 1, 0, 3)
                     .reshape(16, 128, 2048))
    base = {
        "w1t": w1t, "w2t": w2t, "w3t": w3t,
        "wqt": mk(wq), "wkt": mk(wk), "wot": mk(wo),
        "wvt": c(wv.T.reshape(16, 128, 2048)),
        "wmapqt": c(np.asarray(inp["L0_attn_wmap_q"]).T),
        "wmapkt": c(np.asarray(inp["L0_attn_wmap_k"]).T),
        "bqt": c(np.asarray(inp["L0_attn_bq"]).reshape(16, 128).T),
        "bkt": c(np.asarray(inp["L0_attn_bk"]).reshape(16, 128).T),
        "bot": c(np.asarray(inp["L0_attn_bo"]).reshape(16, 128).T),
        "bvr": c(np.asarray(inp["L0_attn_bv"]).reshape(1, 2048)),
    }
    g1 = w1t.reshape(NCORES, 128, GSL)
    g2 = w2t.reshape(NCORES, 128, GSL)
    g3 = w3t.reshape(NCORES, 128, GSL)
    x = np.asarray(inp["L0_attn_hidden_states"]).reshape(8192, 2048)
    maps = []
    for ci in range(NCORES):
        m = dict(base)
        m["xin"] = c(x[ci * T:(ci + 1) * T, :])
        m["gsl"] = c(np.stack([g1[ci], g2[ci], g3[ci]]))
        maps.append(m)
    return maps


def kernel(**inputs):
    maps = host_prep(inputs)
    nc = build(STAGE)
    res = run_bass_kernel_spmd(nc, maps, core_ids=list(range(NCORES)))
    outs = [res.results[i]["out"] for i in range(NCORES)]
    return np.concatenate(outs, 0).reshape(1, 8192, 2048)


if __name__ == "__main__":
    import sys
    stg = int(sys.argv[1]) if len(sys.argv) > 1 else STAGE
    nc = build(stg)
    n = sum(1 for _ in nc.all_instructions())
    print(f"build ok (stage {stg}), instructions: {n}")
